# version 03
# baseline (speedup 1.0000x reference)
import numpy as np
import ml_dtypes
import concourse.bass as bass
import concourse.mybir as mybir
from concourse.bass_utils import run_bass_kernel_spmd
from collections import defaultdict
from contextlib import ExitStack

F32 = mybir.dt.float32
BF16 = mybir.dt.bfloat16
U8 = mybir.dt.uint8
AF = mybir.ActivationFunctionType
ALU = mybir.AluOpType
AX = mybir.AxisListType
BFNP = ml_dtypes.bfloat16

P = 128
T = 2048
D = 1024
KC = D // P
NB = 2
NCORES = 8

ENGINES = ("tensor", "scalar", "vector", "gpsimd", "sync")
DMA_RING = 12


class Region:
    __slots__ = ("name", "last_w", "readers", "exclusive")

    def __init__(self, name, exclusive=False):
        self.name = name
        self.last_w = None
        self.readers = []
        self.exclusive = exclusive


class _Op:
    __slots__ = ("idx", "eng", "fn", "deps", "is_dma", "inc", "waits", "tag")

    def __init__(self, idx, eng, fn, tag):
        self.idx = idx
        self.eng = eng
        self.fn = fn
        self.is_dma = False
        self.deps = set()
        self.inc = False
        self.waits = []
        self.tag = tag


class KB:
    def __init__(self, nc):
        self.nc = nc
        self.ops = []
        self.regions = {}
        self._eng_last = {}

    def reg(self, name, exclusive=False):
        r = self.regions.get(name)
        if r is None:
            r = self.regions[name] = Region(name, exclusive)
        return r

    def op(self, eng, fn, reads=(), writes=(), war_only=(), tag=""):
        idx = len(self.ops)
        o = _Op(idx, eng, fn, tag)
        self.ops.append(o)
        self._eng_last[eng] = idx
        for r in reads:
            if r.last_w is not None:
                o.deps.add(r.last_w)
            if r.exclusive:
                for ri in r.readers:
                    if self.ops[ri].eng != eng:
                        o.deps.add(ri)
            r.readers.append(idx)
        for r in war_only:
            if r.last_w is not None:
                o.deps.add(r.last_w)
            o.deps.update(r.readers)
        for r in writes:
            if r.last_w is not None:
                o.deps.add(r.last_w)
            o.deps.update(r.readers)
            r.last_w = idx
            r.readers = []
        o.deps.discard(idx)
        return o

    def dma(self, queue, fn, reads=(), writes=(), war_only=(), tag=""):
        o = self.op(queue, fn, reads=reads, writes=writes, war_only=war_only,
                    tag=tag)
        o.is_dma = True
        if not hasattr(self, "_dma_idx"):
            self._dma_idx = {}
        self._dma_idx.setdefault(queue, []).append(o.idx)
        return o

    def phase_barrier(self, tag="ph"):
        lasts = [self._eng_last[e] for e in ENGINES if e in self._eng_last]
        extra = []
        for q, idxs in getattr(self, "_dma_idx", {}).items():
            extra.extend(idxs[-DMA_RING:])
        for e in ENGINES:
            o = self.op(e, _nopfn(e), tag=tag)
            o.deps.update(i for i in lasts if i != o.idx)
            o.deps.update(i for i in extra if i != o.idx)
            self._eng_last[e] = o.idx
        self.regions = {}

    def emit(self):
        nc = self.nc
        ops = self.ops
        eng_ops = defaultdict(list)
        for o in ops:
            eng_ops[o.eng].append(o)

        for o in ops:
            for d in o.deps:
                if not ops[d].is_dma:
                    ops[d].inc = True

        comp_val = {}
        run_comp = defaultdict(int)
        dma_count = defaultdict(int)
        run_dma = {}
        for o in ops:
            if o.is_dma:
                slot = dma_count[o.eng, "n"] % DMA_RING
                dma_count[o.eng, "n"] += 1
                key = f"d{slot}_" + o.eng
                dma_count[o.eng, slot] += 16
                comp_val[o.idx] = (key, dma_count[o.eng, slot])
                run_dma[key] = dma_count[o.eng, slot]
                o.inc = True
            elif o.inc:
                run_comp[o.eng] += 1
                comp_val[o.idx] = ("c_" + o.eng, run_comp[o.eng])

        hw = defaultdict(int)
        for o in ops:
            need = {}
            for d in sorted(o.deps):
                sem, val = comp_val[d]
                if val > need.get(sem, 0):
                    need[sem] = val
            for sem, val in sorted(need.items()):
                if val > hw[(o.eng, sem)]:
                    o.waits.append((sem, val))
                    hw[(o.eng, sem)] = val

        sem_names = sorted({s for o in ops for s, _ in o.waits}
                           | {comp_val[o.idx][0] for o in ops if o.idx in comp_val})
        self._sem_stack = ExitStack()
        sems = {s: self._sem_stack.enter_context(nc.semaphore(s)) for s in sem_names}

        final = [(k, v) for k, v in run_dma.items()]
        final += [("c_" + q, v) for q, v in run_comp.items()]
        final = [(s, v) for s, v in final if s in sems]

        stats = dict(n_ops=len(ops), n_incs=sum(1 for o in ops if o.inc),
                     n_waits=sum(len(o.waits) for o in ops))

        with nc.Block() as block:
            def mk_body(ename):
                def body(e):
                    for o in eng_ops[ename]:
                        for sem, val in o.waits:
                            e.wait_ge(sems[sem], val)
                        inst = o.fn(e)
                        if o.inc:
                            sem, _ = comp_val[o.idx]
                            inst.then_inc(sems[sem], 16 if o.is_dma else 1)
                    if ename == "sync":
                        for sem, val in final:
                            e.wait_ge(sems[sem], val)
                return body

            order = [e for e in ENGINES if eng_ops.get(e)]
            if "sync" not in order:
                order.append("sync")
            for e in order:
                getattr(block, e)(mk_body(e))
        return stats


def _nopfn(eng):
    def f(e):
        return e.nop()
    return f


def tiles(w):
    k, o = w.shape
    assert k % P == 0
    return np.ascontiguousarray(
        w.reshape(k // P, P, o).transpose(1, 0, 2).reshape(P, (k // P) * o))


def fm_batch(xb):
    return np.ascontiguousarray(
        xb.T.reshape(KC, P, T).transpose(1, 0, 2).reshape(P, KC * T))


def un_fm_batch(arr):
    return np.ascontiguousarray(
        arr.reshape(P, KC, T).transpose(1, 0, 2).reshape(D, T).T)


def rope_tables():
    hd = 64
    inv = 1.0 / (100000.0 ** (np.arange(0, hd, 2, dtype=np.float32) / hd))
    f = np.arange(T, dtype=np.float32)[None, :] * inv[:, None]
    c = np.cos(f).astype(np.float32)
    s = np.sin(f).astype(np.float32)
    return np.concatenate([c, s, s, c], axis=0)


def host_consts():
    cc = {}
    cc["c_ident"] = np.eye(P, dtype=BFNP)
    cc["c_ones128"] = np.ones((P, P), BFNP)
    cs = rope_tables()
    cc["c_ropek"] = cs.astype(BFNP)
    cc["c_ropeq"] = (cs * 0.125).astype(BFNP)
    m = np.zeros((P, P), np.float32)
    m[np.triu_indices(P, 1)] = -1e30
    cc["c_causal"] = m
    oh = np.zeros((16, 16 * P), np.float32)
    for e in range(16):
        oh[e, e * P:(e + 1) * P] = 1.0
    cc["c_onehot"] = oh.astype(BFNP)
    return cc


CONST_DECLS = {
    "c_ident": ([P, P], BF16),
    "c_ones128": ([P, P], BF16),
}
PHASE_CONST_DECLS = {
    "c_causal": ([P, P], F32),
    "c_onehot": ([16, 16 * P], BF16),
    "c_ropeq": ([P, T], BF16),
    "c_ropek": ([P, T], BF16),
}


OFF_XR = 0
OFF_A = 65536
OFF_B = 98304
OFF_U = 131072
OFF_W = 163840
OFF_SCR = 190464
OFF_SM = 202752
OFF_SM2 = 210944
OFF_IDENT = 211456
OFF_ONES128 = 211712
OFF_END = 211968
SB_COLS = 106120
assert OFF_END <= 2 * SB_COLS


class Ctx:
    def __init__(self, nc, kb, inp, sb, ps_a, ps_b):
        self.nc = nc
        self.kb = kb
        self.inp = inp
        self.sb = sb
        self.ps_a = ps_a
        self.ps_b = ps_b

    def bf(self, byte_off, cols, rows=P, row0=0):
        assert byte_off % 2 == 0 and byte_off + 2 * cols <= 2 * SB_COLS
        v = self.sb[:][:, byte_off // 2: byte_off // 2 + cols]
        if rows != P or row0 != 0:
            v = v[row0:row0 + rows, :]
        return v

    def f32(self, byte_off, cols, rows=P, row0=0):
        assert byte_off % 4 == 0 and byte_off + 4 * cols <= 2 * SB_COLS
        v = self.sb[:].bitcast(F32)[:, byte_off // 4: byte_off // 4 + cols]
        if rows != P or row0 != 0:
            v = v[row0:row0 + rows, :]
        return v

    def u8(self, byte_off, cols, rows=P, row0=0):
        v = self.sb[:].bitcast(U8)[:, byte_off: byte_off + cols]
        if rows != P or row0 != 0:
            v = v[row0:row0 + rows, :]
        return v

    def psa(self, j):
        return self.kb.reg(f"psa{j}", exclusive=True)

    def psb(self, j):
        return self.kb.reg(f"psb{j}", exclusive=True)


def load_consts(cx, names):
    kb = cx.kb
    views = {
        "c_ident": cx.bf(OFF_IDENT, 128),
        "c_ones128": cx.bf(OFF_ONES128, 128),
    }
    for name in names:
        kb.dma("sync", lambda s, v=views[name], src=cx.inp[name]: s.dma_start(v, src),
               writes=[kb.reg("K" + name)], tag="ldc")


def cr(cx, name):
    return cx.kb.reg("K" + name)


def ln_apply(cx, ph, srcs, n, eps, want_mean, src_bf, mode, dsts=None, n0=0,
             lnw=None, homes=None):
    kb = cx.kb
    nsub = n // 512
    nch = len(srcs)
    nd = nch * P
    sq_off, xb_off, rbf_off, var_off = homes or (OFF_SCR, OFF_SCR + 4096,
                                                 OFF_SCR + 8192, OFF_SM)
    sq = cx.bf(sq_off, n)
    rsq = kb.reg(f"{ph}sq")
    xb = cx.bf(xb_off, n)
    rxb = kb.reg(f"{ph}xb")
    rbf = cx.bf(rbf_off, n)
    r_rbf = kb.reg(f"{ph}rbf")
    ones = cx.bf(OFF_ONES128, 128)
    rc = cr(cx, "c_ones128")
    ps_sum = cx.ps_b[:, 0:n]
    ps_sq = cx.ps_a[:, 0:n]
    r_pssum = [cx.psb(j) for j in range(nsub)]
    r_pssq = [cx.psa(j) for j in range(nsub)]

    for i, (sv, rv) in enumerate(srcs):
        kb.op("gpsimd", lambda g, sv=sv: g.tensor_tensor(sq, sv, sv, ALU.mult),
              reads=[rv], writes=[rsq], tag=f"{ph}sq{i}")
        if not src_bf and want_mean:
            kb.op("vector", lambda v, sv=sv: v.tensor_copy(xb, sv),
                  reads=[rv], writes=[rxb], tag=f"{ph}xb{i}")
        for j in range(nsub):
            sl = slice(j * 512, (j + 1) * 512)
            if want_mean:
                msrc = sv if src_bf else xb
                mreads = [rc, rv] if src_bf else [rc, rxb]
                kb.op("tensor",
                      lambda t, msrc=msrc, sl=sl, i=i: t.matmul(
                          ps_sum[:, sl], ones, msrc[:, sl],
                          start=(i == 0), stop=(i == nch - 1)),
                      reads=mreads,
                      war_only=[r_pssum[j]] if i == 0 else (),
                      writes=[r_pssum[j]] if i == nch - 1 else (),
                      tag=f"{ph}summ{i}.{j}")
            kb.op("tensor",
                  lambda t, sl=sl, i=i: t.matmul(
                      ps_sq[:, sl], ones, sq[:, sl],
                      start=(i == 0), stop=(i == nch - 1)),
                  reads=[rc, rsq],
                  war_only=[r_pssq[j]] if i == 0 else (),
                  writes=[r_pssq[j]] if i == nch - 1 else (),
                  tag=f"{ph}sqm{i}.{j}")

    var = cx.f32(var_off, n)
    r_var = kb.reg(f"{ph}var")
    negm = cx.f32(sq_off, n)
    r_negm = kb.reg(f"{ph}negm")
    mrps = cx.ps_b[:, 0:n]
    mrbf = sq
    r_mrbf = kb.reg(f"{ph}mrbf")

    kb.op("scalar", lambda sc: sc.activation(var, ps_sq, AF.Copy, scale=1.0 / nd),
          reads=r_pssq, writes=[r_var], tag=f"{ph}msq")
    if want_mean:
        kb.op("scalar", lambda sc: sc.activation(negm, ps_sum, AF.Copy,
                                                 scale=-1.0 / nd),
              reads=r_pssum, war_only=[rsq, rxb], writes=[r_negm],
              tag=f"{ph}negm")
        kb.op("vector", lambda v: v.tensor_mul(mrps, negm, negm),
              reads=[r_negm], war_only=r_pssum, writes=r_pssum, tag=f"{ph}m2")
        kb.op("vector", lambda v: v.tensor_sub(var, var, mrps),
              reads=r_pssum + [r_var], writes=[r_var], tag=f"{ph}var2")
    kb.op("vector", lambda v: v.tensor_scalar_add(var, var, eps),
          reads=[r_var], writes=[r_var], tag=f"{ph}eps")
    kb.op("scalar", lambda sc: sc.activation(var, var, AF.Sqrt),
          reads=[r_var], writes=[r_var], tag=f"{ph}sqrt")
    kb.op("vector", lambda v: v.reciprocal(var, var),
          reads=[r_var], writes=[r_var], tag=f"{ph}recip")
    if src_bf:
        kb.op("vector", lambda v: v.tensor_copy(rbf, var),
              reads=[r_var], writes=[r_rbf], tag=f"{ph}rcast")
    if want_mean:
        kb.op("vector", lambda v: v.tensor_mul(mrps, negm, var),
              reads=[r_negm, r_var], war_only=r_pssum, writes=r_pssum,
              tag=f"{ph}mr")
        if src_bf:
            kb.op("vector", lambda v: v.tensor_copy(mrbf, mrps),
                  reads=r_pssum, war_only=[rsq, r_negm], writes=[r_mrbf],
                  tag=f"{ph}mrcast")

    tmpf = negm
    r_tmpf = kb.reg(f"{ph}tmpf")
    tmpb = xb
    r_tmpb = kb.reg(f"{ph}tmpb")
    for i, (sv, rv) in enumerate(srcs):
        if mode == "into":
            dv, rd = dsts[i]
        else:
            dv = cx.f32(OFF_XR + 8192 * i, T)[:, n0:n0 + n]
            rd = kb.reg(f"xr{i}")
        for j in range(nsub):
            sl = slice(j * 512, (j + 1) * 512)
            if not src_bf:
                kb.op("vector",
                      lambda v, sv=sv, sl=sl: v.tensor_tensor(
                          tmpf[:, sl], sv[:, sl], var[:, sl], ALU.mult),
                      reads=[rv, r_var], war_only=[r_tmpb, r_mrbf, r_negm,
                                                   rsq, rxb],
                      writes=[r_tmpf], tag=f"{ph}a1.{i}.{j}")
                if mode == "into":
                    last = dv if lnw is None else tmpf
                    lastr = rd if lnw is None else r_tmpf
                    if want_mean:
                        kb.op("vector",
                              lambda v, last=last, sl=sl: v.tensor_tensor(
                                  last[:, sl], tmpf[:, sl], mrps[:, sl], ALU.add),
                              reads=[r_tmpf] + r_pssum, writes=[lastr],
                              tag=f"{ph}a2.{i}.{j}")
                    if lnw is not None:
                        wv, rw, colf = lnw
                        kb.op("vector",
                              lambda v, dv=dv, sl=sl, i=i: v.tensor_scalar_mul(
                                  dv[:, sl], tmpf[:, sl], wv[:, colf(i):colf(i) + 1]),
                              reads=[r_tmpf, rw], writes=[rd], tag=f"{ph}aw.{i}.{j}")
                else:
                    kb.op("vector",
                          lambda v, sl=sl: v.tensor_tensor(
                              ps_sq[:, sl], tmpf[:, sl], mrps[:, sl], ALU.add),
                          reads=[r_tmpf] + r_pssum, war_only=[r_pssq[j]],
                          writes=[r_pssq[j]], tag=f"{ph}a2.{i}.{j}")
                    kb.op("vector",
                          lambda v, dv=dv, sl=sl: v.tensor_tensor(
                              dv[:, sl], dv[:, sl], ps_sq[:, sl], ALU.add),
                          reads=[r_pssq[j], rd], writes=[rd], tag=f"{ph}a3.{i}.{j}")
            else:
                kb.op("gpsimd",
                      lambda g, sv=sv, sl=sl: g.tensor_tensor(
                          tmpb[:, sl], sv[:, sl], rbf[:, sl], ALU.mult),
                      reads=[rv, r_rbf], war_only=[r_tmpf, r_negm, rxb],
                      writes=[r_tmpb], tag=f"{ph}a1.{i}.{j}")
                if mode == "into":
                    last = dv if lnw is None else tmpb
                    lastr = rd if lnw is None else r_tmpb
                    if want_mean:
                        kb.op("gpsimd",
                              lambda g, last=last, sl=sl: g.tensor_tensor(
                                  last[:, sl], tmpb[:, sl], mrbf[:, sl], ALU.add),
                              reads=[r_tmpb, r_mrbf], writes=[lastr],
                              tag=f"{ph}a2.{i}.{j}")
                    if lnw is not None:
                        wv, rw, colf = lnw
                        kb.op("vector",
                              lambda v, dv=dv, sl=sl, i=i: v.tensor_scalar_mul(
                                  dv[:, sl], tmpb[:, sl], wv[:, colf(i):colf(i) + 1]),
                              reads=[r_tmpb, rw], writes=[rd], tag=f"{ph}aw.{i}.{j}")
                else:
                    kb.op("vector",
                          lambda v, sl=sl: v.tensor_tensor(
                              ps_sq[:, sl], tmpb[:, sl], mrbf[:, sl], ALU.add),
                          reads=[r_tmpb, r_mrbf], war_only=[r_pssq[j]],
                          writes=[r_pssq[j]], tag=f"{ph}a2.{i}.{j}")
                    kb.op("vector",
                          lambda v, dv=dv, sl=sl: v.tensor_tensor(
                              dv[:, sl], dv[:, sl], ps_sq[:, sl], ALU.add),
                          reads=[r_pssq[j], rd], writes=[rd], tag=f"{ph}a3.{i}.{j}")


def ln_into(cx, ph, srcs, dsts, n, eps=1e-5, src_bf=False, want_mean=True,
            lnw=None):
    ln_apply(cx, ph, srcs, n, eps, want_mean, src_bf, "into", dsts=dsts, lnw=lnw)


def ln_add_into_xr(cx, ph, srcs, n, n0=0, eps=1e-5):
    ln_apply(cx, ph, srcs, n, eps, True, True, "addxr", n0=n0)


def emit_l0a(cx):
    kb = cx.kb
    xr = [(cx.f32(OFF_XR + 8192 * i, T), kb.reg(f"xr{i}")) for i in range(KC)]
    xn = [(cx.bf(OFF_A + 4096 * i, T), kb.reg(f"l0a_xn{i}")) for i in range(KC)]

    ln_into(cx, "l0aln1_", xr, xn, T)
    kb.phase_barrier("l0a_kf")
    xr = [(cx.f32(OFF_XR + 8192 * i, T), kb.reg(f"xr{i}")) for i in range(KC)]
    xn = [(cx.bf(OFF_A + 4096 * i, T), kb.reg(f"l0a_xn{i}")) for i in range(KC)]

    import os as _os
    _nslot = int(_os.environ.get("L0A_SLOTS", 4))
    wslot = [cx.bf(OFF_W + 2048 * (s % _nslot), 1024) for s in range(4)]
    r_wslot = [kb.reg(f"l0a_w{s % _nslot}") for s in range(4)]
    ab = [cx.f32(OFF_U + 8192 * s, 2048) for s in range(4)]
    r_ab = [kb.reg(f"l0a_ab{s}") for s in range(4)]
    h = [(cx.bf(OFF_B + 4096 * i, T), kb.reg(f"l0a_h{i}")) for i in range(KC)]
    pz = cx.f32(OFF_W + 8192, 2048)
    msk = cx.u8(OFF_W + 16384, 2048)
    r_pz, r_msk = kb.reg("l0a_pz"), kb.reg("l0a_msk")
    wf = cx.inp["wfT_strips"]

    import os
    npairs = int(os.environ.get("L0A_PAIRS", KC))
    for i in range(npairs):
        sk = (i % 2) * 2
        kb.dma("gpsimd", lambda g, i=i, sk=sk: g.dma_start(wslot[sk], wf[i]),
               writes=[r_wslot[sk]], tag=f"l0a_ldwk{i}")
        kb.dma("gpsimd", lambda g, i=i, sk=sk: g.dma_start(wslot[sk + 1], wf[i + 8]),
               writes=[r_wslot[sk + 1]], tag=f"l0a_ldwh{i}")
        r_psk = [cx.psa(j) for j in range(4)]
        r_psh = [cx.psb(j) for j in range(4)]
        for kc in range(KC):
            for j in range(4):
                sl = slice(j * 512, (j + 1) * 512)
                kb.op("tensor",
                      lambda t, sk=sk, kc=kc, sl=sl: t.matmul(
                          cx.ps_a[:, sl], wslot[sk][:, kc * 128:(kc + 1) * 128],
                          xn[kc][0][:, sl], start=(kc == 0), stop=(kc == KC - 1)),
                      reads=[r_wslot[sk], xn[kc][1]],
                      war_only=[r_psk[j]] if kc == 0 else (),
                      writes=[r_psk[j]] if kc == KC - 1 else (),
                      tag=f"l0a_mmk{i}.{j}.{kc}")
        for kc in range(KC):
            for j in range(4):
                sl = slice(j * 512, (j + 1) * 512)
                kb.op("tensor",
                      lambda t, sk=sk, kc=kc, sl=sl: t.matmul(
                          cx.ps_b[:, sl], wslot[sk + 1][:, kc * 128:(kc + 1) * 128],
                          xn[kc][0][:, sl], start=(kc == 0), stop=(kc == KC - 1)),
                      reads=[r_wslot[sk + 1], xn[kc][1]],
                      war_only=[r_psh[j]] if kc == 0 else (),
                      writes=[r_psh[j]] if kc == KC - 1 else (),
                      tag=f"l0a_mmh{i}.{j}.{kc}")
        sa = (i % 2) * 2
        _el = int(os.environ.get("L0A_EL", 8)) if i > 0 else 8
        if _el == 0:
            continue
        kb.op("scalar", lambda sc, sa=sa: sc.activation(ab[sa], cx.ps_a[:, 0:T],
                                                        AF.Sigmoid, scale=-1.0),
              reads=r_psk, writes=[r_ab[sa]], tag=f"l0a_a{i}")
        if _el >= 2:
            kb.op("scalar", lambda sc, sa=sa: sc.activation(ab[sa + 1], cx.ps_b[:, 0:T],
                                                            AF.Sigmoid),
                  reads=r_psh, writes=[r_ab[sa + 1]], tag=f"l0a_sg{i}")
        if _el >= 3:
            kb.op("vector", lambda v: v.tensor_scalar_add(pz, cx.ps_b[:, 0:T], 0.5),
                  reads=r_psh, writes=[r_pz], tag=f"l0a_pos{i}")
        if _el >= 4:
            kb.op("vector", lambda v: v.tensor_scalar(msk, cx.ps_b[:, 0:T], 0.0,
                                                      None, ALU.is_ge),
                  reads=r_psh, writes=[r_msk], tag=f"l0a_msk{i}")
        if _el >= 5:
            kb.op("vector", lambda v, sa=sa: v.copy_predicated(ab[sa + 1], msk, pz),
                  reads=[r_msk, r_pz], writes=[r_ab[sa + 1]], tag=f"l0a_g{i}")
        if _el >= 6:
            kb.op("scalar", lambda sc: sc.activation(pz, cx.ps_a[:, 0:T], AF.Sigmoid),
                  reads=r_psk, writes=[r_pz], tag=f"l0a_z{i}")
        if _el >= 7:
            kb.op("vector", lambda v, sa=sa: v.tensor_mul(ab[sa + 1], ab[sa + 1], pz),
                  reads=[r_pz, r_ab[sa + 1]], writes=[r_ab[sa + 1]], tag=f"l0a_b{i}")
        if _el >= 8:
            kb.op("vector", lambda v, sa=sa, i=i: v.tensor_tensor_scan(
                h[i][0], ab[sa], ab[sa + 1], 0.5, ALU.mult, ALU.add),
                  reads=[r_ab[sa], r_ab[sa + 1]], writes=[h[i][1]], tag=f"l0a_scan{i}")

    import os
    if os.environ.get("L0A_DBG_H"):
        kb.phase_barrier("dbg")
        for i in range(KC):
            hv = cx.bf(OFF_B + 4096 * i, T)
            kb.op("vector", lambda v, hv=hv, i=i: v.tensor_copy(
                cx.f32(OFF_XR + 8192 * i, T), hv),
                reads=[kb.reg(f"dbg_h{i}")], writes=[kb.reg(f"xr{i}")], tag="dbgh")
        return
    if os.environ.get("L0A_CUT") == "1":
        return
    kb.phase_barrier("l0a_ln2")
    h = [(cx.bf(OFF_B + 4096 * i, T), kb.reg(f"l0a2_h{i}")) for i in range(KC)]
    ln_add_into_xr(cx, "l0aln2_", h, T)


def emit_l0f_attn(cx):
    kb = cx.kb
    H, HD = 16, 64
    xr = [(cx.f32(OFF_XR + 8192 * i, T), kb.reg(f"xr{i}")) for i in range(KC)]
    vi = [(cx.bf(OFF_B + 4096 * i, T), kb.reg(f"vi{i}")) for i in range(KC)]
    ln_apply(cx, "f0_", xr, T, 1e-5, True, False, "into", dsts=vi)
    kb.phase_barrier("l0f_f1")
    vi = [(cx.bf(OFF_B + 4096 * i, T), kb.reg(f"vi{i}")) for i in range(KC)]
    xn = [(cx.bf(OFF_A + 4096 * i, T), kb.reg(f"xn{i}")) for i in range(KC)]
    lnw_v = cx.f32(OFF_SM2 + 256, 8)
    r_lnw = kb.reg("ln1w")
    kb.dma("scalar", lambda s: s.dma_start(lnw_v, cx.inp["cw_ln1w"]),
           writes=[r_lnw], tag="ld_ln1w")
    ln_apply(cx, "f1_", vi, T, 1e-6, False, True, "into", dsts=xn,
             lnw=(lnw_v, r_lnw, lambda i: i))
    kb.phase_barrier("l0f_kv")
    xn = [(cx.bf(OFF_A + 4096 * i, T), kb.reg(f"xn{i}")) for i in range(KC)]

    wkv = cx.bf(OFF_W + 10240, 2048)
    r_wkv = kb.reg("wkv")
    kb.dma("gpsimd", lambda g: g.dma_start(wkv, cx.inp["l0_wkv"]),
           writes=[r_wkv], tag="ld_wkv")
    kv = cx.bf(OFF_SCR, 2 * T)
    r_kv = kb.reg("kv")
    for c in range(2):
        for kc in range(KC):
            for j in range(4):
                sl = slice(j * 512, (j + 1) * 512)
                kb.op("tensor", lambda t, c=c, kc=kc, sl=sl: t.matmul(
                    cx.ps_a[:, sl], wkv[:, kc * 256 + c * 128: kc * 256 + c * 128 + 128],
                    xn[kc][0][:, sl], start=(kc == 0), stop=(kc == KC - 1)),
                    reads=[r_wkv, xn[kc][1]],
                    war_only=[cx.psa(j)] if kc == 0 else (),
                    writes=[cx.psa(j)] if kc == KC - 1 else (),
                    tag=f"kvmm{c}.{j}")
        kb.op("scalar", lambda sc, c=c: sc.activation(
            kv[:, c * T:(c + 1) * T], cx.ps_a[:, 0:T], AF.Copy, scale=0.25),
            reads=[cx.psa(j) for j in range(4)], writes=[r_kv], tag=f"kvev{c}")

    kb.phase_barrier("l0f_heads")
    vi = [(cx.bf(OFF_B + 4096 * i, T), kb.reg(f"vi{i}")) for i in range(KC)]
    xn = [(cx.bf(OFF_A + 4096 * i, T), kb.reg(f"xn{i}")) for i in range(KC)]
    kv = cx.bf(OFF_SCR, 2 * T)
    r_kv = kb.reg("kv")

    ropeq = cx.bf(OFF_SM, T)
    ropek = cx.bf(OFF_SM + 4096, T)
    causal = cx.f32(OFF_SCR + 8192, 128)
    kup = cx.bf(OFF_W + 2048, 2048)
    vup = cx.bf(OFF_W + 6144, 2048)
    r_ropeq, r_ropek, r_causal = kb.reg("ropeq"), kb.reg("ropek"), kb.reg("causal")
    r_kup, r_vup = kb.reg("kup"), kb.reg("vup")
    kb.dma("scalar", lambda s: s.dma_start(ropeq, cx.inp["c_ropeq"]),
           writes=[r_ropeq], tag="ld_ropeq")
    kb.dma("scalar", lambda s: s.dma_start(ropek, cx.inp["c_ropek"]),
           writes=[r_ropek], tag="ld_ropek")
    kb.dma("scalar", lambda s: s.dma_start(causal, cx.inp["c_causal"]),
           writes=[r_causal], tag="ld_causal")
    kb.dma("gpsimd", lambda g: g.dma_start(kup, cx.inp["l0_kup"]),
           writes=[r_kup], tag="ld_kup")
    kb.dma("gpsimd", lambda g: g.dma_start(vup, cx.inp["l0_vup"]),
           writes=[r_vup], tag="ld_vup")

    qv = cx.bf(OFF_W + 10240, 2048)
    kh = cx.bf(OFF_W + 14336, 2048)
    tp = cx.bf(OFF_W + 18432, 2048)
    vt = cx.bf(OFF_W + 22528, 1024)
    at = cx.bf(OFF_W + 24576, 1024)
    ctxb = cx.bf(OFF_SCR + 8704, 64)
    nmax = cx.f32(OFF_SM2, 1)
    ssum = cx.f32(OFF_SM2 + 4, 1)
    rsum = cx.f32(OFF_SM2 + 8, 1)
    wq = [cx.bf(OFF_W + 1024 * sl_, 512) for sl_ in range(2)]
    r_wq = [kb.reg(f"wq{s}") for s in range(2)]
    r_qv, r_kh, r_tp, r_vt, r_at = (kb.reg("qv"), kb.reg("kh"), kb.reg("tp"),
                                    kb.reg("vt"), kb.reg("ctb"))
    r_ctxb = kb.reg("ctxb")
    r_nmax, r_ssum, r_rsum = kb.reg("nmax"), kb.reg("ssum"), kb.reg("rsum")
    r_ctx = [kb.reg(f"ctx{c}") for c in range(KC)]
    psbf = cx.ps_b[:].bitcast(BF16)

    def rope_into(dst, dst_r, pse, pse_regs, table, table_r, tag):
        HT = T // 2
        for hf in range(2):
            hc = slice(hf * HT, (hf + 1) * HT)
            L = slice(0, HT)
            R = slice(HT, T)
            kb.op("vector", lambda v, hc=hc: v.tensor_tensor(
                tp[0:32, L], pse[0:32, 0:T][:, hc], table[0:32, hc], ALU.mult),
                reads=pse_regs + [table_r], war_only=[r_tp], writes=[r_tp],
                tag=f"{tag}pa{hf}")
            kb.op("vector", lambda v, hc=hc: v.tensor_tensor(
                tp[0:32, R], pse[32:64, 0:T][:, hc], table[32:64, hc], ALU.mult),
                reads=pse_regs + [table_r], writes=[r_tp], tag=f"{tag}pb{hf}")
            kb.op("vector", lambda v, hc=hc: v.tensor_sub(
                dst[0:32, hc], tp[0:32, L], tp[0:32, R]),
                reads=[r_tp], writes=[dst_r], tag=f"{tag}c1{hf}")
            kb.op("vector", lambda v, hc=hc: v.tensor_tensor(
                tp[32:64, L], pse[0:32, 0:T][:, hc], table[64:96, hc], ALU.mult),
                reads=pse_regs + [table_r], writes=[r_tp], tag=f"{tag}pc{hf}")
            kb.op("vector", lambda v, hc=hc: v.tensor_tensor(
                tp[32:64, R], pse[32:64, 0:T][:, hc], table[96:128, hc], ALU.mult),
                reads=pse_regs + [table_r], writes=[r_tp], tag=f"{tag}pd{hf}")
            kb.op("vector", lambda v, hc=hc: v.tensor_add(
                dst[32:64, hc], tp[32:64, L], tp[32:64, R]),
                reads=[r_tp], writes=[dst_r], tag=f"{tag}c2{hf}")

    for h in range(H):
        wslot = h % 2
        kb.dma("gpsimd", lambda g, h=h, wslot=wslot: g.dma_start(
            wq[wslot], cx.inp["l0_wq_slices"][h]),
            writes=[r_wq[wslot]], tag=f"ldwq{h}")
        for kc in range(KC):
            for j in range(4):
                sl = slice(j * 512, (j + 1) * 512)
                kb.op("tensor", lambda t, kc=kc, sl=sl, wslot=wslot: t.matmul(
                    cx.ps_a[0:64, sl], wq[wslot][:, kc * 64:(kc + 1) * 64],
                    xn[kc][0][:, sl], start=(kc == 0), stop=(kc == KC - 1)),
                    reads=[r_wq[wslot], xn[kc][1]],
                    war_only=[cx.psa(j)] if kc == 0 else (),
                    writes=[cx.psa(j)] if kc == KC - 1 else (),
                    tag=f"qmm{h}.{j}")
        rope_into(qv, r_qv, cx.ps_a, [cx.psa(j) for j in range(4)], ropeq,
                  r_ropeq, f"q{h}")
        for j in range(4):
            sl = slice(j * 512, (j + 1) * 512)
            for c in range(2):
                kb.op("tensor", lambda t, c=c, sl=sl, h=h: t.matmul(
                    cx.ps_b[0:64, sl], vup[:, h * 128 + c * 64: h * 128 + c * 64 + 64],
                    kv[:, c * T:(c + 1) * T][:, sl], start=(c == 0), stop=(c == 1)),
                    reads=[r_vup, r_kv],
                    war_only=[cx.psb(j)] if c == 0 else (),
                    writes=[cx.psb(j)] if c == 1 else (),
                    tag=f"vmm{h}.{j}")
        kb.op("scalar", lambda sc: sc.copy(qv[64:128, :], cx.ps_b[0:64, 0:T]),
              reads=[cx.psb(j) for j in range(4)], writes=[r_qv], tag=f"vev{h}")
        for j in range(4):
            sl = slice(j * 512, (j + 1) * 512)
            for c in range(2):
                kb.op("tensor", lambda t, c=c, sl=sl, h=h: t.matmul(
                    cx.ps_b[0:64, sl], kup[:, h * 128 + c * 64: h * 128 + c * 64 + 64],
                    kv[:, c * T:(c + 1) * T][:, sl], start=(c == 0), stop=(c == 1)),
                    reads=[r_kup, r_kv],
                    war_only=[cx.psb(j)] if c == 0 else (),
                    writes=[cx.psb(j)] if c == 1 else (),
                    tag=f"kmm{h}.{j}")
        rope_into(kh, r_kh, cx.ps_b, [cx.psb(j) for j in range(4)], ropek,
                  r_ropek, f"k{h}")
        for g8 in range(2):
            for jj in range(8):
                j = g8 * 8 + jj
                kb.op("tensor", lambda t, j=j, jj=jj: t.transpose(
                    psbf[:, 3072 + jj * 64: 3072 + (jj + 1) * 64],
                    qv[64:128, j * 128:(j + 1) * 128],
                    cx.bf(OFF_IDENT, 128)[64:128, 64:128]),
                    reads=[r_qv, cr(cx, "c_ident")],
                    war_only=[cx.psb(3)] if jj == 0 else (),
                    writes=[cx.psb(3)] if jj == 7 else (),
                    tag=f"vtp{h}.{j}")
            kb.op("vector", lambda v, g8=g8: v.tensor_copy(
                vt[:, g8 * 512:(g8 + 1) * 512], psbf[:, 3072:3584]),
                reads=[cx.psb(3)], writes=[r_vt], tag=f"vtc{h}.{g8}")
        for qt in range(16):
            kl = (qt + 1) * 128
            nsub = (kl + 511) // 512
            for j in range(nsub):
                sl = slice(j * 512, min(kl, (j + 1) * 512))
                kb.op("tensor", lambda t, qt=qt, sl=sl: t.matmul(
                    cx.ps_a[:, sl], qv[0:64, qt * 128:(qt + 1) * 128],
                    kh[0:64, sl], start=True, stop=True),
                    reads=[r_qv, r_kh],
                    war_only=[cx.psa(sl.start // 512)],
                    writes=[cx.psa(sl.start // 512)],
                    tag=f"sc{h}.{qt}.{j}")
            kb.op("vector", lambda v, qt=qt, kl=kl: v.tensor_tensor(
                cx.ps_a[:, kl - 128:kl], cx.ps_a[:, kl - 128:kl], causal, ALU.add),
                reads=[cx.psa((kl - 128) // 512), r_causal],
                writes=[cx.psa((kl - 128) // 512)], tag=f"msk{h}.{qt}")
            kb.op("vector", lambda v, kl=kl: v.reduce_max(
                nmax, cx.ps_a[:, 0:kl], axis=AX.X, negate=True),
                reads=[cx.psa(j) for j in range((kl + 511) // 512)],
                writes=[r_nmax], tag=f"mx{h}.{qt}")
            kb.op("scalar", lambda sc, kl=kl: sc.activation(
                tp[:, 0:kl], cx.ps_a[:, 0:kl], AF.Exp, bias=nmax,
                accum_out=ssum),
                reads=[r_nmax] + [cx.psa(j) for j in range((kl + 511) // 512)],
                war_only=[r_tp], writes=[r_tp, r_ssum], tag=f"exp{h}.{qt}")
            kb.op("vector", lambda v: v.reciprocal(rsum, ssum),
                  reads=[r_ssum], writes=[r_rsum], tag=f"rs{h}.{qt}")
            nj = qt + 1
            for g8 in range((nj + 7) // 8):
                jlo = g8 * 8
                jhi = min(nj, jlo + 8)
                for j in range(jlo, jhi):
                    kb.op("tensor", lambda t, j=j, jlo=jlo: t.transpose(
                        psbf[:, 3072 + (j - jlo) * 128: 3072 + (j - jlo + 1) * 128],
                        tp[:, j * 128:(j + 1) * 128], cx.bf(OFF_IDENT, 128)),
                        reads=[r_tp, cr(cx, "c_ident")],
                        war_only=[cx.psb(3)] if j == jlo else (),
                        writes=[cx.psb(3)] if j == jhi - 1 else (),
                        tag=f"at{h}.{qt}.{j}")
                kb.op("vector", lambda v, jlo=jlo, jhi=jhi: v.tensor_copy(
                    at[:, 0:(jhi - jlo) * 128], psbf[:, 3072:3072 + (jhi - jlo) * 128]),
                    reads=[cx.psb(3)], war_only=[r_at], writes=[r_at],
                    tag=f"atc{h}.{qt}.{g8}")
                for j in range(jlo, jhi):
                    kb.op("tensor", lambda t, j=j, jlo=jlo, nj=nj: t.matmul(
                        cx.ps_b[:, 0:64], at[:, (j - jlo) * 128:(j - jlo + 1) * 128],
                        vt[:, j * 64:(j + 1) * 64],
                        start=(j == 0), stop=(j == nj - 1), skip_group_check=True),
                        reads=[r_at, r_vt],
                        war_only=[cx.psb(0)] if j == 0 else (),
                        writes=[cx.psb(0)] if j == nj - 1 else (),
                        tag=f"av{h}.{qt}.{j}")
            kb.op("vector", lambda v: v.tensor_scalar_mul(ctxb, cx.ps_b[:, 0:64],
                                                          rsum),
                  reads=[cx.psb(0), r_rsum], war_only=[r_ctxb], writes=[r_ctxb],
                  tag=f"cx{h}.{qt}")
            kb.op("tensor", lambda t: t.transpose(
                psbf[:, 3072:3200][0:64, :], ctxb, cx.bf(OFF_IDENT, 128)),
                reads=[r_ctxb, cr(cx, "c_ident")], war_only=[cx.psb(3)],
                writes=[cx.psb(3)], tag=f"cxt{h}.{qt}")
            c = h // 2
            r0 = (h % 2) * 64
            kb.op("vector", lambda v, c=c, r0=r0, qt=qt: v.tensor_copy(
                cx.bf(OFF_U + 4096 * c, T, rows=64, row0=r0)[:, qt * 128:(qt + 1) * 128],
                psbf[0:64, 3072:3200]),
                reads=[cx.psb(3)], writes=[r_ctx[c]], tag=f"cxc{h}.{qt}")

    kb.phase_barrier("l0f_wo")
    vi = [(cx.bf(OFF_B + 4096 * i, T), kb.reg(f"vi{i}")) for i in range(KC)]
    r_ctx = [kb.reg(f"ctx{c}") for c in range(KC)]
    wo = cx.bf(OFF_W, 8192)
    r_wo = kb.reg("wo")
    kb.dma("gpsimd", lambda g: g.dma_start(wo, cx.inp["l0_wo"]),
           writes=[r_wo], tag="ld_wo")
    stage = cx.bf(OFF_W + 24576, 512)
    r_stage = kb.reg("wo_stage")
    for oc in range(KC):
        for kc in range(KC):
            for j in range(4):
                sl = slice(j * 512, (j + 1) * 512)
                kb.op("tensor", lambda t, oc=oc, kc=kc, sl=sl: t.matmul(
                    cx.ps_a[:, sl], wo[:, kc * 1024 + oc * 128: kc * 1024 + (oc + 1) * 128],
                    cx.bf(OFF_U + 4096 * kc, T)[:, sl],
                    start=(kc == 0), stop=(kc == KC - 1)),
                    reads=[r_wo, r_ctx[kc]],
                    war_only=[cx.psa(j)] if kc == 0 else (),
                    writes=[cx.psa(j)] if kc == KC - 1 else (),
                    tag=f"womm{oc}.{j}")
        for j in range(4):
            sl = slice(j * 512, (j + 1) * 512)
            kb.op("scalar", lambda sc, sl=sl: sc.copy(stage, cx.ps_a[:, sl]),
                  reads=[cx.psa(sl.start // 512)], war_only=[r_stage],
                  writes=[r_stage], tag=f"woev{oc}.{j}")
            kb.op("vector", lambda v, oc=oc, sl=sl: v.tensor_add(
                vi[oc][0][:, sl], vi[oc][0][:, sl], stage),
                reads=[r_stage, vi[oc][1]], writes=[vi[oc][1]],
                tag=f"woadd{oc}.{j}")
    import os
    if os.environ.get("L0F_DBG"):
        kb.phase_barrier("l0f_dbg")
        for i in range(KC):
            kb.op("vector", lambda v, i=i: v.tensor_copy(
                cx.f32(OFF_XR + 8192 * i, T), cx.bf(OFF_B + 4096 * i, T)),
                reads=[kb.reg(f"dbg_vi{i}")], writes=[kb.reg(f"xr{i}")],
                tag="dbgvi")


def emit_l0f_moe(cx):
    kb = cx.kb
    HN = T // 2
    import os as _os0
    _ths = [int(c) for c in _os0.environ.get("L0F_MOE_TH", "01")]
    for th in _ths:
        n0 = th * HN
        kb.phase_barrier(f"moe{th}")
        vi = [(cx.bf(OFF_B + 4096 * i, T)[:, n0:n0 + HN], kb.reg(f"vi{i}"))
              for i in range(KC)]

        def xn2v(c):
            if c < 6:
                return cx.bf(OFF_SCR + 2048 * c, HN)
            return cx.bf(OFF_SM + 2048 * (c - 6), HN)

        xn2 = [(xn2v(c), kb.reg(f"xn2_{c}")) for c in range(KC)]
        import os as _os5
        if _os5.environ.get("L0F_PROBS_DBG") and th == 1:
            kb.op("vector", lambda v: v.tensor_copy(
                cx.f32(OFF_XR + 6 * 8192, T), cx.bf(OFF_B, T)),
                reads=[kb.reg("vi0")], writes=[kb.reg("xr6"), kb.reg("xr7")],
                tag="dbg7")
        lnw_v = cx.f32(OFF_SM2 + 256, 8)
        r_lnw = kb.reg("ln2w")
        kb.dma("scalar", lambda s: s.dma_start(lnw_v, cx.inp["cw_ln2w"]),
               writes=[r_lnw], tag="ld_ln2w")
        ln_apply(cx, f"m{th}r_", vi, HN, 1e-6, False, True, "into", dsts=xn2,
                 lnw=(lnw_v, r_lnw, lambda i: i),
                 homes=(OFF_A, OFF_A + 2048, OFF_A + 4096, OFF_A + 8192))
        import os as _os4
        if _os4.environ.get("L0F_PROBS_DBG") and th == 1:
            kb.op("vector", lambda v: v.tensor_copy(
                cx.f32(OFF_XR + 3 * 8192, HN), cx.bf(OFF_A + 4096, HN)),
                reads=[kb.reg(f"m{th}r_rbf")], writes=[kb.reg("xr3")], tag="dbg4")
            kb.op("vector", lambda v: v.tensor_copy(
                cx.f32(OFF_XR + 4 * 8192, HN), cx.f32(OFF_A + 8192, HN)),
                reads=[kb.reg(f"m{th}r_var")], writes=[kb.reg("xr4")], tag="dbg5")
            kb.op("vector", lambda v: v.tensor_copy(
                cx.f32(OFF_XR + 5 * 8192, HN),
                cx.bf(OFF_B, T)[:, 1024:2048]),
                reads=[kb.reg("vi0")], writes=[kb.reg("xr5")], tag="dbg6")

        wr = cx.bf(OFF_A + 24576, 64)
        r_wr = kb.reg("wr")
        kb.dma("gpsimd", lambda g: g.dma_start(wr, cx.inp["l0_wr"]),
               writes=[r_wr], tag="ld_wr")
        rbias = cx.f32(OFF_SM2 + 128, 1, rows=8)
        r_rb = kb.reg("rbias")
        kb.dma("scalar", lambda s: s.dma_start(rbias, cx.inp["cw_rbias"]),
               writes=[r_rb], tag="ld_rbias")
        for j in range(2):
            sl = slice(j * 512, (j + 1) * 512)
            for kc in range(KC):
                kb.op("tensor", lambda t, kc=kc, sl=sl: t.matmul(
                    cx.ps_a[0:8, sl], wr[:, kc * 8:(kc + 1) * 8],
                    xn2[kc][0][:, sl], start=(kc == 0), stop=(kc == KC - 1)),
                    reads=[r_wr, xn2[kc][1]],
                    war_only=[cx.psa(j)] if kc == 0 else (),
                    writes=[cx.psa(j)] if kc == KC - 1 else (),
                    tag=f"rtm{th}.{j}")
        probs = cx.f32(OFF_A + 12288, HN, rows=8)
        probs_bf = cx.bf(OFF_A + 16384, HN, rows=8)
        r_probs = kb.reg("probs")
        kb.op("vector", lambda v: v.tensor_scalar_add(probs, cx.ps_a[0:8, 0:HN],
                                                      rbias),
              reads=[r_rb, cx.psa(0), cx.psa(1)], writes=[r_probs],
              tag=f"rt{th}bias")
        kb.op("vector", lambda v: v.tensor_scalar(probs, probs, -50.0, 50.0,
                                                  ALU.max, ALU.min),
              reads=[r_probs], writes=[r_probs], tag=f"rt{th}clip")
        kb.op("scalar", lambda sc: sc.activation(probs, probs, AF.Sigmoid),
              reads=[r_probs], writes=[r_probs], tag=f"rt{th}sig")
        kb.op("vector", lambda v: v.tensor_copy(probs_bf, probs),
              reads=[r_probs], writes=[r_probs], tag=f"rt{th}cast")
        pt = cx.bf(OFF_A + 18432, 64)
        r_pt = kb.reg("pt")
        psbf = cx.ps_b[:].bitcast(BF16)
        for tc in range(8):
            kb.op("tensor", lambda t, tc=tc: t.transpose(
                psbf[:, 3072 + tc * 8: 3072 + (tc + 1) * 8][0:128, :],
                probs_bf[:, tc * 128:(tc + 1) * 128],
                cx.bf(OFF_IDENT, 128)[0:8, 0:8]),
                reads=[r_probs, cr(cx, "c_ident")],
                war_only=[cx.psb(3)] if tc == 0 else (),
                writes=[cx.psb(3)] if tc == 7 else (),
                tag=f"rt{th}tp{tc}")
        kb.op("vector", lambda v: v.tensor_copy(pt, psbf[:, 3072:3136]),
              reads=[cx.psb(3)], writes=[r_pt], tag=f"rt{th}ptc")
        import os as _os3
        if _os3.environ.get("L0F_PROBS_DBG") and th == 1:
            kb.op("vector", lambda v: v.tensor_copy(
                cx.f32(OFF_XR, HN, rows=8), probs),
                reads=[r_probs], writes=[kb.reg("xr0")], tag="dbgp1")
            kb.op("vector", lambda v: v.tensor_copy(
                cx.f32(OFF_XR + 8192, 64), pt),
                reads=[r_pt], writes=[kb.reg("xr1")], tag="dbgp2")
            kb.op("vector", lambda v: v.tensor_copy(
                cx.f32(OFF_XR + 16384, HN), xn2[0][0]),
                reads=[xn2[0][1]], writes=[kb.reg("xr2")], tag="dbgp3")
        wtm = cx.f32(OFF_A + 20480, 16)
        wtmb = cx.bf(OFF_A + 20608, 16)
        r_wtm = kb.reg("wtm")
        m1 = cx.f32(OFF_SM2, 1)
        pcl = cx.f32(OFF_SM2 + 4, 1)
        twi = cx.f32(OFF_SM2 + 8, 1)
        sw = cx.f32(OFF_SM2 + 12, 1)
        msk1 = cx.f32(OFF_SM2 + 16, 1)
        r_m1, r_p, r_twi, r_sw, r_mk = (kb.reg("m1"), kb.reg("pcl"),
                                        kb.reg("twi"), kb.reg("sw"), kb.reg("mk"))
        wt_sb = cx.bf(OFF_SM + 4096, 1024, rows=16)
        r_wtsb = kb.reg("wt_sb")
        for tc in range(8):
            ptv = pt[:, tc * 8:(tc + 1) * 8]
            kb.op("vector", lambda v, ptv=ptv: v.reduce_max(m1, ptv, axis=AX.X),
                  reads=[r_pt], writes=[r_m1], tag=f"w{th}.{tc}m1")
            kb.op("vector", lambda v: v.tensor_scalar(pcl, m1, 1e-8, 1.0 - 1e-8,
                                                      ALU.max, ALU.min),
                  reads=[r_m1], writes=[r_p], tag=f"w{th}.{tc}p")
            kb.op("vector", lambda v: v.tensor_scalar(twi, pcl, 0.5 + 1e-8, 0.5,
                                                      ALU.add, ALU.max),
                  reads=[r_p], writes=[r_twi], tag=f"w{th}.{tc}t1")
            kb.op("vector", lambda v: v.tensor_scalar_min(twi, twi, 2.0),
                  reads=[r_twi], writes=[r_twi], tag=f"w{th}.{tc}t2")
            kb.op("vector", lambda v: v.reciprocal(twi, twi),
                  reads=[r_twi], writes=[r_twi], tag=f"w{th}.{tc}t3")
            kb.op("vector", lambda v: v.tensor_scalar_mul(wtm[:, 8:9], twi, 0.5),
                  reads=[r_twi], writes=[r_wtm], tag=f"w{th}.{tc}s0")
            kb.op("vector", lambda v: v.tensor_mul(sw, pcl, twi),
                  reads=[r_p, r_twi], writes=[r_sw], tag=f"w{th}.{tc}sw")
            for e in range(8):
                kb.op("vector", lambda v, ptv=ptv, e=e: v.tensor_scalar(
                    msk1, ptv[:, e:e + 1], m1, None, ALU.is_ge),
                    reads=[r_pt, r_m1], writes=[r_mk], tag=f"w{th}.{tc}k{e}")
                kb.op("vector", lambda v, e=e: v.tensor_mul(wtm[:, e:e + 1],
                                                            msk1, sw),
                      reads=[r_mk, r_sw], writes=[r_wtm], tag=f"w{th}.{tc}e{e}")
            kb.op("vector", lambda v: v.tensor_copy(wtmb, wtm),
                  reads=[r_wtm], writes=[r_wtm], tag=f"w{th}.{tc}c")
            kb.op("tensor", lambda t: t.transpose(
                psbf[0:16, 3072:3200], wtmb, cx.bf(OFF_IDENT, 128)),
                reads=[r_wtm, cr(cx, "c_ident")], war_only=[cx.psb(3)],
                writes=[cx.psb(3)], tag=f"w{th}.{tc}tp")
            kb.op("vector", lambda v, tc=tc: v.tensor_copy(
                wt_sb[:, tc * 128:(tc + 1) * 128], psbf[0:16, 3072:3200]),
                reads=[cx.psb(3)], writes=[r_wtsb], tag=f"w{th}.{tc}cc")

        if _os5.environ.get("L0F_TH1_STOP") and th == 1:
            continue
        kb.phase_barrier(f"moe{th}x")
        xn2 = [(xn2v(c), kb.reg(f"xn2_{c}")) for c in range(KC)]
        wt_sb = cx.bf(OFF_SM + 4096, 1024, rows=16)
        r_wtsb = kb.reg("wt_sb")
        vi = [(cx.bf(OFF_B + 4096 * i, T)[:, n0:n0 + HN], kb.reg(f"vi{i}"))
              for i in range(KC)]
        uacc = [(cx.f32(OFF_U + 4096 * i, HN), kb.reg(f"uacc{i}"))
                for i in range(KC)]
        for i in range(KC):
            kb.op("vector",
                  lambda v, uu=uacc[i][0], vv=vi[i][0]: v.tensor_copy(uu, vv),
                  reads=[vi[i][1]], writes=[uacc[i][1]], tag=f"u{th}init{i}")

        w1 = [cx.bf(OFF_W + 4096 * k, 2048) for k in range(2)]
        w3 = [cx.bf(OFF_W + 8192 + 4096 * k, 2048) for k in range(2)]
        w2 = [cx.bf(OFF_W + 16384 + 4096 * k, 2048) for k in range(2)]
        r_w1 = [kb.reg(f"mw1{k}") for k in range(2)]
        r_w3 = [kb.reg(f"mw3{k}") for k in range(2)]
        r_w2 = [kb.reg(f"mw2{k}") for k in range(2)]
        wrow = cx.bf(OFF_W + 24576, 1024, rows=1)
        r_wrow = kb.reg("wrow")
        sg_sb = cx.bf(OFF_W + 24576, 1024)
        r_sg = kb.reg("sg_sb")
        wbc = cx.bf(OFF_SM + 6144, 1024)
        r_wbc = kb.reg("wbc")
        hch = [(cx.bf(OFF_A + 2048 * c, HN), kb.reg(f"h{c}")) for c in range(16)]

        NE = int(os.environ.get("L0F_MOE_NE", 9)) if "os" in dir() else 9
        import os as _os
        NE = int(_os.environ.get("L0F_MOE_NE", 9))
        for e in ([8] if NE == -1 else range(NE)):
            kb.dma("gpsimd", lambda g, e=e: g.dma_start(
                wrow, wt_sb[e:e + 1, :]),
                reads=[r_wtsb], war_only=[r_sg], writes=[r_wrow],
                tag=f"e{th}.{e}row")
            for j in range(2):
                sl = slice(j * 512, (j + 1) * 512)
                kb.op("tensor", lambda t, sl=sl: t.matmul(
                    cx.ps_b[:, sl], cx.bf(OFF_ONES128, 128)[0:1, :],
                    wrow[:, sl], start=True, stop=True),
                    reads=[r_wrow, cr(cx, "c_ones128")],
                    war_only=[cx.psb(j)], writes=[cx.psb(j)],
                    tag=f"e{th}.{e}bc{j}")
            kb.op("vector", lambda v: v.tensor_copy(wbc, cx.ps_b[:, 0:1024]),
                  reads=[cx.psb(0), cx.psb(1)], war_only=[r_wbc],
                  writes=[r_wbc], tag=f"e{th}.{e}bcc")
            for st_ in range(8):
                sk = st_ % 2
                kb.dma("gpsimd", lambda g, e=e, st_=st_, sk=sk: g.dma_start(
                    w1[sk], cx.inp["l0_moe_w1"][e, st_]),
                    writes=[r_w1[sk]], tag=f"e{th}.{e}.{st_}w1")
                kb.dma("gpsimd", lambda g, e=e, st_=st_, sk=sk: g.dma_start(
                    w3[sk], cx.inp["l0_moe_w3"][e, st_]),
                    writes=[r_w3[sk]], tag=f"e{th}.{e}.{st_}w3")
                for c in range(2):
                    hc = st_ * 2 + c
                    pr = hc % 2
                    gps = cx.ps_a[:, pr * 1024:(pr + 1) * 1024]
                    ups = cx.ps_b[:, pr * 1024:(pr + 1) * 1024]
                    gr = [cx.psa(pr * 2), cx.psa(pr * 2 + 1)]
                    ur = [cx.psb(pr * 2), cx.psb(pr * 2 + 1)]
                    for kc in range(KC):
                        for j in range(2):
                            sl = slice(j * 512, (j + 1) * 512)
                            kb.op("tensor", lambda t, sk=sk, c=c, kc=kc, sl=sl,
                                  gps=gps: t.matmul(gps[:, sl],
                                           w1[sk][:, kc * 256 + c * 128:
                                                  kc * 256 + c * 128 + 128],
                                           xn2[kc][0][:, sl],
                                           start=(kc == 0), stop=(kc == KC - 1)),
                                  reads=[r_w1[sk], xn2[kc][1]],
                                  war_only=[gr[j]] if kc == 0 else (),
                                  writes=[gr[j]] if kc == KC - 1 else (),
                                  tag=f"g{th}.{e}.{hc}.{j}")
                    for kc in range(KC):
                        for j in range(2):
                            sl = slice(j * 512, (j + 1) * 512)
                            kb.op("tensor", lambda t, sk=sk, c=c, kc=kc, sl=sl,
                                  ups=ups: t.matmul(ups[:, sl],
                                           w3[sk][:, kc * 256 + c * 128:
                                                  kc * 256 + c * 128 + 128],
                                           xn2[kc][0][:, sl],
                                           start=(kc == 0), stop=(kc == KC - 1)),
                                  reads=[r_w3[sk], xn2[kc][1]],
                                  war_only=[ur[j]] if kc == 0 else (),
                                  writes=[ur[j]] if kc == KC - 1 else (),
                                  tag=f"u{th}.{e}.{hc}.{j}")
                    kb.op("scalar", lambda sc, hc=hc, ups=ups: sc.copy(
                        hch[hc][0], ups),
                        reads=ur, writes=[hch[hc][1]],
                        tag=f"he{th}.{e}.{hc}")
                    kb.op("gpsimd", lambda g, hc=hc: g.tensor_tensor(
                        hch[hc][0], hch[hc][0], wbc, ALU.mult),
                        reads=[hch[hc][1], r_wbc], writes=[hch[hc][1]],
                        tag=f"hw{th}.{e}.{hc}")
                    kb.op("scalar", lambda sc, hc=hc, gps=gps: sc.activation(
                        sg_sb, gps, AF.Silu),
                        reads=gr,
                        war_only=[r_wrow], writes=[r_sg],
                        tag=f"sg{th}.{e}.{hc}")
                    kb.op("vector", lambda v, hc=hc: v.tensor_tensor(
                        hch[hc][0], hch[hc][0], sg_sb, ALU.mult),
                        reads=[hch[hc][1], r_sg], writes=[hch[hc][1]],
                        tag=f"hs{th}.{e}.{hc}")
            for j in range(2):
                sl = slice(j * 512, (j + 1) * 512)
                for st_ in range(8):
                    sk = st_ % 2
                    kb.dma("gpsimd", lambda g, e=e, st_=st_, sk=sk: g.dma_start(
                        w2[sk], cx.inp["l0_moe_w2"][e, st_]),
                        writes=[r_w2[sk]], tag=f"e{th}.{e}.{j}.{st_}w2")
                    for c2 in range(2):
                        hc = st_ * 2 + c2
                        first = st_ == 0 and c2 == 0
                        last = st_ == 7 and c2 == 1
                        for od in range(8):
                            ps = cx.ps_a if od < 4 else cx.ps_b
                            psr = (cx.psa if od < 4 else cx.psb)(od % 4)
                            kb.op("tensor",
                                  lambda t, sk=sk, c2=c2, od=od, sl=sl, hc=hc,
                                  first=first, last=last, ps=ps: t.matmul(
                                      ps[:, (od % 4) * 512:(od % 4 + 1) * 512],
                                      w2[sk][:, c2 * 1024 + od * 128:
                                             c2 * 1024 + (od + 1) * 128],
                                      hch[hc][0][:, sl],
                                      start=first, stop=last),
                                  reads=[r_w2[sk], hch[hc][1]],
                                  war_only=[psr] if first else (),
                                  writes=[psr] if last else (),
                                  tag=f"d{th}.{e}.{j}.{hc}.{od}")
                for od in range(8):
                    ps = cx.ps_a if od < 4 else cx.ps_b
                    psr = (cx.psa if od < 4 else cx.psb)(od % 4)
                    kb.op("vector", lambda v, ps=ps, od=od, sl=sl: v.tensor_add(
                        uacc[od][0][:, sl], uacc[od][0][:, sl],
                        ps[:, (od % 4) * 512:(od % 4 + 1) * 512]),
                        reads=[psr, uacc[od][1]], writes=[uacc[od][1]],
                        tag=f"ue{th}.{e}.{j}.{od}")
        for i in range(KC):
            kb.op("vector",
                  lambda v, vv=vi[i][0], uu=uacc[i][0]: v.tensor_copy(vv, uu),
                  reads=[uacc[i][1]], writes=[vi[i][1]], tag=f"vb{th}.{i}")
        if _os5.environ.get("L0F_PROBS_DBG") and th == 0:
            kb.op("vector", lambda v: v.tensor_copy(
                cx.f32(OFF_XR + 4 * 8192, T), cx.bf(OFF_B, T)),
                reads=[kb.reg("vi0")], writes=[kb.reg("xr4"), kb.reg("xr5")],
                tag="dbg8")

    kb.phase_barrier("l0f_out")
    import os as _os2
    if _os2.environ.get("L0F_PROBS_DBG"):
        return
    if _os2.environ.get("L0F_WBC_DBG"):
        kb.op("vector", lambda v: v.tensor_copy(
            cx.f32(OFF_XR, 1024), cx.bf(OFF_SM + 6144, 1024)),
            reads=[kb.reg("wbc")], writes=[kb.reg("xr0")], tag="dbgwbc")
        kb.op("vector", lambda v: v.tensor_copy(
            cx.f32(OFF_XR + 8192, 1024), cx.bf(OFF_SM + 4096, 1024)),
            reads=[kb.reg("wt_sb")], writes=[kb.reg("xr1")], tag="dbgwt")
        return
    if _os2.environ.get("L0F_MOE_DBG"):
        for i in range(KC):
            kb.op("vector", lambda v, i=i: v.tensor_copy(
                cx.f32(OFF_XR + 8192 * i, T), cx.bf(OFF_B + 4096 * i, T)),
                reads=[kb.reg(f"dvi{i}")], writes=[kb.reg(f"xr{i}")], tag="dbgmoe")
        return
    vif = [(cx.bf(OFF_B + 4096 * i, T), kb.reg(f"vif{i}")) for i in range(KC)]
    ln_add_into_xr(cx, "l0fln_", vif, T)


def emit_l1a(cx):
    kb = cx.kb
    H = 16
    xr = [(cx.f32(OFF_XR + 8192 * i, T), kb.reg(f"xr{i}")) for i in range(KC)]
    xn = [(cx.bf(OFF_A + 4096 * i, T), kb.reg(f"xn{i}")) for i in range(KC)]
    ln_apply(cx, "g0_", xr, T, 1e-5, True, False, "into", dsts=xn)
    kb.phase_barrier("l1a_heads")
    xn = [(cx.bf(OFF_A + 4096 * i, T), kb.reg(f"xn{i}")) for i in range(KC)]

    qhat = cx.bf(OFF_B, 2048)
    khat = cx.bf(OFF_B + 4096, 2048)
    vbf = cx.bf(OFF_B + 8192, 2048)
    vt = cx.bf(OFF_B + 12288, 1024)
    tp = cx.bf(OFF_B + 16384, 2048)
    at = cx.bf(OFF_B + 20480, 1024)
    qvar = cx.bf(OFF_B + 22528, 2048, rows=1)
    kvar = cx.bf(OFF_B + 26624, 2048, rows=1)
    rowf = cx.f32(OFF_SM, 2048, rows=1)
    nmax = cx.f32(OFF_SM2, 1)
    ssum = cx.f32(OFF_SM2 + 4, 1)
    rsum = cx.f32(OFF_SM2 + 8, 1)
    r_qh, r_kh, r_vbf, r_vt, r_tp, r_at = (kb.reg("qh"), kb.reg("kh"),
                                           kb.reg("vbf"), kb.reg("vt"),
                                           kb.reg("tp"), kb.reg("at"))
    r_qv, r_kv = kb.reg("qvar"), kb.reg("kvar")
    r_rowf = kb.reg("rowf")
    r_nmax, r_ssum, r_rsum = kb.reg("nmax"), kb.reg("ssum"), kb.reg("rsum")
    r_ctx = [kb.reg(f"ctx{c}") for c in range(KC)]
    wsl = [cx.bf(OFF_W + 1024 * k, 512) for k in range(6)]
    r_wsl = [kb.reg(f"l1w{k}") for k in range(6)]
    psbf = cx.ps_b[:].bitcast(BF16)
    ident = cx.bf(OFF_IDENT, 128)
    ones = cx.bf(OFF_ONES128, 128)

    def proj(h, wslot, r_w, wname, dst, r_dst, dst_row0, scale, tag):
        kb.dma("gpsimd", lambda g, wslot=wslot, h=h, wname=wname: g.dma_start(
            wslot, cx.inp[wname][h]), writes=[r_w], tag=f"{tag}ld{h}")
        for kc in range(KC):
            for j in range(4):
                sl = slice(j * 512, (j + 1) * 512)
                kb.op("tensor", lambda t, wslot=wslot, kc=kc, sl=sl: t.matmul(
                    cx.ps_a[0:64, sl], wslot[:, kc * 64:(kc + 1) * 64],
                    xn[kc][0][:, sl], start=(kc == 0), stop=(kc == KC - 1)),
                    reads=[r_w, xn[kc][1]],
                    war_only=[cx.psa(j)] if kc == 0 else (),
                    writes=[cx.psa(j)] if kc == KC - 1 else (),
                    tag=f"{tag}mm{h}.{j}")
        kb.op("scalar", lambda sc, dst=dst, dst_row0=dst_row0, scale=scale:
              sc.activation(dst[dst_row0:dst_row0 + 64, :], cx.ps_a[0:64, 0:T],
                            AF.Copy, scale=scale),
              reads=[cx.psa(j) for j in range(4)], writes=[r_dst],
              tag=f"{tag}ev{h}")

    def stats(h, src, r_src, extra_dst, r_ed, varrow, r_vr, mscale, vscale, tag):
        kb.op("gpsimd", lambda g, src=src: g.tensor_tensor(
            tp[0:64, :], src[0:64, :], src[0:64, :], ALU.mult),
            reads=[r_src], war_only=[r_tp], writes=[r_tp], tag=f"{tag}sq{h}")
        for j in range(4):
            sl = slice(j * 512, (j + 1) * 512)
            kb.op("tensor", lambda t, src=src, sl=sl: t.matmul(
                cx.ps_b[0:1, sl], ones[0:64, 0:1], src[0:64, sl],
                start=True, stop=True),
                reads=[r_src, cr(cx, "c_ones128")],
                war_only=[cx.psb(j)], writes=[cx.psb(j)], tag=f"{tag}sm{h}.{j}")
            kb.op("tensor", lambda t, sl=sl: t.matmul(
                cx.ps_a[0:1, sl], ones[0:64, 0:1], tp[0:64, sl],
                start=True, stop=True),
                reads=[r_tp, cr(cx, "c_ones128")],
                war_only=[cx.psa(j)], writes=[cx.psa(j)], tag=f"{tag}qm{h}.{j}")
        kb.op("scalar", lambda sc, extra_dst=extra_dst, mscale=mscale:
              sc.activation(extra_dst[64:65, :], cx.ps_b[0:1, 0:T], AF.Copy,
                            scale=mscale),
              reads=[cx.psb(j) for j in range(4)], writes=[r_ed],
              tag=f"{tag}mr{h}")
        kb.op("scalar", lambda sc: sc.activation(rowf, cx.ps_b[0:1, 0:T],
                                                 AF.Copy),
              reads=[cx.psb(j) for j in range(4)], war_only=[r_rowf],
              writes=[r_rowf], tag=f"{tag}r1{h}")
        kb.op("vector", lambda v: v.tensor_mul(rowf, rowf, rowf),
              reads=[r_rowf], writes=[r_rowf], tag=f"{tag}r2{h}")
        kb.op("vector", lambda v, varrow=varrow: v.scalar_tensor_tensor(
            varrow[0:1, :], rowf, -1.0 / 64.0, cx.ps_a[0:1, 0:T],
            ALU.mult, ALU.add),
            reads=[r_rowf] + [cx.psa(j) for j in range(4)],
            war_only=[r_vr], writes=[r_vr], tag=f"{tag}r3{h}")
        kb.op("vector", lambda v, varrow=varrow, vscale=vscale:
              v.tensor_scalar_mul(varrow[0:1, :], varrow[0:1, :], vscale),
              reads=[r_vr], writes=[r_vr], tag=f"{tag}r4{h}")

    for h in range(H):
        proj(h, wsl[0], r_wsl[0], "l1_wq_slices", qhat, r_qh, 0, 1.0 / 64.0, "q")
        stats(h, qhat, r_qh, qhat, r_qh, qvar, r_qv, -1.0, 64.0 / 63.0, "qs")
        proj(h, wsl[1], r_wsl[1], "l1_wk_slices", khat, r_kh, 0, 1.0, "k")
        stats(h, khat, r_kh, khat, r_kh, kvar, r_kv, 1.0 / 64.0, 1.0 / 63.0, "ks")
        proj(h, wsl[2], r_wsl[2], "l1_wv_slices", vbf, r_vbf, 0, 1.0, "v")
        for g8 in range(2):
            for jj in range(8):
                j = g8 * 8 + jj
                kb.op("tensor", lambda t, j=j, jj=jj: t.transpose(
                    psbf[:, 3072 + jj * 64: 3072 + (jj + 1) * 64],
                    vbf[0:64, j * 128:(j + 1) * 128], ident[0:64, 0:64]),
                    reads=[r_vbf, cr(cx, "c_ident")],
                    war_only=[cx.psb(3)] if jj == 0 else (),
                    writes=[cx.psb(3)] if jj == 7 else (),
                    tag=f"vtp{h}.{j}")
            kb.op("vector", lambda v, g8=g8: v.tensor_copy(
                vt[:, g8 * 512:(g8 + 1) * 512], psbf[:, 3072:3584]),
                reads=[cx.psb(3)], war_only=[r_vt], writes=[r_vt],
                tag=f"vtc{h}.{g8}")
        for qt in range(16):
            for j in range(4):
                sl = slice(j * 512, (j + 1) * 512)
                kb.op("tensor", lambda t, qt=qt, sl=sl: t.matmul(
                    cx.ps_a[:, sl], qhat[0:65, qt * 128:(qt + 1) * 128],
                    khat[0:65, sl], start=True, stop=False,
                    skip_group_check=True),
                    reads=[r_qh, r_kh],
                    war_only=[cx.psa(j)], writes=[], tag=f"sc{h}.{qt}.{j}")
                kb.op("tensor", lambda t, qt=qt, sl=sl: t.matmul(
                    cx.ps_a[:, sl], qvar[0:1, qt * 128:(qt + 1) * 128],
                    kvar[0:1, sl], start=False, stop=True,
                    skip_group_check=True),
                    reads=[r_qv, r_kv],
                    writes=[cx.psa(j)], tag=f"sv{h}.{qt}.{j}")
            kb.op("vector", lambda v: v.reduce_max(
                nmax, cx.ps_a[:, 0:T], axis=AX.X, negate=True),
                reads=[cx.psa(j) for j in range(4)],
                writes=[r_nmax], tag=f"mx{h}.{qt}")
            kb.op("scalar", lambda sc: sc.activation(
                tp, cx.ps_a[:, 0:T], AF.Exp, bias=nmax, accum_out=ssum),
                reads=[r_nmax] + [cx.psa(j) for j in range(4)],
                war_only=[r_tp], writes=[r_tp, r_ssum], tag=f"exp{h}.{qt}")
            kb.op("vector", lambda v: v.reciprocal(rsum, ssum),
                  reads=[r_ssum], writes=[r_rsum], tag=f"rs{h}.{qt}")
            for g8 in range(2):
                jlo, jhi = g8 * 8, g8 * 8 + 8
                for j in range(jlo, jhi):
                    kb.op("tensor", lambda t, j=j, jlo=jlo: t.transpose(
                        psbf[:, 3072 + (j - jlo) * 128: 3072 + (j - jlo + 1) * 128],
                        tp[:, j * 128:(j + 1) * 128], ident),
                        reads=[r_tp, cr(cx, "c_ident")],
                        war_only=[cx.psb(3)] if j == jlo else (),
                        writes=[cx.psb(3)] if j == jhi - 1 else (),
                        tag=f"at{h}.{qt}.{j}")
                kb.op("vector", lambda v: v.tensor_copy(
                    at, psbf[:, 3072:4096]),
                    reads=[cx.psb(3)], war_only=[r_at], writes=[r_at],
                    tag=f"atc{h}.{qt}.{g8}")
                for j in range(jlo, jhi):
                    kb.op("tensor", lambda t, j=j, jlo=jlo: t.matmul(
                        cx.ps_b[:, 0:64], at[:, (j - jlo) * 128:(j - jlo + 1) * 128],
                        vt[:, j * 64:(j + 1) * 64],
                        start=(j == 0), stop=(j == 15), skip_group_check=True),
                        reads=[r_at, r_vt],
                        war_only=[cx.psb(0)] if j == 0 else (),
                        writes=[cx.psb(0)] if j == 15 else (),
                        tag=f"av{h}.{qt}.{j}")
            ctxb = cx.bf(OFF_SCR + 8704, 64)
            r_ctxb = kb.reg("ctxb")
            kb.op("vector", lambda v, ctxb=ctxb: v.tensor_scalar_mul(
                ctxb, cx.ps_b[:, 0:64], rsum),
                reads=[cx.psb(0), r_rsum], war_only=[r_ctxb], writes=[r_ctxb],
                tag=f"cx{h}.{qt}")
            kb.op("tensor", lambda t, ctxb=ctxb: t.transpose(
                psbf[:, 3072:3200][0:64, :], ctxb, ident),
                reads=[r_ctxb, cr(cx, "c_ident")], war_only=[cx.psb(3)],
                writes=[cx.psb(3)], tag=f"cxt{h}.{qt}")
            c = h // 2
            r0 = (h % 2) * 64
            kb.op("vector", lambda v, c=c, r0=r0, qt=qt: v.tensor_copy(
                cx.bf(OFF_U + 4096 * c, T, rows=64, row0=r0)[:, qt * 128:(qt + 1) * 128],
                psbf[0:64, 3072:3200]),
                reads=[cx.psb(3)], writes=[r_ctx[c]], tag=f"cxc{h}.{qt}")

    kb.phase_barrier("l1a_wo")
    r_ctx = [kb.reg(f"ctx{c}") for c in range(KC)]
    wo = cx.bf(OFF_W + 8192, 8192)
    r_wo = kb.reg("l1wo")
    kb.dma("gpsimd", lambda g: g.dma_start(wo, cx.inp["l1_wo"]),
           writes=[r_wo], tag="l1ld_wo")
    outp = [(cx.bf(OFF_B + 4096 * i, T), kb.reg(f"op{i}")) for i in range(KC)]
    for oc in range(KC):
        for kc in range(KC):
            for j in range(4):
                sl = slice(j * 512, (j + 1) * 512)
                kb.op("tensor", lambda t, oc=oc, kc=kc, sl=sl: t.matmul(
                    cx.ps_a[:, sl], wo[:, kc * 1024 + oc * 128: kc * 1024 + (oc + 1) * 128],
                    cx.bf(OFF_U + 4096 * kc, T)[:, sl],
                    start=(kc == 0), stop=(kc == KC - 1)),
                    reads=[r_wo, r_ctx[kc]],
                    war_only=[cx.psa(j)] if kc == 0 else (),
                    writes=[cx.psa(j)] if kc == KC - 1 else (),
                    tag=f"l1wo{oc}.{j}")
        for j in range(4):
            sl = slice(j * 512, (j + 1) * 512)
            kb.op("scalar", lambda sc, oc=oc, sl=sl: sc.copy(
                outp[oc][0][:, sl], cx.ps_a[:, sl]),
                reads=[cx.psa(sl.start // 512)], writes=[outp[oc][1]],
                tag=f"l1woev{oc}.{j}")
    kb.phase_barrier("l1a_ln")
    outp = [(cx.bf(OFF_B + 4096 * i, T), kb.reg(f"op{i}")) for i in range(KC)]
    ln_add_into_xr(cx, "l1aln_", outp, T)


def emit_l1f(cx):
    kb = cx.kb
    HN = T // 2

    for th in range(2):
        n0 = th * HN
        kb.phase_barrier(f"l1f{th}")
        xr = [(cx.f32(OFF_XR + 8192 * i, T)[:, n0:n0 + HN], kb.reg(f"xr{i}"))
              for i in range(KC)]

        def xn2v(c):
            if c < 6:
                return cx.bf(OFF_SCR + 2048 * c, HN)
            return cx.bf(OFF_SM + 2048 * (c - 6), HN)

        xn2 = [(xn2v(c), kb.reg(f"xn2_{c}")) for c in range(KC)]
        ln_apply(cx, f"f{th}l_", xr, HN, 1e-5, True, False, "into", dsts=xn2,
                 homes=(OFF_A, OFF_A + 2048, OFF_A + 4096, OFF_A + 8192))

        gw = cx.bf(OFF_A + 24576, 64)
        r_gw = kb.reg("gw")
        kb.dma("gpsimd", lambda g: g.dma_start(gw, cx.inp["l1_gatew"]),
               writes=[r_gw], tag=f"f{th}ldgw")
        for j in range(2):
            sl = slice(j * 512, (j + 1) * 512)
            for kc in range(KC):
                kb.op("tensor", lambda t, kc=kc, sl=sl: t.matmul(
                    cx.ps_a[0:8, sl], gw[:, kc * 8:(kc + 1) * 8],
                    xn2[kc][0][:, sl], start=(kc == 0), stop=(kc == KC - 1)),
                    reads=[r_gw, xn2[kc][1]],
                    war_only=[cx.psa(j)] if kc == 0 else (),
                    writes=[cx.psa(j)] if kc == KC - 1 else (),
                    tag=f"f{th}rt{j}")
        lg_bf = cx.bf(OFF_A + 16384, HN, rows=8)
        r_lg = kb.reg("lg")
        kb.op("vector", lambda v: v.tensor_copy(lg_bf, cx.ps_a[0:8, 0:HN]),
              reads=[cx.psa(0), cx.psa(1)], writes=[r_lg], tag=f"f{th}lgc")
        ltm = cx.bf(OFF_A + 18432, 64)
        r_ltm = kb.reg("ltm")
        psbf = cx.ps_b[:].bitcast(BF16)
        ident = cx.bf(OFF_IDENT, 128)
        for tc in range(8):
            kb.op("tensor", lambda t, tc=tc: t.transpose(
                psbf[:, 3072 + tc * 8: 3072 + (tc + 1) * 8][0:128, :],
                lg_bf[:, tc * 128:(tc + 1) * 128], ident[0:8, 0:8]),
                reads=[r_lg, cr(cx, "c_ident")],
                war_only=[cx.psb(3)] if tc == 0 else (),
                writes=[cx.psb(3)] if tc == 7 else (),
                tag=f"f{th}tp{tc}")
        kb.op("vector", lambda v: v.tensor_copy(ltm, psbf[:, 3072:3136]),
              reads=[cx.psb(3)], writes=[r_ltm], tag=f"f{th}ltc")
        pt = cx.f32(OFF_A + 20480, 64)
        wtm = cx.f32(OFF_A + 20736, 8)
        wtmb = cx.bf(OFF_A + 20800, 8)
        msked = cx.f32(OFF_A + 20864, 8)
        r_pt, r_wtm, r_msk2 = kb.reg("pt"), kb.reg("wtm"), kb.reg("mskd")
        nm = cx.f32(OFF_SM2, 1)
        sm = cx.f32(OFF_SM2 + 4, 1)
        rs = cx.f32(OFF_SM2 + 8, 1)
        m1 = cx.f32(OFF_SM2 + 12, 1)
        m2 = cx.f32(OFF_SM2 + 16, 1)
        den = cx.f32(OFF_SM2 + 20, 1)
        mka = cx.f32(OFF_SM2 + 24, 1)
        r_nm, r_sm, r_rs = kb.reg("nm"), kb.reg("sm"), kb.reg("rs")
        r_m1, r_m2, r_den, r_mka = (kb.reg("m1f"), kb.reg("m2f"),
                                    kb.reg("denf"), kb.reg("mkaf"))
        wt_sb = cx.bf(OFF_SM + 4096, 1024, rows=8)
        r_wtsb = kb.reg("wt_sb")
        for tc in range(8):
            lv = ltm[:, tc * 8:(tc + 1) * 8]
            kb.op("vector", lambda v, lv=lv: v.reduce_max(nm, lv, axis=AX.X,
                                                          negate=True),
                  reads=[r_ltm], writes=[r_nm], tag=f"f{th}w{tc}m")
            kb.op("scalar", lambda sc, lv=lv: sc.activation(
                pt[:, tc * 8:(tc + 1) * 8] if False else pt[:, 0:8], lv, AF.Exp,
                bias=nm, accum_out=sm),
                reads=[r_nm, r_ltm], war_only=[r_pt], writes=[r_pt, r_sm],
                tag=f"f{th}w{tc}e")
            kb.op("vector", lambda v: v.reciprocal(rs, sm),
                  reads=[r_sm], writes=[r_rs], tag=f"f{th}w{tc}r")
            kb.op("vector", lambda v: v.tensor_scalar_mul(pt[:, 0:8],
                                                          pt[:, 0:8], rs),
                  reads=[r_pt, r_rs], writes=[r_pt], tag=f"f{th}w{tc}p")
            kb.op("vector", lambda v: v.reduce_max(m1, pt[:, 0:8], axis=AX.X),
                  reads=[r_pt], writes=[r_m1], tag=f"f{th}w{tc}m1")
            kb.op("vector", lambda v: v.tensor_scalar(msked, pt[:, 0:8], m1,
                                                      None, ALU.is_ge),
                  reads=[r_pt, r_m1], writes=[r_msk2], tag=f"f{th}w{tc}k")
            kb.op("vector", lambda v: v.scalar_tensor_tensor(
                msked, msked, -1e9, pt[:, 0:8], ALU.mult, ALU.add),
                reads=[r_msk2, r_pt], writes=[r_msk2], tag=f"f{th}w{tc}s")
            kb.op("vector", lambda v: v.reduce_max(m2, msked, axis=AX.X),
                  reads=[r_msk2], writes=[r_m2], tag=f"f{th}w{tc}m2")
            kb.op("vector", lambda v: v.tensor_add(den, m1, m2),
                  reads=[r_m1, r_m2], writes=[r_den], tag=f"f{th}w{tc}d")
            kb.op("vector", lambda v: v.reciprocal(den, den),
                  reads=[r_den], writes=[r_den], tag=f"f{th}w{tc}dr")
            for e in range(8):
                kb.op("vector", lambda v, e=e: v.tensor_scalar(
                    mka, pt[:, e:e + 1], m2, None, ALU.is_ge),
                    reads=[r_pt, r_m2], writes=[r_mka], tag=f"f{th}w{tc}k{e}")
                kb.op("vector", lambda v, e=e: v.tensor_mul(
                    wtm[:, e:e + 1], mka, pt[:, e:e + 1]),
                    reads=[r_mka, r_pt], writes=[r_wtm], tag=f"f{th}w{tc}x{e}")
            kb.op("vector", lambda v: v.tensor_scalar_mul(wtm, wtm, den),
                  reads=[r_wtm, r_den], writes=[r_wtm], tag=f"f{th}w{tc}n")
            kb.op("vector", lambda v: v.tensor_copy(wtmb, wtm),
                  reads=[r_wtm], writes=[r_wtm], tag=f"f{th}w{tc}c")
            kb.op("tensor", lambda t: t.transpose(
                psbf[0:8, 3072:3200], wtmb, ident),
                reads=[r_wtm, cr(cx, "c_ident")], war_only=[cx.psb(3)],
                writes=[cx.psb(3)], tag=f"f{th}w{tc}t")
            kb.op("vector", lambda v, tc=tc: v.tensor_copy(
                wt_sb[:, tc * 128:(tc + 1) * 128], psbf[0:8, 3072:3200]),
                reads=[cx.psb(3)], writes=[r_wtsb], tag=f"f{th}w{tc}o")

        kb.phase_barrier(f"l1f{th}x")
        xn2 = [(xn2v(c), kb.reg(f"xn2_{c}")) for c in range(KC)]
        wt_sb = cx.bf(OFF_SM + 4096, 1024, rows=8)
        r_wtsb = kb.reg("wt_sb")
        uacc = [(cx.f32(OFF_U + 4096 * i, HN), kb.reg(f"uacc{i}"))
                for i in range(KC)]

        gu = [cx.bf(OFF_W + 8192 * k, 4096) for k in range(2)]
        dn = [cx.bf(OFF_W + 16384 + 4096 * k, 2048) for k in range(2)]
        r_gu = [kb.reg(f"gu{k}") for k in range(2)]
        r_dn = [kb.reg(f"dn{k}") for k in range(2)]
        wrow = cx.bf(OFF_W + 24576, 1024, rows=1)
        r_wrow = kb.reg("wrow")
        sg_sb = cx.bf(OFF_W + 24576, 1024)
        r_sg = kb.reg("sg_sb")
        wbc = cx.bf(OFF_SM + 6144, 1024)
        r_wbc = kb.reg("wbc")
        hch = [(cx.bf(OFF_A + 2048 * c, HN), kb.reg(f"h{c}")) for c in range(16)]

        for e in range(8):
            kb.dma("gpsimd", lambda g, e=e: g.dma_start(wrow, wt_sb[e:e + 1, :]),
                   reads=[r_wtsb], war_only=[r_sg], writes=[r_wrow],
                   tag=f"f{th}.{e}row")
            for j in range(2):
                sl = slice(j * 512, (j + 1) * 512)
                kb.op("tensor", lambda t, sl=sl: t.matmul(
                    cx.ps_b[:, sl], cx.bf(OFF_ONES128, 128)[0:1, :],
                    wrow[:, sl], start=True, stop=True),
                    reads=[r_wrow, cr(cx, "c_ones128")],
                    war_only=[cx.psb(j)], writes=[cx.psb(j)],
                    tag=f"f{th}.{e}bc{j}")
            kb.op("vector", lambda v: v.tensor_copy(wbc, cx.ps_b[:, 0:1024]),
                  reads=[cx.psb(0), cx.psb(1)], war_only=[r_wbc],
                  writes=[r_wbc], tag=f"f{th}.{e}bcc")
            for st_ in range(8):
                sk = st_ % 2
                kb.dma("gpsimd", lambda g, e=e, st_=st_, sk=sk: g.dma_start(
                    gu[sk], cx.inp["l1_gu"][e, st_]),
                    writes=[r_gu[sk]], tag=f"f{th}.{e}.{st_}gu")
                for c in range(2):
                    hc = st_ * 2 + c
                    pr = hc % 2
                    gps = cx.ps_a[:, pr * 1024:(pr + 1) * 1024]
                    ups = cx.ps_b[:, pr * 1024:(pr + 1) * 1024]
                    gr = [cx.psa(pr * 2), cx.psa(pr * 2 + 1)]
                    ur = [cx.psb(pr * 2), cx.psb(pr * 2 + 1)]
                    for kc in range(KC):
                        for j in range(2):
                            sl = slice(j * 512, (j + 1) * 512)
                            kb.op("tensor", lambda t, sk=sk, c=c, kc=kc, sl=sl,
                                  gps=gps: t.matmul(gps[:, sl],
                                           gu[sk][:, kc * 512 + c * 128:
                                                  kc * 512 + c * 128 + 128],
                                           xn2[kc][0][:, sl],
                                           start=(kc == 0), stop=(kc == KC - 1)),
                                  reads=[r_gu[sk], xn2[kc][1]],
                                  war_only=[gr[j]] if kc == 0 else (),
                                  writes=[gr[j]] if kc == KC - 1 else (),
                                  tag=f"f{th}g{e}.{hc}.{j}")
                    for kc in range(KC):
                        for j in range(2):
                            sl = slice(j * 512, (j + 1) * 512)
                            kb.op("tensor", lambda t, sk=sk, c=c, kc=kc, sl=sl,
                                  ups=ups: t.matmul(ups[:, sl],
                                           gu[sk][:, kc * 512 + 256 + c * 128:
                                                  kc * 512 + 256 + c * 128 + 128],
                                           xn2[kc][0][:, sl],
                                           start=(kc == 0), stop=(kc == KC - 1)),
                                  reads=[r_gu[sk], xn2[kc][1]],
                                  war_only=[ur[j]] if kc == 0 else (),
                                  writes=[ur[j]] if kc == KC - 1 else (),
                                  tag=f"f{th}u{e}.{hc}.{j}")
                    kb.op("scalar", lambda sc, hc=hc, ups=ups: sc.copy(
                        hch[hc][0], ups),
                        reads=ur, writes=[hch[hc][1]],
                        tag=f"f{th}he{e}.{hc}")
                    kb.op("gpsimd", lambda g, hc=hc: g.tensor_tensor(
                        hch[hc][0], hch[hc][0], wbc, ALU.mult),
                        reads=[hch[hc][1], r_wbc], writes=[hch[hc][1]],
                        tag=f"f{th}hw{e}.{hc}")
                    kb.op("scalar", lambda sc, gps=gps: sc.activation(
                        sg_sb, gps, AF.Silu),
                        reads=gr,
                        war_only=[r_wrow], writes=[r_sg],
                        tag=f"f{th}sg{e}.{hc}")
                    kb.op("vector", lambda v, hc=hc: v.tensor_tensor(
                        hch[hc][0], hch[hc][0], sg_sb, ALU.mult),
                        reads=[hch[hc][1], r_sg], writes=[hch[hc][1]],
                        tag=f"f{th}hs{e}.{hc}")
            for j in range(2):
                sl = slice(j * 512, (j + 1) * 512)
                for st_ in range(8):
                    sk = st_ % 2
                    kb.dma("gpsimd", lambda g, e=e, st_=st_, sk=sk: g.dma_start(
                        dn[sk], cx.inp["l1_dn"][e, st_]),
                        writes=[r_dn[sk]], tag=f"f{th}.{e}.{j}.{st_}dn")
                    for c2 in range(2):
                        hc = st_ * 2 + c2
                        first = st_ == 0 and c2 == 0
                        last = st_ == 7 and c2 == 1
                        for od in range(8):
                            ps = cx.ps_a if od < 4 else cx.ps_b
                            psr = (cx.psa if od < 4 else cx.psb)(od % 4)
                            kb.op("tensor",
                                  lambda t, sk=sk, c2=c2, od=od, sl=sl, hc=hc,
                                  first=first, last=last, ps=ps: t.matmul(
                                      ps[:, (od % 4) * 512:(od % 4 + 1) * 512],
                                      dn[sk][:, c2 * 1024 + od * 128:
                                             c2 * 1024 + (od + 1) * 128],
                                      hch[hc][0][:, sl],
                                      start=first, stop=last),
                                  reads=[r_dn[sk], hch[hc][1]],
                                  war_only=[psr] if first else (),
                                  writes=[psr] if last else (),
                                  tag=f"f{th}d{e}.{j}.{hc}.{od}")
                for od in range(8):
                    ps = cx.ps_a if od < 4 else cx.ps_b
                    psr = (cx.psa if od < 4 else cx.psb)(od % 4)
                    if e == 0:
                        kb.op("vector",
                              lambda v, ps=ps, od=od, sl=sl,
                              uu=uacc[od][0]: v.tensor_copy(
                                  uu[:, sl], ps[:, (od % 4) * 512:(od % 4 + 1) * 512]),
                              reads=[psr], war_only=[uacc[od][1]],
                              writes=[uacc[od][1]], tag=f"f{th}ue{e}.{j}.{od}")
                    else:
                        kb.op("vector",
                              lambda v, ps=ps, od=od, sl=sl,
                              uu=uacc[od][0]: v.tensor_add(
                                  uu[:, sl], uu[:, sl],
                                  ps[:, (od % 4) * 512:(od % 4 + 1) * 512]),
                              reads=[psr, uacc[od][1]], writes=[uacc[od][1]],
                              tag=f"f{th}ue{e}.{j}.{od}")
        kb.phase_barrier(f"l1f{th}ln")
        uacc = [(cx.f32(OFF_U + 4096 * i, HN), kb.reg(f"uacc{i}"))
                for i in range(KC)]
        ln_apply(cx, f"f{th}o_", uacc, HN, 1e-5, True, False, "addxr", n0=n0)


def build_program(phases):
    nc = bass.Bass("TRN2", target_bir_lowering=False)
    needed = ["xin"] + list(CONST_DECLS.keys())
    decls = [("xin", [NB, P, KC * T], F32)]
    decls += [(k, list(s), d) for k, (s, d) in CONST_DECLS.items()]
    if "l0a" in phases:
        decls.append(("wfT_strips", [16, P, KC * 128], F32))
        needed.append("wfT_strips")
    if "l0f" in phases:
        extra = [("cw_ln1w", [P, 8], F32), ("l0_wq_slices", [16, P, 512], F32),
                 ("l0_wkv", [P, 2048], F32), ("l0_kup", [P, 2048], F32),
                 ("l0_vup", [P, 2048], F32), ("l0_wo", [P, 8192], F32),
                 ("c_ropeq", [P, T], BF16), ("c_ropek", [P, T], BF16),
                 ("c_causal", [P, P], F32),
                 ("cw_ln2w", [P, 8], F32), ("cw_rbias", [8, 1], F32),
                 ("l0_wr", [P, 64], F32),
                 ("l0_moe_w1", [9, 8, P, 2048], F32),
                 ("l0_moe_w3", [9, 8, P, 2048], F32),
                 ("l0_moe_w2", [9, 8, P, 2048], F32)]
        decls += extra
        needed += [n for n, _, _ in extra]
    if "l1a" in phases:
        extra = [("l1_wq_slices", [16, P, 512], F32),
                 ("l1_wk_slices", [16, P, 512], F32),
                 ("l1_wv_slices", [16, P, 512], F32),
                 ("l1_wo", [P, 8192], F32)]
        decls += extra
        needed += [n for n, _, _ in extra]
    if "l1f" in phases:
        extra = [("l1_gatew", [P, 64], F32),
                 ("l1_gu", [8, 8, P, 4096], F32),
                 ("l1_dn", [8, 8, P, 2048], F32)]
        decls += extra
        needed += [n for n, _, _ in extra]

    inp = {}
    for name, shape, dt in decls:
        inp[name] = nc.dram_tensor(name, list(shape), dt, kind="ExternalInput").ap()
    xout = nc.dram_tensor("xout", [NB, P, KC * T], F32, kind="ExternalOutput").ap()

    with ExitStack() as st:
        sb = st.enter_context(nc.sbuf_tensor("SB", [P, SB_COLS], BF16))
        ps_a = st.enter_context(nc.psum_tensor("PSA", [P, 2048], F32))
        ps_b = st.enter_context(nc.psum_tensor("PSB", [P, 2048], F32))
        kb = KB(nc)
        cx = Ctx(nc, kb, inp, sb, ps_a, ps_b)

        load_consts(cx, list(CONST_DECLS.keys()))
        for b in range(NB):
            kb.dma("sync",
                   lambda s, b=b: s.dma_start(cx.f32(OFF_XR, KC * T), inp["xin"][b]),
                   writes=[kb.reg(f"xr{i}") for i in range(KC)], tag=f"ldx{b}")
            kb.phase_barrier(f"b{b}start")
            for ph in phases:
                EMITTERS[ph](cx)
                kb.phase_barrier(f"b{b}{ph}end")
            kb.dma("sync",
                   lambda s, b=b: s.dma_start(xout[b], cx.f32(OFF_XR, KC * T)),
                   reads=[kb.reg(f"xr{i}") for i in range(KC)], tag=f"stx{b}")
            kb.phase_barrier(f"b{b}end")
        stats = kb.emit()
    return nc, needed, stats


def emit_l0f(cx):
    emit_l0f_attn(cx)
    import os
    if os.environ.get("L0F_DBG"):
        return
    cx.kb.phase_barrier("l0f_mid")
    if os.environ.get("L0F_VI_MID"):
        for i in range(KC):
            cx.kb.op("vector", lambda v, i=i: v.tensor_copy(
                cx.f32(OFF_XR + 8192 * i, T), cx.bf(OFF_B + 4096 * i, T)),
                reads=[cx.kb.reg(f"dmid{i}")], writes=[cx.kb.reg(f"xr{i}")],
                tag="dbgmid")
        return
    emit_l0f_moe(cx)


EMITTERS = {"l0a": emit_l0a, "l0f": emit_l0f, "l1a": emit_l1a,
            "l1f": emit_l1f}

LAUNCHES = [["l0a", "l0f", "l1a", "l1f"]]


def prep_weights(inputs):
    g = lambda k: np.asarray(inputs[k], np.float32)
    w = {}
    wft = np.ascontiguousarray(g("L0_attn_W_f").T)
    w["wfT_strips"] = np.stack([tiles(wft[:, s * 128:(s + 1) * 128])
                                for s in range(16)])
    w["cw_ln1w"] = np.ascontiguousarray(g("L0_ffn_ln1_w").reshape(KC, P).T)
    wq = g("L0_ffn_Wq")
    w["l0_wq_slices"] = np.stack([tiles(wq[:, h * 64:(h + 1) * 64])
                                  for h in range(16)])
    w["l0_wkv"] = tiles(g("L0_ffn_Wkv"))
    w["l0_kup"] = np.concatenate([tiles(g("L0_ffn_Wk_up")[h]) for h in range(16)],
                                 axis=1)
    w["l0_vup"] = np.concatenate([tiles(g("L0_ffn_Wv_up")[h]) for h in range(16)],
                                 axis=1)
    w["l0_wo"] = tiles(g("L0_ffn_Wo"))
    w["cw_ln2w"] = np.ascontiguousarray(g("L0_ffn_ln2_w").reshape(KC, P).T)
    w["cw_rbias"] = np.ascontiguousarray(g("L0_ffn_r_bias").reshape(8, 1))
    w["l0_wr"] = tiles(g("L0_ffn_Wr"))
    w1 = np.concatenate([g("L0_ffn_ew1"), g("L0_ffn_sw1")[None]], axis=0)
    w3 = np.concatenate([g("L0_ffn_ew3"), g("L0_ffn_sw3")[None]], axis=0)
    w2 = np.concatenate([g("L0_ffn_ew2"), g("L0_ffn_sw2")[None]], axis=0)
    w["l0_moe_w1"] = np.stack([np.stack([tiles(w1[e][:, s2 * 256:(s2 + 1) * 256])
                                         for s2 in range(8)]) for e in range(9)])
    w["l0_moe_w3"] = np.stack([np.stack([tiles(w3[e][:, s2 * 256:(s2 + 1) * 256])
                                         for s2 in range(8)]) for e in range(9)])
    w["l0_moe_w2"] = np.stack([np.stack([tiles(w2[e][s2 * 256:(s2 + 1) * 256, :])
                                         for s2 in range(8)]) for e in range(9)])
    for nm, key in [("l1_wq_slices", "L1_attn_Wq"), ("l1_wk_slices", "L1_attn_Wk"),
                    ("l1_wv_slices", "L1_attn_Wv")]:
        wt = np.ascontiguousarray(g(key).T)
        w[nm] = np.stack([tiles(wt[:, h * 64:(h + 1) * 64]) for h in range(16)])
    w["l1_wo"] = tiles(np.ascontiguousarray(g("L1_attn_Wo").T))
    w["l1_gatew"] = tiles(g("L1_ffn_gate_w"))
    gup = g("L1_ffn_gate_up_w")
    dwn = g("L1_ffn_down_w")
    w["l1_gu"] = np.stack([np.stack([
        tiles(np.concatenate([gup[e][:, s2 * 256:(s2 + 1) * 256],
                              gup[e][:, 2048 + s2 * 256: 2048 + (s2 + 1) * 256]],
                             axis=1)) for s2 in range(8)]) for e in range(8)])
    w["l1_dn"] = np.stack([np.stack([tiles(dwn[e][s2 * 256:(s2 + 1) * 256, :])
                                     for s2 in range(8)]) for e in range(8)])
    return w


_PROG_CACHE = {}


def kernel(**inputs):
    cc = host_consts()
    w = prep_weights(inputs)

    x = np.asarray(inputs["L0_attn_x"], dtype=np.float32)
    cur = [np.stack([fm_batch(x[ci * NB + b]) for b in range(NB)])
           for ci in range(NCORES)]

    for phases in LAUNCHES:
        key = tuple(phases)
        if key not in _PROG_CACHE:
            _PROG_CACHE[key] = build_program(phases)
        nc, needed, stats = _PROG_CACHE[key]
        in_maps = []
        for ci in range(NCORES):
            m = {"xin": cur[ci]}
            for k in needed:
                if k in cc:
                    m[k] = cc[k]
                elif k in w:
                    m[k] = w[k]
            in_maps.append(m)
        res = run_bass_kernel_spmd(nc, in_maps, core_ids=list(range(NCORES)))
        cur = [np.asarray(res.results[ci]["xout"], dtype=np.float32)
               for ci in range(NCORES)]

    return np.concatenate(
        [np.stack([un_fm_batch(cur[ci][b]) for b in range(NB)]) for ci in range(NCORES)],
        axis=0)


# version 05
# speedup vs baseline: 1.0693x; 1.0693x over previous
import numpy as np
import ml_dtypes
import concourse.bass as bass
import concourse.mybir as mybir
from concourse.bass_utils import run_bass_kernel_spmd
from collections import defaultdict
from contextlib import ExitStack

F32 = mybir.dt.float32
BF16 = mybir.dt.bfloat16
U8 = mybir.dt.uint8
AF = mybir.ActivationFunctionType
ALU = mybir.AluOpType
AX = mybir.AxisListType
BFNP = ml_dtypes.bfloat16

P = 128
T = 2048
D = 1024
KC = D // P
NB = 2
NCORES = 8

ENGINES = ("tensor", "scalar", "vector", "gpsimd", "sync")
DMA_RING = 12


class Region:
    __slots__ = ("name", "last_w", "readers", "exclusive")

    def __init__(self, name, exclusive=False):
        self.name = name
        self.last_w = None
        self.readers = []
        self.exclusive = exclusive


class _Op:
    __slots__ = ("idx", "eng", "fn", "deps", "is_dma", "inc", "waits", "tag")

    def __init__(self, idx, eng, fn, tag):
        self.idx = idx
        self.eng = eng
        self.fn = fn
        self.is_dma = False
        self.deps = set()
        self.inc = False
        self.waits = []
        self.tag = tag


class KB:
    def __init__(self, nc):
        self.nc = nc
        self.ops = []
        self.regions = {}
        self._eng_last = {}

    def reg(self, name, exclusive=False):
        r = self.regions.get(name)
        if r is None:
            r = self.regions[name] = Region(name, exclusive)
        return r

    def op(self, eng, fn, reads=(), writes=(), war_only=(), tag=""):
        idx = len(self.ops)
        o = _Op(idx, eng, fn, tag)
        self.ops.append(o)
        self._eng_last[eng] = idx
        for r in reads:
            if r.last_w is not None:
                o.deps.add(r.last_w)
            if r.exclusive:
                for ri in r.readers:
                    if self.ops[ri].eng != eng:
                        o.deps.add(ri)
            r.readers.append(idx)
        for r in war_only:
            if r.last_w is not None:
                o.deps.add(r.last_w)
            o.deps.update(r.readers)
        for r in writes:
            if r.last_w is not None:
                o.deps.add(r.last_w)
            o.deps.update(r.readers)
            r.last_w = idx
            r.readers = []
        o.deps.discard(idx)
        return o

    def dma(self, queue, fn, reads=(), writes=(), war_only=(), tag=""):
        o = self.op(queue, fn, reads=reads, writes=writes, war_only=war_only,
                    tag=tag)
        o.is_dma = True
        if not hasattr(self, "_dma_idx"):
            self._dma_idx = {}
        self._dma_idx.setdefault(queue, []).append(o.idx)
        return o

    def phase_barrier(self, tag="ph"):
        lasts = [self._eng_last[e] for e in ENGINES if e in self._eng_last]
        extra = []
        for q, idxs in getattr(self, "_dma_idx", {}).items():
            extra.extend(idxs[-DMA_RING:])
        for e in ENGINES:
            o = self.op(e, _nopfn(e), tag=tag)
            o.deps.update(i for i in lasts if i != o.idx)
            o.deps.update(i for i in extra if i != o.idx)
            self._eng_last[e] = o.idx
        self.regions = {}

    def emit(self):
        nc = self.nc
        ops = self.ops
        eng_ops = defaultdict(list)
        for o in ops:
            eng_ops[o.eng].append(o)

        for o in ops:
            for d in o.deps:
                if not ops[d].is_dma:
                    ops[d].inc = True

        comp_val = {}
        run_comp = defaultdict(int)
        dma_count = defaultdict(int)
        run_dma = {}
        for o in ops:
            if o.is_dma:
                slot = dma_count[o.eng, "n"] % DMA_RING
                dma_count[o.eng, "n"] += 1
                key = f"d{slot}_" + o.eng
                dma_count[o.eng, slot] += 16
                comp_val[o.idx] = (key, dma_count[o.eng, slot])
                run_dma[key] = dma_count[o.eng, slot]
                o.inc = True
            elif o.inc:
                run_comp[o.eng] += 1
                comp_val[o.idx] = ("c_" + o.eng, run_comp[o.eng])

        hw = defaultdict(int)
        for o in ops:
            need = {}
            for d in sorted(o.deps):
                sem, val = comp_val[d]
                if val > need.get(sem, 0):
                    need[sem] = val
            for sem, val in sorted(need.items()):
                if val > hw[(o.eng, sem)]:
                    o.waits.append((sem, val))
                    hw[(o.eng, sem)] = val

        sem_names = sorted({s for o in ops for s, _ in o.waits}
                           | {comp_val[o.idx][0] for o in ops if o.idx in comp_val})
        self._sem_stack = ExitStack()
        sems = {s: self._sem_stack.enter_context(nc.semaphore(s)) for s in sem_names}

        final = [(k, v) for k, v in run_dma.items()]
        final += [("c_" + q, v) for q, v in run_comp.items()]
        final = [(s, v) for s, v in final if s in sems]

        stats = dict(n_ops=len(ops), n_incs=sum(1 for o in ops if o.inc),
                     n_waits=sum(len(o.waits) for o in ops))

        with nc.Block() as block:
            def mk_body(ename):
                def body(e):
                    for o in eng_ops[ename]:
                        for sem, val in o.waits:
                            e.wait_ge(sems[sem], val)
                        inst = o.fn(e)
                        if o.inc:
                            sem, _ = comp_val[o.idx]
                            inst.then_inc(sems[sem], 16 if o.is_dma else 1)
                    if ename == "sync":
                        for sem, val in final:
                            e.wait_ge(sems[sem], val)
                return body

            order = [e for e in ENGINES if eng_ops.get(e)]
            if "sync" not in order:
                order.append("sync")
            for e in order:
                getattr(block, e)(mk_body(e))
        return stats


def _nopfn(eng):
    def f(e):
        return e.nop()
    return f


def tiles(w):
    k, o = w.shape
    assert k % P == 0
    return np.ascontiguousarray(
        w.reshape(k // P, P, o).transpose(1, 0, 2).reshape(P, (k // P) * o))


def fm_batch(xb):
    return np.ascontiguousarray(
        xb.T.reshape(KC, P, T).transpose(1, 0, 2).reshape(P, KC * T))


def un_fm_batch(arr):
    return np.ascontiguousarray(
        arr.reshape(P, KC, T).transpose(1, 0, 2).reshape(D, T).T)


def rope_tables():
    hd = 64
    inv = 1.0 / (100000.0 ** (np.arange(0, hd, 2, dtype=np.float32) / hd))
    f = np.arange(T, dtype=np.float32)[None, :] * inv[:, None]
    c = np.cos(f).astype(np.float32)
    s = np.sin(f).astype(np.float32)
    return np.concatenate([c, s, s, c], axis=0)


def host_consts():
    cc = {}
    cc["c_ident"] = np.eye(P, dtype=BFNP)
    cc["c_ones128"] = np.ones((P, P), BFNP)
    cs = rope_tables()
    cc["c_ropek"] = cs.astype(BFNP)
    cc["c_ropeq"] = (cs * 0.125).astype(BFNP)
    m = np.zeros((P, P), np.float32)
    m[np.triu_indices(P, 1)] = -1e30
    cc["c_causal"] = m
    oh = np.zeros((16, 16 * P), np.float32)
    for e in range(16):
        oh[e, e * P:(e + 1) * P] = 1.0
    cc["c_onehot"] = oh.astype(BFNP)
    return cc


CONST_DECLS = {
    "c_ident": ([P, P], BF16),
    "c_ones128": ([P, P], BF16),
}
PHASE_CONST_DECLS = {
    "c_causal": ([P, P], F32),
    "c_onehot": ([16, 16 * P], BF16),
    "c_ropeq": ([P, T], BF16),
    "c_ropek": ([P, T], BF16),
}


OFF_XR = 0
OFF_A = 65536
OFF_B = 98304
OFF_U = 131072
OFF_W = 163840
OFF_SCR = 190464
OFF_SM = 202752
OFF_SM2 = 210944
OFF_IDENT = 211456
OFF_ONES128 = 211712
OFF_END = 211968
SB_COLS = 106120
assert OFF_END <= 2 * SB_COLS


class Ctx:
    def __init__(self, nc, kb, inp, sb, ps_a, ps_b):
        self.nc = nc
        self.kb = kb
        self.inp = inp
        self.sb = sb
        self.ps_a = ps_a
        self.ps_b = ps_b

    def bf(self, byte_off, cols, rows=P, row0=0):
        assert byte_off % 2 == 0 and byte_off + 2 * cols <= 2 * SB_COLS
        v = self.sb[:][:, byte_off // 2: byte_off // 2 + cols]
        if rows != P or row0 != 0:
            v = v[row0:row0 + rows, :]
        return v

    def f32(self, byte_off, cols, rows=P, row0=0):
        assert byte_off % 4 == 0 and byte_off + 4 * cols <= 2 * SB_COLS
        v = self.sb[:].bitcast(F32)[:, byte_off // 4: byte_off // 4 + cols]
        if rows != P or row0 != 0:
            v = v[row0:row0 + rows, :]
        return v

    def u8(self, byte_off, cols, rows=P, row0=0):
        v = self.sb[:].bitcast(U8)[:, byte_off: byte_off + cols]
        if rows != P or row0 != 0:
            v = v[row0:row0 + rows, :]
        return v

    def psa(self, j):
        return self.kb.reg(f"psa{j}", exclusive=True)

    def psb(self, j):
        return self.kb.reg(f"psb{j}", exclusive=True)


def load_consts(cx, names):
    kb = cx.kb
    views = {
        "c_ident": cx.bf(OFF_IDENT, 128),
        "c_ones128": cx.bf(OFF_ONES128, 128),
    }
    for name in names:
        kb.dma("sync", lambda s, v=views[name], src=cx.inp[name]: s.dma_start(v, src),
               writes=[kb.reg("K" + name)], tag="ldc")


def cr(cx, name):
    return cx.kb.reg("K" + name)


def ln_apply(cx, ph, srcs, n, eps, want_mean, src_bf, mode, dsts=None, n0=0,
             lnw=None, homes=None):
    kb = cx.kb
    nsub = n // 512
    nch = len(srcs)
    nd = nch * P
    sq_off, xb_off, rbf_off, var_off = homes or (OFF_SCR, OFF_SCR + 4096,
                                                 OFF_SCR + 8192, OFF_SM)
    sq = cx.bf(sq_off, n)
    rsq = kb.reg(f"{ph}sq")
    xb = cx.bf(xb_off, n)
    rxb = kb.reg(f"{ph}xb")
    rbf = cx.bf(rbf_off, n)
    r_rbf = kb.reg(f"{ph}rbf")
    ones = cx.bf(OFF_ONES128, 128)
    rc = cr(cx, "c_ones128")
    ps_sum = cx.ps_b[:, 0:n]
    ps_sq = cx.ps_a[:, 0:n]
    r_pssum = [cx.psb(j) for j in range(nsub)]
    r_pssq = [cx.psa(j) for j in range(nsub)]

    for i, (sv, rv) in enumerate(srcs):
        kb.op("gpsimd", lambda g, sv=sv: g.tensor_tensor(sq, sv, sv, ALU.mult),
              reads=[rv], writes=[rsq], tag=f"{ph}sq{i}")
        if not src_bf and want_mean:
            kb.op("vector", lambda v, sv=sv: v.tensor_copy(xb, sv),
                  reads=[rv], writes=[rxb], tag=f"{ph}xb{i}")
        for j in range(nsub):
            sl = slice(j * 512, (j + 1) * 512)
            if want_mean:
                msrc = sv if src_bf else xb
                mreads = [rc, rv] if src_bf else [rc, rxb]
                kb.op("tensor",
                      lambda t, msrc=msrc, sl=sl, i=i: t.matmul(
                          ps_sum[:, sl], ones, msrc[:, sl],
                          start=(i == 0), stop=(i == nch - 1)),
                      reads=mreads,
                      war_only=[r_pssum[j]] if i == 0 else (),
                      writes=[r_pssum[j]] if i == nch - 1 else (),
                      tag=f"{ph}summ{i}.{j}")
            kb.op("tensor",
                  lambda t, sl=sl, i=i: t.matmul(
                      ps_sq[:, sl], ones, sq[:, sl],
                      start=(i == 0), stop=(i == nch - 1)),
                  reads=[rc, rsq],
                  war_only=[r_pssq[j]] if i == 0 else (),
                  writes=[r_pssq[j]] if i == nch - 1 else (),
                  tag=f"{ph}sqm{i}.{j}")

    var = cx.f32(var_off, n)
    r_var = kb.reg(f"{ph}var")
    negm = cx.f32(sq_off, n)
    r_negm = kb.reg(f"{ph}negm")
    mrps = cx.ps_b[:, 0:n]
    mrbf = sq
    r_mrbf = kb.reg(f"{ph}mrbf")

    kb.op("scalar", lambda sc: sc.activation(var, ps_sq, AF.Copy, scale=1.0 / nd),
          reads=r_pssq, writes=[r_var], tag=f"{ph}msq")
    if want_mean:
        kb.op("scalar", lambda sc: sc.activation(negm, ps_sum, AF.Copy,
                                                 scale=-1.0 / nd),
              reads=r_pssum, war_only=[rsq, rxb], writes=[r_negm],
              tag=f"{ph}negm")
        kb.op("vector", lambda v: v.tensor_mul(mrps, negm, negm),
              reads=[r_negm], war_only=r_pssum, writes=r_pssum, tag=f"{ph}m2")
        kb.op("vector", lambda v: v.tensor_sub(var, var, mrps),
              reads=r_pssum + [r_var], writes=[r_var], tag=f"{ph}var2")
    kb.op("vector", lambda v: v.tensor_scalar_add(var, var, eps),
          reads=[r_var], writes=[r_var], tag=f"{ph}eps")
    kb.op("scalar", lambda sc: sc.activation(var, var, AF.Sqrt),
          reads=[r_var], writes=[r_var], tag=f"{ph}sqrt")
    kb.op("vector", lambda v: v.reciprocal(var, var),
          reads=[r_var], writes=[r_var], tag=f"{ph}recip")
    if src_bf:
        kb.op("vector", lambda v: v.tensor_copy(rbf, var),
              reads=[r_var], writes=[r_rbf], tag=f"{ph}rcast")
    if want_mean:
        kb.op("vector", lambda v: v.tensor_mul(mrps, negm, var),
              reads=[r_negm, r_var], war_only=r_pssum, writes=r_pssum,
              tag=f"{ph}mr")
        if src_bf:
            kb.op("vector", lambda v: v.tensor_copy(mrbf, mrps),
                  reads=r_pssum, war_only=[rsq, r_negm], writes=[r_mrbf],
                  tag=f"{ph}mrcast")

    tmpf = negm
    r_tmpf = kb.reg(f"{ph}tmpf")
    tmpb = xb
    r_tmpb = kb.reg(f"{ph}tmpb")
    for i, (sv, rv) in enumerate(srcs):
        if mode == "into":
            dv, rd = dsts[i]
        else:
            dv = cx.f32(OFF_XR + 8192 * i, T)[:, n0:n0 + n]
            rd = kb.reg(f"xr{i}")
        for j in range(nsub):
            sl = slice(j * 512, (j + 1) * 512)
            if not src_bf:
                kb.op("vector",
                      lambda v, sv=sv, sl=sl: v.tensor_tensor(
                          tmpf[:, sl], sv[:, sl], var[:, sl], ALU.mult),
                      reads=[rv, r_var], war_only=[r_tmpb, r_mrbf, r_negm,
                                                   rsq, rxb],
                      writes=[r_tmpf], tag=f"{ph}a1.{i}.{j}")
                if mode == "into":
                    last = dv if lnw is None else tmpf
                    lastr = rd if lnw is None else r_tmpf
                    if want_mean:
                        kb.op("vector",
                              lambda v, last=last, sl=sl: v.tensor_tensor(
                                  last[:, sl], tmpf[:, sl], mrps[:, sl], ALU.add),
                              reads=[r_tmpf] + r_pssum, writes=[lastr],
                              tag=f"{ph}a2.{i}.{j}")
                    if lnw is not None:
                        wv, rw, colf = lnw
                        kb.op("vector",
                              lambda v, dv=dv, sl=sl, i=i: v.tensor_scalar_mul(
                                  dv[:, sl], tmpf[:, sl], wv[:, colf(i):colf(i) + 1]),
                              reads=[r_tmpf, rw], writes=[rd], tag=f"{ph}aw.{i}.{j}")
                else:
                    kb.op("vector",
                          lambda v, sl=sl: v.tensor_tensor(
                              ps_sq[:, sl], tmpf[:, sl], mrps[:, sl], ALU.add),
                          reads=[r_tmpf] + r_pssum, war_only=[r_pssq[j]],
                          writes=[r_pssq[j]], tag=f"{ph}a2.{i}.{j}")
                    kb.op("vector",
                          lambda v, dv=dv, sl=sl: v.tensor_tensor(
                              dv[:, sl], dv[:, sl], ps_sq[:, sl], ALU.add),
                          reads=[r_pssq[j], rd], writes=[rd], tag=f"{ph}a3.{i}.{j}")
            else:
                kb.op("gpsimd",
                      lambda g, sv=sv, sl=sl: g.tensor_tensor(
                          tmpb[:, sl], sv[:, sl], rbf[:, sl], ALU.mult),
                      reads=[rv, r_rbf], war_only=[r_tmpf, r_negm, rxb],
                      writes=[r_tmpb], tag=f"{ph}a1.{i}.{j}")
                if mode == "into":
                    last = dv if lnw is None else tmpb
                    lastr = rd if lnw is None else r_tmpb
                    if want_mean:
                        kb.op("gpsimd",
                              lambda g, last=last, sl=sl: g.tensor_tensor(
                                  last[:, sl], tmpb[:, sl], mrbf[:, sl], ALU.add),
                              reads=[r_tmpb, r_mrbf], writes=[lastr],
                              tag=f"{ph}a2.{i}.{j}")
                    if lnw is not None:
                        wv, rw, colf = lnw
                        kb.op("vector",
                              lambda v, dv=dv, sl=sl, i=i: v.tensor_scalar_mul(
                                  dv[:, sl], tmpb[:, sl], wv[:, colf(i):colf(i) + 1]),
                              reads=[r_tmpb, rw], writes=[rd], tag=f"{ph}aw.{i}.{j}")
                else:
                    kb.op("vector",
                          lambda v, sl=sl: v.tensor_tensor(
                              ps_sq[:, sl], tmpb[:, sl], mrbf[:, sl], ALU.add),
                          reads=[r_tmpb, r_mrbf], war_only=[r_pssq[j]],
                          writes=[r_pssq[j]], tag=f"{ph}a2.{i}.{j}")
                    kb.op("vector",
                          lambda v, dv=dv, sl=sl: v.tensor_tensor(
                              dv[:, sl], dv[:, sl], ps_sq[:, sl], ALU.add),
                          reads=[r_pssq[j], rd], writes=[rd], tag=f"{ph}a3.{i}.{j}")


def ln_into(cx, ph, srcs, dsts, n, eps=1e-5, src_bf=False, want_mean=True,
            lnw=None):
    ln_apply(cx, ph, srcs, n, eps, want_mean, src_bf, "into", dsts=dsts, lnw=lnw)


def ln_add_into_xr(cx, ph, srcs, n, n0=0, eps=1e-5):
    ln_apply(cx, ph, srcs, n, eps, True, True, "addxr", n0=n0)


def emit_l0a(cx):
    kb = cx.kb
    xr = [(cx.f32(OFF_XR + 8192 * i, T), kb.reg(f"xr{i}")) for i in range(KC)]
    xn = [(cx.bf(OFF_A + 4096 * i, T), kb.reg(f"l0a_xn{i}")) for i in range(KC)]

    ln_into(cx, "l0aln1_", xr, xn, T)
    kb.phase_barrier("l0a_kf")
    xr = [(cx.f32(OFF_XR + 8192 * i, T), kb.reg(f"xr{i}")) for i in range(KC)]
    xn = [(cx.bf(OFF_A + 4096 * i, T), kb.reg(f"l0a_xn{i}")) for i in range(KC)]

    import os as _os
    _nslot = int(_os.environ.get("L0A_SLOTS", 4))
    wslot = [cx.bf(OFF_W + 2048 * (s % _nslot), 1024) for s in range(4)]
    r_wslot = [kb.reg(f"l0a_w{s % _nslot}") for s in range(4)]
    ab = [cx.f32(OFF_U + 8192 * s, 2048) for s in range(4)]
    r_ab = [kb.reg(f"l0a_ab{s}") for s in range(4)]
    h = [(cx.bf(OFF_B + 4096 * i, T), kb.reg(f"l0a_h{i}")) for i in range(KC)]
    pz = cx.f32(OFF_W + 8192, 2048)
    msk = cx.u8(OFF_W + 16384, 2048)
    r_pz, r_msk = kb.reg("l0a_pz"), kb.reg("l0a_msk")
    wf = cx.inp["wfT_strips"]

    import os
    npairs = int(os.environ.get("L0A_PAIRS", KC))
    for i in range(npairs):
        sk = (i % 2) * 2
        kb.dma("gpsimd", lambda g, i=i, sk=sk: g.dma_start(wslot[sk], wf[i]),
               writes=[r_wslot[sk]], tag=f"l0a_ldwk{i}")
        kb.dma("gpsimd", lambda g, i=i, sk=sk: g.dma_start(wslot[sk + 1], wf[i + 8]),
               writes=[r_wslot[sk + 1]], tag=f"l0a_ldwh{i}")
        r_psk = [cx.psa(j) for j in range(4)]
        r_psh = [cx.psb(j) for j in range(4)]
        for kc in range(KC):
            for j in range(4):
                sl = slice(j * 512, (j + 1) * 512)
                kb.op("tensor",
                      lambda t, sk=sk, kc=kc, sl=sl: t.matmul(
                          cx.ps_a[:, sl], wslot[sk][:, kc * 128:(kc + 1) * 128],
                          xn[kc][0][:, sl], start=(kc == 0), stop=(kc == KC - 1)),
                      reads=[r_wslot[sk], xn[kc][1]],
                      war_only=[r_psk[j]] if kc == 0 else (),
                      writes=[r_psk[j]] if kc == KC - 1 else (),
                      tag=f"l0a_mmk{i}.{j}.{kc}")
        for kc in range(KC):
            for j in range(4):
                sl = slice(j * 512, (j + 1) * 512)
                kb.op("tensor",
                      lambda t, sk=sk, kc=kc, sl=sl: t.matmul(
                          cx.ps_b[:, sl], wslot[sk + 1][:, kc * 128:(kc + 1) * 128],
                          xn[kc][0][:, sl], start=(kc == 0), stop=(kc == KC - 1)),
                      reads=[r_wslot[sk + 1], xn[kc][1]],
                      war_only=[r_psh[j]] if kc == 0 else (),
                      writes=[r_psh[j]] if kc == KC - 1 else (),
                      tag=f"l0a_mmh{i}.{j}.{kc}")
        sa = (i % 2) * 2
        _el = int(os.environ.get("L0A_EL", 8)) if i > 0 else 8
        if _el == 0:
            continue
        kb.op("scalar", lambda sc, sa=sa: sc.activation(ab[sa], cx.ps_a[:, 0:T],
                                                        AF.Sigmoid, scale=-1.0),
              reads=r_psk, writes=[r_ab[sa]], tag=f"l0a_a{i}")
        if _el >= 2:
            kb.op("scalar", lambda sc, sa=sa: sc.activation(ab[sa + 1], cx.ps_b[:, 0:T],
                                                            AF.Sigmoid),
                  reads=r_psh, writes=[r_ab[sa + 1]], tag=f"l0a_sg{i}")
        if _el >= 3:
            kb.op("vector", lambda v: v.tensor_scalar_add(pz, cx.ps_b[:, 0:T], 0.5),
                  reads=r_psh, writes=[r_pz], tag=f"l0a_pos{i}")
        if _el >= 4:
            kb.op("vector", lambda v: v.tensor_scalar(msk, cx.ps_b[:, 0:T], 0.0,
                                                      None, ALU.is_ge),
                  reads=r_psh, writes=[r_msk], tag=f"l0a_msk{i}")
        if _el >= 5:
            kb.op("vector", lambda v, sa=sa: v.copy_predicated(ab[sa + 1], msk, pz),
                  reads=[r_msk, r_pz], writes=[r_ab[sa + 1]], tag=f"l0a_g{i}")
        if _el >= 6:
            kb.op("scalar", lambda sc: sc.activation(pz, cx.ps_a[:, 0:T], AF.Sigmoid),
                  reads=r_psk, writes=[r_pz], tag=f"l0a_z{i}")
        if _el >= 7:
            kb.op("vector", lambda v, sa=sa: v.tensor_mul(ab[sa + 1], ab[sa + 1], pz),
                  reads=[r_pz, r_ab[sa + 1]], writes=[r_ab[sa + 1]], tag=f"l0a_b{i}")
        if _el >= 8:
            kb.op("vector", lambda v, sa=sa, i=i: v.tensor_tensor_scan(
                h[i][0], ab[sa], ab[sa + 1], 0.5, ALU.mult, ALU.add),
                  reads=[r_ab[sa], r_ab[sa + 1]], writes=[h[i][1]], tag=f"l0a_scan{i}")

    import os
    if os.environ.get("L0A_DBG_H"):
        kb.phase_barrier("dbg")
        for i in range(KC):
            hv = cx.bf(OFF_B + 4096 * i, T)
            kb.op("vector", lambda v, hv=hv, i=i: v.tensor_copy(
                cx.f32(OFF_XR + 8192 * i, T), hv),
                reads=[kb.reg(f"dbg_h{i}")], writes=[kb.reg(f"xr{i}")], tag="dbgh")
        return
    if os.environ.get("L0A_CUT") == "1":
        return
    kb.phase_barrier("l0a_ln2")
    h = [(cx.bf(OFF_B + 4096 * i, T), kb.reg(f"l0a2_h{i}")) for i in range(KC)]
    ln_add_into_xr(cx, "l0aln2_", h, T)


def emit_l0f_attn(cx):
    kb = cx.kb
    H, HD = 16, 64
    xr = [(cx.f32(OFF_XR + 8192 * i, T), kb.reg(f"xr{i}")) for i in range(KC)]
    vi = [(cx.bf(OFF_B + 4096 * i, T), kb.reg(f"vi{i}")) for i in range(KC)]
    ln_apply(cx, "f0_", xr, T, 1e-5, True, False, "into", dsts=vi)
    kb.phase_barrier("l0f_f1")
    vi = [(cx.bf(OFF_B + 4096 * i, T), kb.reg(f"vi{i}")) for i in range(KC)]
    xn = [(cx.bf(OFF_A + 4096 * i, T), kb.reg(f"xn{i}")) for i in range(KC)]
    lnw_v = cx.f32(OFF_SM2 + 256, 8)
    r_lnw = kb.reg("ln1w")
    kb.dma("scalar", lambda s: s.dma_start(lnw_v, cx.inp["cw_ln1w"]),
           writes=[r_lnw], tag="ld_ln1w")
    ln_apply(cx, "f1_", vi, T, 1e-6, False, True, "into", dsts=xn,
             lnw=(lnw_v, r_lnw, lambda i: i))
    kb.phase_barrier("l0f_kv")
    xn = [(cx.bf(OFF_A + 4096 * i, T), kb.reg(f"xn{i}")) for i in range(KC)]

    wkv = cx.bf(OFF_W + 10240, 2048)
    r_wkv = kb.reg("wkv")
    kb.dma("gpsimd", lambda g: g.dma_start(wkv, cx.inp["l0_wkv"]),
           writes=[r_wkv], tag="ld_wkv")
    kv = cx.bf(OFF_SCR, 2 * T)
    r_kv = kb.reg("kv")
    for c in range(2):
        for kc in range(KC):
            for j in range(4):
                sl = slice(j * 512, (j + 1) * 512)
                kb.op("tensor", lambda t, c=c, kc=kc, sl=sl: t.matmul(
                    cx.ps_a[:, sl], wkv[:, kc * 256 + c * 128: kc * 256 + c * 128 + 128],
                    xn[kc][0][:, sl], start=(kc == 0), stop=(kc == KC - 1)),
                    reads=[r_wkv, xn[kc][1]],
                    war_only=[cx.psa(j)] if kc == 0 else (),
                    writes=[cx.psa(j)] if kc == KC - 1 else (),
                    tag=f"kvmm{c}.{j}")
        kb.op("scalar", lambda sc, c=c: sc.activation(
            kv[:, c * T:(c + 1) * T], cx.ps_a[:, 0:T], AF.Copy, scale=0.25),
            reads=[cx.psa(j) for j in range(4)], writes=[r_kv], tag=f"kvev{c}")

    kb.phase_barrier("l0f_heads")
    vi = [(cx.bf(OFF_B + 4096 * i, T), kb.reg(f"vi{i}")) for i in range(KC)]
    xn = [(cx.bf(OFF_A + 4096 * i, T), kb.reg(f"xn{i}")) for i in range(KC)]
    kv = cx.bf(OFF_SCR, 2 * T)
    r_kv = kb.reg("kv")

    ropeq = cx.bf(OFF_SM, T)
    ropek = cx.bf(OFF_SM + 4096, T)
    causal = cx.f32(OFF_SCR + 8192, 128)
    kup = cx.bf(OFF_W + 2048, 2048)
    vup = cx.bf(OFF_W + 6144, 2048)
    r_ropeq, r_ropek, r_causal = kb.reg("ropeq"), kb.reg("ropek"), kb.reg("causal")
    r_kup, r_vup = kb.reg("kup"), kb.reg("vup")
    kb.dma("scalar", lambda s: s.dma_start(ropeq, cx.inp["c_ropeq"]),
           writes=[r_ropeq], tag="ld_ropeq")
    kb.dma("scalar", lambda s: s.dma_start(ropek, cx.inp["c_ropek"]),
           writes=[r_ropek], tag="ld_ropek")
    kb.dma("scalar", lambda s: s.dma_start(causal, cx.inp["c_causal"]),
           writes=[r_causal], tag="ld_causal")
    kb.dma("gpsimd", lambda g: g.dma_start(kup, cx.inp["l0_kup"]),
           writes=[r_kup], tag="ld_kup")
    kb.dma("gpsimd", lambda g: g.dma_start(vup, cx.inp["l0_vup"]),
           writes=[r_vup], tag="ld_vup")

    qv = cx.bf(OFF_W + 10240, 2048)
    kh = cx.bf(OFF_W + 14336, 2048)
    tp = cx.bf(OFF_W + 18432, 2048)
    vt = cx.bf(OFF_W + 22528, 1024)
    at = cx.bf(OFF_W + 24576, 1024)
    ctxb = cx.bf(OFF_SCR + 8704, 64)
    tp2 = [tp, tp]
    _rtp = kb.reg("tp")
    r_tp2 = [_rtp, _rtp]
    nmax2 = [cx.f32(OFF_SM2, 1), cx.f32(OFF_SM2 + 12, 1)]
    ssum2 = [cx.f32(OFF_SM2 + 4, 1), cx.f32(OFF_SM2 + 16, 1)]
    rsum2 = [cx.f32(OFF_SM2 + 8, 1), cx.f32(OFF_SM2 + 20, 1)]
    r_nmax2 = [kb.reg("nmax0"), kb.reg("nmax1")]
    r_ssum2 = [kb.reg("ssum0"), kb.reg("ssum1")]
    r_rsum2 = [kb.reg("rsum0"), kb.reg("rsum1")]
    nmax = nmax2[0]
    ssum = ssum2[0]
    rsum = rsum2[0]
    wq = [cx.bf(OFF_W + 1024 * sl_, 512) for sl_ in range(2)]
    r_wq = [kb.reg(f"wq{s}") for s in range(2)]
    r_qv, r_kh, r_tp, r_vt, r_at = (kb.reg("qv"), kb.reg("kh"), kb.reg("tp"),
                                    kb.reg("vt"), kb.reg("ctb"))
    r_ctxb = kb.reg("ctxb")
    r_nmax, r_ssum, r_rsum = kb.reg("nmax"), kb.reg("ssum"), kb.reg("rsum")
    r_ctx = [kb.reg(f"ctx{c}") for c in range(KC)]
    psbf = cx.ps_b[:].bitcast(BF16)

    def rope_into(dst, dst_r, pse, pse_regs, table, table_r, tag):
        HT = T // 2
        for hf in range(2):
            hc = slice(hf * HT, (hf + 1) * HT)
            L = slice(0, HT)
            R = slice(HT, T)
            kb.op("vector", lambda v, hc=hc: v.tensor_tensor(
                tp[0:32, L], pse[0:32, 0:T][:, hc], table[0:32, hc], ALU.mult),
                reads=pse_regs + [table_r], war_only=[r_tp], writes=[r_tp],
                tag=f"{tag}pa{hf}")
            kb.op("vector", lambda v, hc=hc: v.tensor_tensor(
                tp[0:32, R], pse[32:64, 0:T][:, hc], table[32:64, hc], ALU.mult),
                reads=pse_regs + [table_r], writes=[r_tp], tag=f"{tag}pb{hf}")
            kb.op("vector", lambda v, hc=hc: v.tensor_sub(
                dst[0:32, hc], tp[0:32, L], tp[0:32, R]),
                reads=[r_tp], writes=[dst_r], tag=f"{tag}c1{hf}")
            kb.op("vector", lambda v, hc=hc: v.tensor_tensor(
                tp[32:64, L], pse[0:32, 0:T][:, hc], table[64:96, hc], ALU.mult),
                reads=pse_regs + [table_r], writes=[r_tp], tag=f"{tag}pc{hf}")
            kb.op("vector", lambda v, hc=hc: v.tensor_tensor(
                tp[32:64, R], pse[32:64, 0:T][:, hc], table[96:128, hc], ALU.mult),
                reads=pse_regs + [table_r], writes=[r_tp], tag=f"{tag}pd{hf}")
            kb.op("vector", lambda v, hc=hc: v.tensor_add(
                dst[32:64, hc], tp[32:64, L], tp[32:64, R]),
                reads=[r_tp], writes=[dst_r], tag=f"{tag}c2{hf}")

    for h in range(H):
        wslot = h % 2
        kb.dma("gpsimd", lambda g, h=h, wslot=wslot: g.dma_start(
            wq[wslot], cx.inp["l0_wq_slices"][h]),
            writes=[r_wq[wslot]], tag=f"ldwq{h}")
        for kc in range(KC):
            for j in range(4):
                sl = slice(j * 512, (j + 1) * 512)
                kb.op("tensor", lambda t, kc=kc, sl=sl, wslot=wslot: t.matmul(
                    cx.ps_a[0:64, sl], wq[wslot][:, kc * 64:(kc + 1) * 64],
                    xn[kc][0][:, sl], start=(kc == 0), stop=(kc == KC - 1)),
                    reads=[r_wq[wslot], xn[kc][1]],
                    war_only=[cx.psa(j)] if kc == 0 else (),
                    writes=[cx.psa(j)] if kc == KC - 1 else (),
                    tag=f"qmm{h}.{j}")
        rope_into(qv, r_qv, cx.ps_a, [cx.psa(j) for j in range(4)], ropeq,
                  r_ropeq, f"q{h}")
        for j in range(4):
            sl = slice(j * 512, (j + 1) * 512)
            for c in range(2):
                kb.op("tensor", lambda t, c=c, sl=sl, h=h: t.matmul(
                    cx.ps_b[0:64, sl], vup[:, h * 128 + c * 64: h * 128 + c * 64 + 64],
                    kv[:, c * T:(c + 1) * T][:, sl], start=(c == 0), stop=(c == 1)),
                    reads=[r_vup, r_kv],
                    war_only=[cx.psb(j)] if c == 0 else (),
                    writes=[cx.psb(j)] if c == 1 else (),
                    tag=f"vmm{h}.{j}")
        kb.op("scalar", lambda sc: sc.copy(qv[64:128, :], cx.ps_b[0:64, 0:T]),
              reads=[cx.psb(j) for j in range(4)], writes=[r_qv], tag=f"vev{h}")
        for j in range(4):
            sl = slice(j * 512, (j + 1) * 512)
            for c in range(2):
                kb.op("tensor", lambda t, c=c, sl=sl, h=h: t.matmul(
                    cx.ps_b[0:64, sl], kup[:, h * 128 + c * 64: h * 128 + c * 64 + 64],
                    kv[:, c * T:(c + 1) * T][:, sl], start=(c == 0), stop=(c == 1)),
                    reads=[r_kup, r_kv],
                    war_only=[cx.psb(j)] if c == 0 else (),
                    writes=[cx.psb(j)] if c == 1 else (),
                    tag=f"kmm{h}.{j}")
        rope_into(kh, r_kh, cx.ps_b, [cx.psb(j) for j in range(4)], ropek,
                  r_ropek, f"k{h}")
        for g8 in range(2):
            for jj in range(8):
                j = g8 * 8 + jj
                kb.op("tensor", lambda t, j=j, jj=jj: t.transpose(
                    psbf[:, 3072 + jj * 64: 3072 + (jj + 1) * 64],
                    qv[64:128, j * 128:(j + 1) * 128],
                    cx.bf(OFF_IDENT, 128)[64:128, 64:128]),
                    reads=[r_qv, cr(cx, "c_ident")],
                    war_only=[cx.psb(3)] if jj == 0 else (),
                    writes=[cx.psb(3)] if jj == 7 else (),
                    tag=f"vtp{h}.{j}")
            kb.op("vector", lambda v, g8=g8: v.tensor_copy(
                vt[:, g8 * 512:(g8 + 1) * 512], psbf[:, 3072:3584]),
                reads=[cx.psb(3)], writes=[r_vt], tag=f"vtc{h}.{g8}")
        def stage2_l0f(qt, h=h):
            base = cx.ps_a if qt % 2 == 0 else cx.ps_b
            br = cx.psa if qt % 2 == 0 else cx.psb
            bbf = base[:].bitcast(BF16)
            par = qt % 2
            nj = qt + 1
            for g8 in range((nj + 7) // 8):
                jlo = g8 * 8
                jhi = min(nj, jlo + 8)
                for j in range(jlo, jhi):
                    kb.op("tensor", lambda t, j=j, jlo=jlo, bbf=bbf: t.transpose(
                        bbf[:, 3072 + (j - jlo) * 128: 3072 + (j - jlo + 1) * 128],
                        tp2[par][:, j * 128:(j + 1) * 128], cx.bf(OFF_IDENT, 128)),
                        reads=[r_tp2[par], cr(cx, "c_ident")],
                        war_only=[br(3)] if j == jlo else (),
                        writes=[br(3)] if j == jhi - 1 else (),
                        tag=f"at{h}.{qt}.{j}")
                kb.op("vector", lambda v, jlo=jlo, jhi=jhi, bbf=bbf: v.tensor_copy(
                    at[:, 0:(jhi - jlo) * 128], bbf[:, 3072:3072 + (jhi - jlo) * 128]),
                    reads=[br(3)], war_only=[r_at], writes=[r_at],
                    tag=f"atc{h}.{qt}.{g8}")
                for j in range(jlo, jhi):
                    kb.op("tensor", lambda t, j=j, jlo=jlo, nj=nj, base=base:
                          t.matmul(
                        base[:, 0:64], at[:, (j - jlo) * 128:(j - jlo + 1) * 128],
                        vt[:, j * 64:(j + 1) * 64],
                        start=(j == 0), stop=(j == nj - 1), skip_group_check=True),
                        reads=[r_at, r_vt],
                        war_only=[br(0)] if j == 0 else (),
                        writes=[br(0)] if j == nj - 1 else (),
                        tag=f"av{h}.{qt}.{j}")
            kb.op("vector", lambda v, base=base: v.tensor_scalar_mul(
                ctxb, base[:, 0:64], rsum2[par]),
                reads=[br(0), r_rsum2[par]], war_only=[r_ctxb], writes=[r_ctxb],
                tag=f"cx{h}.{qt}")
            kb.op("tensor", lambda t, bbf=bbf: t.transpose(
                bbf[:, 3072:3200][0:64, :], ctxb, cx.bf(OFF_IDENT, 128)),
                reads=[r_ctxb, cr(cx, "c_ident")], war_only=[br(3)],
                writes=[br(3)], tag=f"cxt{h}.{qt}")
            c = h // 2
            r0 = (h % 2) * 64
            kb.op("vector", lambda v, c=c, r0=r0, qt=qt, bbf=bbf: v.tensor_copy(
                cx.bf(OFF_U + 4096 * c, T, rows=64, row0=r0)[:, qt * 128:(qt + 1) * 128],
                bbf[0:64, 3072:3200]),
                reads=[br(3)], writes=[r_ctx[c]], tag=f"cxc{h}.{qt}")

        def scores_l0f(qt, h=h):
            kl = (qt + 1) * 128
            base = cx.ps_a if qt % 2 == 0 else cx.ps_b
            br = cx.psa if qt % 2 == 0 else cx.psb
            nsub = (kl + 511) // 512
            for j in range(nsub):
                sl = slice(j * 512, min(kl, (j + 1) * 512))
                kb.op("tensor", lambda t, qt=qt, sl=sl, base=base: t.matmul(
                    base[:, sl], qv[0:64, qt * 128:(qt + 1) * 128],
                    kh[0:64, sl], start=True, stop=True),
                    reads=[r_qv, r_kh],
                    war_only=[br(sl.start // 512)],
                    writes=[br(sl.start // 512)],
                    tag=f"sc{h}.{qt}.{j}")

        def sm1_l0f(qt, h=h):
            kl = (qt + 1) * 128
            base = cx.ps_a if qt % 2 == 0 else cx.ps_b
            br = cx.psa if qt % 2 == 0 else cx.psb
            par = qt % 2
            kb.op("vector", lambda v, qt=qt, kl=kl, base=base: v.tensor_tensor(
                base[:, kl - 128:kl], base[:, kl - 128:kl], causal, ALU.add),
                reads=[br((kl - 128) // 512), r_causal],
                writes=[br((kl - 128) // 512)], tag=f"msk{h}.{qt}")
            kb.op("vector", lambda v, kl=kl, base=base: v.reduce_max(
                nmax2[par], base[:, 0:kl], axis=AX.X, negate=True),
                reads=[br(j) for j in range((kl + 511) // 512)],
                writes=[r_nmax2[par]], tag=f"mx{h}.{qt}")
            kb.op("scalar", lambda sc, kl=kl, base=base: sc.activation(
                tp2[par][:, 0:kl], base[:, 0:kl], AF.Exp, bias=nmax2[par],
                accum_out=ssum2[par]),
                reads=[r_nmax2[par]] + [br(j) for j in range((kl + 511) // 512)],
                war_only=[r_tp2[par]], writes=[r_tp2[par], r_ssum2[par]],
                tag=f"exp{h}.{qt}")
            kb.op("vector", lambda v: v.reciprocal(rsum2[par], ssum2[par]),
                  reads=[r_ssum2[par]], writes=[r_rsum2[par]], tag=f"rs{h}.{qt}")

        for qt in range(16):
            if qt >= 2:
                stage2_l0f(qt - 2)
            scores_l0f(qt)
            if qt >= 1:
                sm1_l0f(qt - 1)
        stage2_l0f(14)
        sm1_l0f(15)
        stage2_l0f(15)

    kb.phase_barrier("l0f_wo")
    vi = [(cx.bf(OFF_B + 4096 * i, T), kb.reg(f"vi{i}")) for i in range(KC)]
    r_ctx = [kb.reg(f"ctx{c}") for c in range(KC)]
    wo = cx.bf(OFF_W, 8192)
    r_wo = kb.reg("wo")
    kb.dma("gpsimd", lambda g: g.dma_start(wo, cx.inp["l0_wo"]),
           writes=[r_wo], tag="ld_wo")
    stage = cx.bf(OFF_W + 24576, 512)
    r_stage = kb.reg("wo_stage")
    for oc in range(KC):
        for kc in range(KC):
            for j in range(4):
                sl = slice(j * 512, (j + 1) * 512)
                kb.op("tensor", lambda t, oc=oc, kc=kc, sl=sl: t.matmul(
                    cx.ps_a[:, sl], wo[:, kc * 1024 + oc * 128: kc * 1024 + (oc + 1) * 128],
                    cx.bf(OFF_U + 4096 * kc, T)[:, sl],
                    start=(kc == 0), stop=(kc == KC - 1)),
                    reads=[r_wo, r_ctx[kc]],
                    war_only=[cx.psa(j)] if kc == 0 else (),
                    writes=[cx.psa(j)] if kc == KC - 1 else (),
                    tag=f"womm{oc}.{j}")
        for j in range(4):
            sl = slice(j * 512, (j + 1) * 512)
            kb.op("scalar", lambda sc, sl=sl: sc.copy(stage, cx.ps_a[:, sl]),
                  reads=[cx.psa(sl.start // 512)], war_only=[r_stage],
                  writes=[r_stage], tag=f"woev{oc}.{j}")
            kb.op("vector", lambda v, oc=oc, sl=sl: v.tensor_add(
                vi[oc][0][:, sl], vi[oc][0][:, sl], stage),
                reads=[r_stage, vi[oc][1]], writes=[vi[oc][1]],
                tag=f"woadd{oc}.{j}")
    import os
    if os.environ.get("L0F_DBG"):
        kb.phase_barrier("l0f_dbg")
        for i in range(KC):
            kb.op("vector", lambda v, i=i: v.tensor_copy(
                cx.f32(OFF_XR + 8192 * i, T), cx.bf(OFF_B + 4096 * i, T)),
                reads=[kb.reg(f"dbg_vi{i}")], writes=[kb.reg(f"xr{i}")],
                tag="dbgvi")


def emit_l0f_moe(cx):
    kb = cx.kb
    HN = T // 2
    import os as _os0
    _ths = [int(c) for c in _os0.environ.get("L0F_MOE_TH", "01")]
    for th in _ths:
        n0 = th * HN
        kb.phase_barrier(f"moe{th}")
        vi = [(cx.bf(OFF_B + 4096 * i, T)[:, n0:n0 + HN], kb.reg(f"vi{i}"))
              for i in range(KC)]

        def xn2v(c):
            if c < 6:
                return cx.bf(OFF_SCR + 2048 * c, HN)
            return cx.bf(OFF_SM + 2048 * (c - 6), HN)

        xn2 = [(xn2v(c), kb.reg(f"xn2_{c}")) for c in range(KC)]
        import os as _os5
        if _os5.environ.get("L0F_PROBS_DBG") and th == 1:
            kb.op("vector", lambda v: v.tensor_copy(
                cx.f32(OFF_XR + 6 * 8192, T), cx.bf(OFF_B, T)),
                reads=[kb.reg("vi0")], writes=[kb.reg("xr6"), kb.reg("xr7")],
                tag="dbg7")
        lnw_v = cx.f32(OFF_SM2 + 256, 8)
        r_lnw = kb.reg("ln2w")
        kb.dma("scalar", lambda s: s.dma_start(lnw_v, cx.inp["cw_ln2w"]),
               writes=[r_lnw], tag="ld_ln2w")
        ln_apply(cx, f"m{th}r_", vi, HN, 1e-6, False, True, "into", dsts=xn2,
                 lnw=(lnw_v, r_lnw, lambda i: i),
                 homes=(OFF_A, OFF_A + 2048, OFF_A + 4096, OFF_A + 8192))
        import os as _os4
        if _os4.environ.get("L0F_PROBS_DBG") and th == 1:
            kb.op("vector", lambda v: v.tensor_copy(
                cx.f32(OFF_XR + 3 * 8192, HN), cx.bf(OFF_A + 4096, HN)),
                reads=[kb.reg(f"m{th}r_rbf")], writes=[kb.reg("xr3")], tag="dbg4")
            kb.op("vector", lambda v: v.tensor_copy(
                cx.f32(OFF_XR + 4 * 8192, HN), cx.f32(OFF_A + 8192, HN)),
                reads=[kb.reg(f"m{th}r_var")], writes=[kb.reg("xr4")], tag="dbg5")
            kb.op("vector", lambda v: v.tensor_copy(
                cx.f32(OFF_XR + 5 * 8192, HN),
                cx.bf(OFF_B, T)[:, 1024:2048]),
                reads=[kb.reg("vi0")], writes=[kb.reg("xr5")], tag="dbg6")

        wr = cx.bf(OFF_A + 24576, 64)
        r_wr = kb.reg("wr")
        kb.dma("gpsimd", lambda g: g.dma_start(wr, cx.inp["l0_wr"]),
               writes=[r_wr], tag="ld_wr")
        rbias = cx.f32(OFF_SM2 + 128, 1, rows=8)
        r_rb = kb.reg("rbias")
        kb.dma("scalar", lambda s: s.dma_start(rbias, cx.inp["cw_rbias"]),
               writes=[r_rb], tag="ld_rbias")
        for j in range(2):
            sl = slice(j * 512, (j + 1) * 512)
            for kc in range(KC):
                kb.op("tensor", lambda t, kc=kc, sl=sl: t.matmul(
                    cx.ps_a[0:8, sl], wr[:, kc * 8:(kc + 1) * 8],
                    xn2[kc][0][:, sl], start=(kc == 0), stop=(kc == KC - 1)),
                    reads=[r_wr, xn2[kc][1]],
                    war_only=[cx.psa(j)] if kc == 0 else (),
                    writes=[cx.psa(j)] if kc == KC - 1 else (),
                    tag=f"rtm{th}.{j}")
        probs = cx.f32(OFF_A + 12288, HN, rows=8)
        probs_bf = cx.bf(OFF_A + 16384, HN, rows=8)
        r_probs = kb.reg("probs")
        kb.op("vector", lambda v: v.tensor_scalar_add(probs, cx.ps_a[0:8, 0:HN],
                                                      rbias),
              reads=[r_rb, cx.psa(0), cx.psa(1)], writes=[r_probs],
              tag=f"rt{th}bias")
        kb.op("vector", lambda v: v.tensor_scalar(probs, probs, -50.0, 50.0,
                                                  ALU.max, ALU.min),
              reads=[r_probs], writes=[r_probs], tag=f"rt{th}clip")
        kb.op("scalar", lambda sc: sc.activation(probs, probs, AF.Sigmoid),
              reads=[r_probs], writes=[r_probs], tag=f"rt{th}sig")
        kb.op("vector", lambda v: v.tensor_copy(probs_bf, probs),
              reads=[r_probs], writes=[r_probs], tag=f"rt{th}cast")
        pt = cx.bf(OFF_A + 18432, 64)
        r_pt = kb.reg("pt")
        psbf = cx.ps_b[:].bitcast(BF16)
        for tc in range(8):
            kb.op("tensor", lambda t, tc=tc: t.transpose(
                psbf[:, 3072 + tc * 8: 3072 + (tc + 1) * 8][0:128, :],
                probs_bf[:, tc * 128:(tc + 1) * 128],
                cx.bf(OFF_IDENT, 128)[0:8, 0:8]),
                reads=[r_probs, cr(cx, "c_ident")],
                war_only=[cx.psb(3)] if tc == 0 else (),
                writes=[cx.psb(3)] if tc == 7 else (),
                tag=f"rt{th}tp{tc}")
        kb.op("vector", lambda v: v.tensor_copy(pt, psbf[:, 3072:3136]),
              reads=[cx.psb(3)], writes=[r_pt], tag=f"rt{th}ptc")
        import os as _os3
        if _os3.environ.get("L0F_PROBS_DBG") and th == 1:
            kb.op("vector", lambda v: v.tensor_copy(
                cx.f32(OFF_XR, HN, rows=8), probs),
                reads=[r_probs], writes=[kb.reg("xr0")], tag="dbgp1")
            kb.op("vector", lambda v: v.tensor_copy(
                cx.f32(OFF_XR + 8192, 64), pt),
                reads=[r_pt], writes=[kb.reg("xr1")], tag="dbgp2")
            kb.op("vector", lambda v: v.tensor_copy(
                cx.f32(OFF_XR + 16384, HN), xn2[0][0]),
                reads=[xn2[0][1]], writes=[kb.reg("xr2")], tag="dbgp3")
        wtm = cx.f32(OFF_A + 20480, 16)
        wtmb = cx.bf(OFF_A + 20608, 16)
        r_wtm = kb.reg("wtm")
        m1 = cx.f32(OFF_SM2, 1)
        pcl = cx.f32(OFF_SM2 + 4, 1)
        twi = cx.f32(OFF_SM2 + 8, 1)
        sw = cx.f32(OFF_SM2 + 12, 1)
        msk1 = cx.f32(OFF_SM2 + 16, 1)
        r_m1, r_p, r_twi, r_sw, r_mk = (kb.reg("m1"), kb.reg("pcl"),
                                        kb.reg("twi"), kb.reg("sw"), kb.reg("mk"))
        wt_sb = cx.bf(OFF_SM + 4096, 1024, rows=16)
        r_wtsb = kb.reg("wt_sb")
        for tc in range(8):
            ptv = pt[:, tc * 8:(tc + 1) * 8]
            kb.op("vector", lambda v, ptv=ptv: v.reduce_max(m1, ptv, axis=AX.X),
                  reads=[r_pt], writes=[r_m1], tag=f"w{th}.{tc}m1")
            kb.op("vector", lambda v: v.tensor_scalar(pcl, m1, 1e-8, 1.0 - 1e-8,
                                                      ALU.max, ALU.min),
                  reads=[r_m1], writes=[r_p], tag=f"w{th}.{tc}p")
            kb.op("vector", lambda v: v.tensor_scalar(twi, pcl, 0.5 + 1e-8, 0.5,
                                                      ALU.add, ALU.max),
                  reads=[r_p], writes=[r_twi], tag=f"w{th}.{tc}t1")
            kb.op("vector", lambda v: v.tensor_scalar_min(twi, twi, 2.0),
                  reads=[r_twi], writes=[r_twi], tag=f"w{th}.{tc}t2")
            kb.op("vector", lambda v: v.reciprocal(twi, twi),
                  reads=[r_twi], writes=[r_twi], tag=f"w{th}.{tc}t3")
            kb.op("vector", lambda v: v.tensor_scalar_mul(wtm[:, 8:9], twi, 0.5),
                  reads=[r_twi], writes=[r_wtm], tag=f"w{th}.{tc}s0")
            kb.op("vector", lambda v: v.tensor_mul(sw, pcl, twi),
                  reads=[r_p, r_twi], writes=[r_sw], tag=f"w{th}.{tc}sw")
            for e in range(8):
                kb.op("vector", lambda v, ptv=ptv, e=e: v.tensor_scalar(
                    msk1, ptv[:, e:e + 1], m1, None, ALU.is_ge),
                    reads=[r_pt, r_m1], writes=[r_mk], tag=f"w{th}.{tc}k{e}")
                kb.op("vector", lambda v, e=e: v.tensor_mul(wtm[:, e:e + 1],
                                                            msk1, sw),
                      reads=[r_mk, r_sw], writes=[r_wtm], tag=f"w{th}.{tc}e{e}")
            kb.op("vector", lambda v: v.tensor_copy(wtmb, wtm),
                  reads=[r_wtm], writes=[r_wtm], tag=f"w{th}.{tc}c")
            kb.op("tensor", lambda t: t.transpose(
                psbf[0:16, 3072:3200], wtmb, cx.bf(OFF_IDENT, 128)),
                reads=[r_wtm, cr(cx, "c_ident")], war_only=[cx.psb(3)],
                writes=[cx.psb(3)], tag=f"w{th}.{tc}tp")
            kb.op("vector", lambda v, tc=tc: v.tensor_copy(
                wt_sb[:, tc * 128:(tc + 1) * 128], psbf[0:16, 3072:3200]),
                reads=[cx.psb(3)], writes=[r_wtsb], tag=f"w{th}.{tc}cc")

        if _os5.environ.get("L0F_TH1_STOP") and th == 1:
            continue
        kb.phase_barrier(f"moe{th}x")
        xn2 = [(xn2v(c), kb.reg(f"xn2_{c}")) for c in range(KC)]
        wt_sb = cx.bf(OFF_SM + 4096, 1024, rows=16)
        r_wtsb = kb.reg("wt_sb")
        vi = [(cx.bf(OFF_B + 4096 * i, T)[:, n0:n0 + HN], kb.reg(f"vi{i}"))
              for i in range(KC)]
        uacc = [(cx.f32(OFF_U + 4096 * i, HN), kb.reg(f"uacc{i}"))
                for i in range(KC)]
        for i in range(KC):
            kb.op("vector",
                  lambda v, uu=uacc[i][0], vv=vi[i][0]: v.tensor_copy(uu, vv),
                  reads=[vi[i][1]], writes=[uacc[i][1]], tag=f"u{th}init{i}")

        w1 = [cx.bf(OFF_W + 4096 * k, 2048) for k in range(2)]
        w3 = [cx.bf(OFF_W + 8192 + 4096 * k, 2048) for k in range(2)]
        w2 = [cx.bf(OFF_W + 16384 + 4096 * k, 2048) for k in range(2)]
        r_w1 = [kb.reg(f"mw1{k}") for k in range(2)]
        r_w3 = [kb.reg(f"mw3{k}") for k in range(2)]
        r_w2 = [kb.reg(f"mw2{k}") for k in range(2)]
        wrow = cx.bf(OFF_W + 24576, 1024, rows=1)
        r_wrow = kb.reg("wrow")
        sg_sb = cx.bf(OFF_W + 24576, 1024)
        r_sg = kb.reg("sg_sb")
        wbc = cx.bf(OFF_SM + 6144, 1024)
        r_wbc = kb.reg("wbc")
        hch = [(cx.bf(OFF_A + 2048 * c, HN), kb.reg(f"h{c}")) for c in range(16)]

        NE = int(os.environ.get("L0F_MOE_NE", 9)) if "os" in dir() else 9
        import os as _os
        NE = int(_os.environ.get("L0F_MOE_NE", 9))
        for e in ([8] if NE == -1 else range(NE)):
            kb.dma("gpsimd", lambda g, e=e: g.dma_start(
                wrow, wt_sb[e:e + 1, :]),
                reads=[r_wtsb], war_only=[r_sg], writes=[r_wrow],
                tag=f"e{th}.{e}row")
            for j in range(2):
                sl = slice(j * 512, (j + 1) * 512)
                kb.op("tensor", lambda t, sl=sl: t.matmul(
                    cx.ps_b[:, sl], cx.bf(OFF_ONES128, 128)[0:1, :],
                    wrow[:, sl], start=True, stop=True),
                    reads=[r_wrow, cr(cx, "c_ones128")],
                    war_only=[cx.psb(j)], writes=[cx.psb(j)],
                    tag=f"e{th}.{e}bc{j}")
            kb.op("vector", lambda v: v.tensor_copy(wbc, cx.ps_b[:, 0:1024]),
                  reads=[cx.psb(0), cx.psb(1)], war_only=[r_wbc],
                  writes=[r_wbc], tag=f"e{th}.{e}bcc")
            for st_ in range(8):
                sk = st_ % 2
                kb.dma("gpsimd", lambda g, e=e, st_=st_, sk=sk: g.dma_start(
                    w1[sk], cx.inp["l0_moe_w1"][e, st_]),
                    writes=[r_w1[sk]], tag=f"e{th}.{e}.{st_}w1")
                kb.dma("gpsimd", lambda g, e=e, st_=st_, sk=sk: g.dma_start(
                    w3[sk], cx.inp["l0_moe_w3"][e, st_]),
                    writes=[r_w3[sk]], tag=f"e{th}.{e}.{st_}w3")
                for c in range(2):
                    hc = st_ * 2 + c
                    pr = hc % 2
                    gps = cx.ps_a[:, pr * 1024:(pr + 1) * 1024]
                    ups = cx.ps_b[:, pr * 1024:(pr + 1) * 1024]
                    gr = [cx.psa(pr * 2), cx.psa(pr * 2 + 1)]
                    ur = [cx.psb(pr * 2), cx.psb(pr * 2 + 1)]
                    for kc in range(KC):
                        for j in range(2):
                            sl = slice(j * 512, (j + 1) * 512)
                            kb.op("tensor", lambda t, sk=sk, c=c, kc=kc, sl=sl,
                                  gps=gps: t.matmul(gps[:, sl],
                                           w1[sk][:, kc * 256 + c * 128:
                                                  kc * 256 + c * 128 + 128],
                                           xn2[kc][0][:, sl],
                                           start=(kc == 0), stop=(kc == KC - 1)),
                                  reads=[r_w1[sk], xn2[kc][1]],
                                  war_only=[gr[j]] if kc == 0 else (),
                                  writes=[gr[j]] if kc == KC - 1 else (),
                                  tag=f"g{th}.{e}.{hc}.{j}")
                    for kc in range(KC):
                        for j in range(2):
                            sl = slice(j * 512, (j + 1) * 512)
                            kb.op("tensor", lambda t, sk=sk, c=c, kc=kc, sl=sl,
                                  ups=ups: t.matmul(ups[:, sl],
                                           w3[sk][:, kc * 256 + c * 128:
                                                  kc * 256 + c * 128 + 128],
                                           xn2[kc][0][:, sl],
                                           start=(kc == 0), stop=(kc == KC - 1)),
                                  reads=[r_w3[sk], xn2[kc][1]],
                                  war_only=[ur[j]] if kc == 0 else (),
                                  writes=[ur[j]] if kc == KC - 1 else (),
                                  tag=f"u{th}.{e}.{hc}.{j}")
                    kb.op("scalar", lambda sc, hc=hc, ups=ups: sc.copy(
                        hch[hc][0], ups),
                        reads=ur, writes=[hch[hc][1]],
                        tag=f"he{th}.{e}.{hc}")
                    kb.op("gpsimd", lambda g, hc=hc: g.tensor_tensor(
                        hch[hc][0], hch[hc][0], wbc, ALU.mult),
                        reads=[hch[hc][1], r_wbc], writes=[hch[hc][1]],
                        tag=f"hw{th}.{e}.{hc}")
                    kb.op("scalar", lambda sc, hc=hc, gps=gps: sc.activation(
                        sg_sb, gps, AF.Silu),
                        reads=gr,
                        war_only=[r_wrow], writes=[r_sg],
                        tag=f"sg{th}.{e}.{hc}")
                    kb.op("vector", lambda v, hc=hc: v.tensor_tensor(
                        hch[hc][0], hch[hc][0], sg_sb, ALU.mult),
                        reads=[hch[hc][1], r_sg], writes=[hch[hc][1]],
                        tag=f"hs{th}.{e}.{hc}")
            for j in range(2):
                sl = slice(j * 512, (j + 1) * 512)
                for st_ in range(8):
                    sk = st_ % 2
                    kb.dma("gpsimd", lambda g, e=e, st_=st_, sk=sk: g.dma_start(
                        w2[sk], cx.inp["l0_moe_w2"][e, st_]),
                        writes=[r_w2[sk]], tag=f"e{th}.{e}.{j}.{st_}w2")
                    for c2 in range(2):
                        hc = st_ * 2 + c2
                        first = st_ == 0 and c2 == 0
                        last = st_ == 7 and c2 == 1
                        for od in range(8):
                            ps = cx.ps_a if od < 4 else cx.ps_b
                            psr = (cx.psa if od < 4 else cx.psb)(od % 4)
                            kb.op("tensor",
                                  lambda t, sk=sk, c2=c2, od=od, sl=sl, hc=hc,
                                  first=first, last=last, ps=ps: t.matmul(
                                      ps[:, (od % 4) * 512:(od % 4 + 1) * 512],
                                      w2[sk][:, c2 * 1024 + od * 128:
                                             c2 * 1024 + (od + 1) * 128],
                                      hch[hc][0][:, sl],
                                      start=first, stop=last),
                                  reads=[r_w2[sk], hch[hc][1]],
                                  war_only=[psr] if first else (),
                                  writes=[psr] if last else (),
                                  tag=f"d{th}.{e}.{j}.{hc}.{od}")
                for od in range(8):
                    ps = cx.ps_a if od < 4 else cx.ps_b
                    psr = (cx.psa if od < 4 else cx.psb)(od % 4)
                    kb.op("vector", lambda v, ps=ps, od=od, sl=sl: v.tensor_add(
                        uacc[od][0][:, sl], uacc[od][0][:, sl],
                        ps[:, (od % 4) * 512:(od % 4 + 1) * 512]),
                        reads=[psr, uacc[od][1]], writes=[uacc[od][1]],
                        tag=f"ue{th}.{e}.{j}.{od}")
        for i in range(KC):
            kb.op("vector",
                  lambda v, vv=vi[i][0], uu=uacc[i][0]: v.tensor_copy(vv, uu),
                  reads=[uacc[i][1]], writes=[vi[i][1]], tag=f"vb{th}.{i}")
        if _os5.environ.get("L0F_PROBS_DBG") and th == 0:
            kb.op("vector", lambda v: v.tensor_copy(
                cx.f32(OFF_XR + 4 * 8192, T), cx.bf(OFF_B, T)),
                reads=[kb.reg("vi0")], writes=[kb.reg("xr4"), kb.reg("xr5")],
                tag="dbg8")

    kb.phase_barrier("l0f_out")
    import os as _os2
    if _os2.environ.get("L0F_PROBS_DBG"):
        return
    if _os2.environ.get("L0F_WBC_DBG"):
        kb.op("vector", lambda v: v.tensor_copy(
            cx.f32(OFF_XR, 1024), cx.bf(OFF_SM + 6144, 1024)),
            reads=[kb.reg("wbc")], writes=[kb.reg("xr0")], tag="dbgwbc")
        kb.op("vector", lambda v: v.tensor_copy(
            cx.f32(OFF_XR + 8192, 1024), cx.bf(OFF_SM + 4096, 1024)),
            reads=[kb.reg("wt_sb")], writes=[kb.reg("xr1")], tag="dbgwt")
        return
    if _os2.environ.get("L0F_MOE_DBG"):
        for i in range(KC):
            kb.op("vector", lambda v, i=i: v.tensor_copy(
                cx.f32(OFF_XR + 8192 * i, T), cx.bf(OFF_B + 4096 * i, T)),
                reads=[kb.reg(f"dvi{i}")], writes=[kb.reg(f"xr{i}")], tag="dbgmoe")
        return
    vif = [(cx.bf(OFF_B + 4096 * i, T), kb.reg(f"vif{i}")) for i in range(KC)]
    ln_add_into_xr(cx, "l0fln_", vif, T)


def emit_l1a(cx):
    kb = cx.kb
    H = 16
    xr = [(cx.f32(OFF_XR + 8192 * i, T), kb.reg(f"xr{i}")) for i in range(KC)]
    xn = [(cx.bf(OFF_A + 4096 * i, T), kb.reg(f"xn{i}")) for i in range(KC)]
    ln_apply(cx, "g0_", xr, T, 1e-5, True, False, "into", dsts=xn)
    kb.phase_barrier("l1a_heads")
    xn = [(cx.bf(OFF_A + 4096 * i, T), kb.reg(f"xn{i}")) for i in range(KC)]

    qhat = cx.bf(OFF_B, 2048)
    khat = cx.bf(OFF_B + 4096, 2048)
    vbf = cx.bf(OFF_B + 8192, 2048)
    vt = cx.bf(OFF_B + 12288, 1024)
    tp = cx.bf(OFF_B + 16384, 2048)
    at = cx.bf(OFF_B + 20480, 1024)
    qvar = cx.bf(OFF_B + 22528, 2048, rows=1)
    kvar = cx.bf(OFF_B + 26624, 2048, rows=1)
    rowf = cx.f32(OFF_SM, 2048, rows=1)
    nmax2 = [cx.f32(OFF_SM2, 1), cx.f32(OFF_SM2 + 12, 1)]
    ssum2 = [cx.f32(OFF_SM2 + 4, 1), cx.f32(OFF_SM2 + 16, 1)]
    rsum2 = [cx.f32(OFF_SM2 + 8, 1), cx.f32(OFF_SM2 + 20, 1)]
    r_nmax2 = [None, None]
    r_ssum2 = [None, None]
    r_rsum2 = [None, None]
    r_qh, r_kh, r_vbf, r_vt, r_tp, r_at = (kb.reg("qh"), kb.reg("kh"),
                                           kb.reg("vbf"), kb.reg("vt"),
                                           kb.reg("tp"), kb.reg("at"))
    r_qv, r_kv = kb.reg("qvar"), kb.reg("kvar")
    r_rowf = kb.reg("rowf")
    r_nmax2 = [kb.reg("nmax0"), kb.reg("nmax1")]
    r_ssum2 = [kb.reg("ssum0"), kb.reg("ssum1")]
    r_rsum2 = [kb.reg("rsum0"), kb.reg("rsum1")]
    r_ctx = [kb.reg(f"ctx{c}") for c in range(KC)]
    wsl = [cx.bf(OFF_W + 1024 * k, 512) for k in range(6)]
    r_wsl = [kb.reg(f"l1w{k}") for k in range(6)]
    psbf = cx.ps_b[:].bitcast(BF16)
    ident = cx.bf(OFF_IDENT, 128)
    ones = cx.bf(OFF_ONES128, 128)

    def proj(h, wslot, r_w, wname, dst, r_dst, dst_row0, scale, tag):
        kb.dma("gpsimd", lambda g, wslot=wslot, h=h, wname=wname: g.dma_start(
            wslot, cx.inp[wname][h]), writes=[r_w], tag=f"{tag}ld{h}")
        for kc in range(KC):
            for j in range(4):
                sl = slice(j * 512, (j + 1) * 512)
                kb.op("tensor", lambda t, wslot=wslot, kc=kc, sl=sl: t.matmul(
                    cx.ps_a[0:64, sl], wslot[:, kc * 64:(kc + 1) * 64],
                    xn[kc][0][:, sl], start=(kc == 0), stop=(kc == KC - 1)),
                    reads=[r_w, xn[kc][1]],
                    war_only=[cx.psa(j)] if kc == 0 else (),
                    writes=[cx.psa(j)] if kc == KC - 1 else (),
                    tag=f"{tag}mm{h}.{j}")
        kb.op("scalar", lambda sc, dst=dst, dst_row0=dst_row0, scale=scale:
              sc.activation(dst[dst_row0:dst_row0 + 64, :], cx.ps_a[0:64, 0:T],
                            AF.Copy, scale=scale),
              reads=[cx.psa(j) for j in range(4)], writes=[r_dst],
              tag=f"{tag}ev{h}")

    def stats(h, src, r_src, extra_dst, r_ed, varrow, r_vr, mscale, vscale, tag):
        kb.op("gpsimd", lambda g, src=src: g.tensor_tensor(
            tp[0:64, :], src[0:64, :], src[0:64, :], ALU.mult),
            reads=[r_src], war_only=[r_tp], writes=[r_tp], tag=f"{tag}sq{h}")
        for j in range(4):
            sl = slice(j * 512, (j + 1) * 512)
            kb.op("tensor", lambda t, src=src, sl=sl: t.matmul(
                cx.ps_b[0:1, sl], ones[0:64, 0:1], src[0:64, sl],
                start=True, stop=True),
                reads=[r_src, cr(cx, "c_ones128")],
                war_only=[cx.psb(j)], writes=[cx.psb(j)], tag=f"{tag}sm{h}.{j}")
            kb.op("tensor", lambda t, sl=sl: t.matmul(
                cx.ps_a[0:1, sl], ones[0:64, 0:1], tp[0:64, sl],
                start=True, stop=True),
                reads=[r_tp, cr(cx, "c_ones128")],
                war_only=[cx.psa(j)], writes=[cx.psa(j)], tag=f"{tag}qm{h}.{j}")
        kb.op("scalar", lambda sc, extra_dst=extra_dst, mscale=mscale:
              sc.activation(extra_dst[64:65, :], cx.ps_b[0:1, 0:T], AF.Copy,
                            scale=mscale),
              reads=[cx.psb(j) for j in range(4)], writes=[r_ed],
              tag=f"{tag}mr{h}")
        kb.op("scalar", lambda sc: sc.activation(rowf, cx.ps_b[0:1, 0:T],
                                                 AF.Copy),
              reads=[cx.psb(j) for j in range(4)], war_only=[r_rowf],
              writes=[r_rowf], tag=f"{tag}r1{h}")
        kb.op("vector", lambda v: v.tensor_mul(rowf, rowf, rowf),
              reads=[r_rowf], writes=[r_rowf], tag=f"{tag}r2{h}")
        kb.op("vector", lambda v, varrow=varrow: v.scalar_tensor_tensor(
            varrow[0:1, :], rowf, -1.0 / 64.0, cx.ps_a[0:1, 0:T],
            ALU.mult, ALU.add),
            reads=[r_rowf] + [cx.psa(j) for j in range(4)],
            war_only=[r_vr], writes=[r_vr], tag=f"{tag}r3{h}")
        kb.op("vector", lambda v, varrow=varrow, vscale=vscale:
              v.tensor_scalar_mul(varrow[0:1, :], varrow[0:1, :], vscale),
              reads=[r_vr], writes=[r_vr], tag=f"{tag}r4{h}")

    for h in range(H):
        proj(h, wsl[0], r_wsl[0], "l1_wq_slices", qhat, r_qh, 0, 1.0 / 64.0, "q")
        stats(h, qhat, r_qh, qhat, r_qh, qvar, r_qv, -1.0, 64.0 / 63.0, "qs")
        proj(h, wsl[1], r_wsl[1], "l1_wk_slices", khat, r_kh, 0, 1.0, "k")
        stats(h, khat, r_kh, khat, r_kh, kvar, r_kv, 1.0 / 64.0, 1.0 / 63.0, "ks")
        proj(h, wsl[2], r_wsl[2], "l1_wv_slices", vbf, r_vbf, 0, 1.0, "v")
        for g8 in range(2):
            for jj in range(8):
                j = g8 * 8 + jj
                kb.op("tensor", lambda t, j=j, jj=jj: t.transpose(
                    psbf[:, 3072 + jj * 64: 3072 + (jj + 1) * 64],
                    vbf[0:64, j * 128:(j + 1) * 128], ident[0:64, 0:64]),
                    reads=[r_vbf, cr(cx, "c_ident")],
                    war_only=[cx.psb(3)] if jj == 0 else (),
                    writes=[cx.psb(3)] if jj == 7 else (),
                    tag=f"vtp{h}.{j}")
            kb.op("vector", lambda v, g8=g8: v.tensor_copy(
                vt[:, g8 * 512:(g8 + 1) * 512], psbf[:, 3072:3584]),
                reads=[cx.psb(3)], war_only=[r_vt], writes=[r_vt],
                tag=f"vtc{h}.{g8}")
        def stage2_l1a(qt, h=h):
            base = cx.ps_a if qt % 2 == 0 else cx.ps_b
            br = cx.psa if qt % 2 == 0 else cx.psb
            bbf = base[:].bitcast(BF16)
            par = qt % 2
            for g8 in range(2):
                jlo, jhi = g8 * 8, g8 * 8 + 8
                for j in range(jlo, jhi):
                    kb.op("tensor", lambda t, j=j, jlo=jlo, bbf=bbf: t.transpose(
                        bbf[:, 3072 + (j - jlo) * 128: 3072 + (j - jlo + 1) * 128],
                        tp[:, j * 128:(j + 1) * 128], ident),
                        reads=[r_tp, cr(cx, "c_ident")],
                        war_only=[br(3)] if j == jlo else (),
                        writes=[br(3)] if j == jhi - 1 else (),
                        tag=f"at{h}.{qt}.{j}")
                kb.op("vector", lambda v, bbf=bbf: v.tensor_copy(
                    at, bbf[:, 3072:4096]),
                    reads=[br(3)], war_only=[r_at], writes=[r_at],
                    tag=f"atc{h}.{qt}.{g8}")
                for j in range(jlo, jhi):
                    kb.op("tensor", lambda t, j=j, jlo=jlo, base=base: t.matmul(
                        base[:, 0:64], at[:, (j - jlo) * 128:(j - jlo + 1) * 128],
                        vt[:, j * 64:(j + 1) * 64],
                        start=(j == 0), stop=(j == 15), skip_group_check=True),
                        reads=[r_at, r_vt],
                        war_only=[br(0)] if j == 0 else (),
                        writes=[br(0)] if j == 15 else (),
                        tag=f"av{h}.{qt}.{j}")
            ctxb = cx.bf(OFF_SCR + 8704, 64)
            r_ctxb = kb.reg("ctxb")
            kb.op("vector", lambda v, ctxb=ctxb, base=base: v.tensor_scalar_mul(
                ctxb, base[:, 0:64], rsum2[par]),
                reads=[br(0), r_rsum2[par]], war_only=[r_ctxb], writes=[r_ctxb],
                tag=f"cx{h}.{qt}")
            kb.op("tensor", lambda t, ctxb=ctxb, bbf=bbf: t.transpose(
                bbf[:, 3072:3200][0:64, :], ctxb, ident),
                reads=[r_ctxb, cr(cx, "c_ident")], war_only=[br(3)],
                writes=[br(3)], tag=f"cxt{h}.{qt}")
            c = h // 2
            r0 = (h % 2) * 64
            kb.op("vector", lambda v, c=c, r0=r0, qt=qt, bbf=bbf: v.tensor_copy(
                cx.bf(OFF_U + 4096 * c, T, rows=64, row0=r0)[:, qt * 128:(qt + 1) * 128],
                bbf[0:64, 3072:3200]),
                reads=[br(3)], writes=[r_ctx[c]], tag=f"cxc{h}.{qt}")

        def scores_l1a(qt, h=h):
            base = cx.ps_a if qt % 2 == 0 else cx.ps_b
            br = cx.psa if qt % 2 == 0 else cx.psb
            for j in range(4):
                sl = slice(j * 512, (j + 1) * 512)
                kb.op("tensor", lambda t, qt=qt, sl=sl, base=base: t.matmul(
                    base[:, sl], qhat[0:65, qt * 128:(qt + 1) * 128],
                    khat[0:65, sl], start=True, stop=False,
                    skip_group_check=True),
                    reads=[r_qh, r_kh],
                    war_only=[br(j)], writes=[], tag=f"sc{h}.{qt}.{j}")
                kb.op("tensor", lambda t, qt=qt, sl=sl, base=base: t.matmul(
                    base[:, sl], qvar[0:1, qt * 128:(qt + 1) * 128],
                    kvar[0:1, sl], start=False, stop=True,
                    skip_group_check=True),
                    reads=[r_qv, r_kv],
                    writes=[br(j)], tag=f"sv{h}.{qt}.{j}")

        def sm1_l1a(qt, h=h):
            base = cx.ps_a if qt % 2 == 0 else cx.ps_b
            br = cx.psa if qt % 2 == 0 else cx.psb
            par = qt % 2
            kb.op("vector", lambda v, base=base: v.reduce_max(
                nmax2[par], base[:, 0:T], axis=AX.X, negate=True),
                reads=[br(j) for j in range(4)],
                writes=[r_nmax2[par]], tag=f"mx{h}.{qt}")
            kb.op("scalar", lambda sc, base=base: sc.activation(
                tp, base[:, 0:T], AF.Exp, bias=nmax2[par], accum_out=ssum2[par]),
                reads=[r_nmax2[par]] + [br(j) for j in range(4)],
                war_only=[r_tp], writes=[r_tp, r_ssum2[par]], tag=f"exp{h}.{qt}")
            kb.op("vector", lambda v: v.reciprocal(rsum2[par], ssum2[par]),
                  reads=[r_ssum2[par]], writes=[r_rsum2[par]], tag=f"rs{h}.{qt}")

        for qt in range(16):
            if qt >= 2:
                stage2_l1a(qt - 2)
            scores_l1a(qt)
            if qt >= 1:
                sm1_l1a(qt - 1)
        stage2_l1a(14)
        sm1_l1a(15)
        stage2_l1a(15)

    kb.phase_barrier("l1a_wo")
    r_ctx = [kb.reg(f"ctx{c}") for c in range(KC)]
    wo = cx.bf(OFF_W + 8192, 8192)
    r_wo = kb.reg("l1wo")
    kb.dma("gpsimd", lambda g: g.dma_start(wo, cx.inp["l1_wo"]),
           writes=[r_wo], tag="l1ld_wo")
    outp = [(cx.bf(OFF_B + 4096 * i, T), kb.reg(f"op{i}")) for i in range(KC)]
    for oc in range(KC):
        for kc in range(KC):
            for j in range(4):
                sl = slice(j * 512, (j + 1) * 512)
                kb.op("tensor", lambda t, oc=oc, kc=kc, sl=sl: t.matmul(
                    cx.ps_a[:, sl], wo[:, kc * 1024 + oc * 128: kc * 1024 + (oc + 1) * 128],
                    cx.bf(OFF_U + 4096 * kc, T)[:, sl],
                    start=(kc == 0), stop=(kc == KC - 1)),
                    reads=[r_wo, r_ctx[kc]],
                    war_only=[cx.psa(j)] if kc == 0 else (),
                    writes=[cx.psa(j)] if kc == KC - 1 else (),
                    tag=f"l1wo{oc}.{j}")
        for j in range(4):
            sl = slice(j * 512, (j + 1) * 512)
            kb.op("scalar", lambda sc, oc=oc, sl=sl: sc.copy(
                outp[oc][0][:, sl], cx.ps_a[:, sl]),
                reads=[cx.psa(sl.start // 512)], writes=[outp[oc][1]],
                tag=f"l1woev{oc}.{j}")
    kb.phase_barrier("l1a_ln")
    outp = [(cx.bf(OFF_B + 4096 * i, T), kb.reg(f"op{i}")) for i in range(KC)]
    ln_add_into_xr(cx, "l1aln_", outp, T)


def emit_l1f(cx):
    kb = cx.kb
    HN = T // 2

    for th in range(2):
        n0 = th * HN
        kb.phase_barrier(f"l1f{th}")
        xr = [(cx.f32(OFF_XR + 8192 * i, T)[:, n0:n0 + HN], kb.reg(f"xr{i}"))
              for i in range(KC)]

        def xn2v(c):
            if c < 6:
                return cx.bf(OFF_SCR + 2048 * c, HN)
            return cx.bf(OFF_SM + 2048 * (c - 6), HN)

        xn2 = [(xn2v(c), kb.reg(f"xn2_{c}")) for c in range(KC)]
        ln_apply(cx, f"f{th}l_", xr, HN, 1e-5, True, False, "into", dsts=xn2,
                 homes=(OFF_A, OFF_A + 2048, OFF_A + 4096, OFF_A + 8192))

        gw = cx.bf(OFF_A + 24576, 64)
        r_gw = kb.reg("gw")
        kb.dma("gpsimd", lambda g: g.dma_start(gw, cx.inp["l1_gatew"]),
               writes=[r_gw], tag=f"f{th}ldgw")
        for j in range(2):
            sl = slice(j * 512, (j + 1) * 512)
            for kc in range(KC):
                kb.op("tensor", lambda t, kc=kc, sl=sl: t.matmul(
                    cx.ps_a[0:8, sl], gw[:, kc * 8:(kc + 1) * 8],
                    xn2[kc][0][:, sl], start=(kc == 0), stop=(kc == KC - 1)),
                    reads=[r_gw, xn2[kc][1]],
                    war_only=[cx.psa(j)] if kc == 0 else (),
                    writes=[cx.psa(j)] if kc == KC - 1 else (),
                    tag=f"f{th}rt{j}")
        lg_bf = cx.bf(OFF_A + 16384, HN, rows=8)
        r_lg = kb.reg("lg")
        kb.op("vector", lambda v: v.tensor_copy(lg_bf, cx.ps_a[0:8, 0:HN]),
              reads=[cx.psa(0), cx.psa(1)], writes=[r_lg], tag=f"f{th}lgc")
        ltm = cx.bf(OFF_A + 18432, 64)
        r_ltm = kb.reg("ltm")
        psbf = cx.ps_b[:].bitcast(BF16)
        ident = cx.bf(OFF_IDENT, 128)
        for tc in range(8):
            kb.op("tensor", lambda t, tc=tc: t.transpose(
                psbf[:, 3072 + tc * 8: 3072 + (tc + 1) * 8][0:128, :],
                lg_bf[:, tc * 128:(tc + 1) * 128], ident[0:8, 0:8]),
                reads=[r_lg, cr(cx, "c_ident")],
                war_only=[cx.psb(3)] if tc == 0 else (),
                writes=[cx.psb(3)] if tc == 7 else (),
                tag=f"f{th}tp{tc}")
        kb.op("vector", lambda v: v.tensor_copy(ltm, psbf[:, 3072:3136]),
              reads=[cx.psb(3)], writes=[r_ltm], tag=f"f{th}ltc")
        pt = cx.f32(OFF_A + 20480, 64)
        wtm = cx.f32(OFF_A + 20736, 8)
        wtmb = cx.bf(OFF_A + 20800, 8)
        msked = cx.f32(OFF_A + 20864, 8)
        r_pt, r_wtm, r_msk2 = kb.reg("pt"), kb.reg("wtm"), kb.reg("mskd")
        nm = cx.f32(OFF_SM2, 1)
        sm = cx.f32(OFF_SM2 + 4, 1)
        rs = cx.f32(OFF_SM2 + 8, 1)
        m1 = cx.f32(OFF_SM2 + 12, 1)
        m2 = cx.f32(OFF_SM2 + 16, 1)
        den = cx.f32(OFF_SM2 + 20, 1)
        mka = cx.f32(OFF_SM2 + 24, 1)
        r_nm, r_sm, r_rs = kb.reg("nm"), kb.reg("sm"), kb.reg("rs")
        r_m1, r_m2, r_den, r_mka = (kb.reg("m1f"), kb.reg("m2f"),
                                    kb.reg("denf"), kb.reg("mkaf"))
        wt_sb = cx.bf(OFF_SM + 4096, 1024, rows=8)
        r_wtsb = kb.reg("wt_sb")
        for tc in range(8):
            lv = ltm[:, tc * 8:(tc + 1) * 8]
            kb.op("vector", lambda v, lv=lv: v.reduce_max(nm, lv, axis=AX.X,
                                                          negate=True),
                  reads=[r_ltm], writes=[r_nm], tag=f"f{th}w{tc}m")
            kb.op("scalar", lambda sc, lv=lv: sc.activation(
                pt[:, tc * 8:(tc + 1) * 8] if False else pt[:, 0:8], lv, AF.Exp,
                bias=nm, accum_out=sm),
                reads=[r_nm, r_ltm], war_only=[r_pt], writes=[r_pt, r_sm],
                tag=f"f{th}w{tc}e")
            kb.op("vector", lambda v: v.reciprocal(rs, sm),
                  reads=[r_sm], writes=[r_rs], tag=f"f{th}w{tc}r")
            kb.op("vector", lambda v: v.tensor_scalar_mul(pt[:, 0:8],
                                                          pt[:, 0:8], rs),
                  reads=[r_pt, r_rs], writes=[r_pt], tag=f"f{th}w{tc}p")
            kb.op("vector", lambda v: v.reduce_max(m1, pt[:, 0:8], axis=AX.X),
                  reads=[r_pt], writes=[r_m1], tag=f"f{th}w{tc}m1")
            kb.op("vector", lambda v: v.tensor_scalar(msked, pt[:, 0:8], m1,
                                                      None, ALU.is_ge),
                  reads=[r_pt, r_m1], writes=[r_msk2], tag=f"f{th}w{tc}k")
            kb.op("vector", lambda v: v.scalar_tensor_tensor(
                msked, msked, -1e9, pt[:, 0:8], ALU.mult, ALU.add),
                reads=[r_msk2, r_pt], writes=[r_msk2], tag=f"f{th}w{tc}s")
            kb.op("vector", lambda v: v.reduce_max(m2, msked, axis=AX.X),
                  reads=[r_msk2], writes=[r_m2], tag=f"f{th}w{tc}m2")
            kb.op("vector", lambda v: v.tensor_add(den, m1, m2),
                  reads=[r_m1, r_m2], writes=[r_den], tag=f"f{th}w{tc}d")
            kb.op("vector", lambda v: v.reciprocal(den, den),
                  reads=[r_den], writes=[r_den], tag=f"f{th}w{tc}dr")
            for e in range(8):
                kb.op("vector", lambda v, e=e: v.tensor_scalar(
                    mka, pt[:, e:e + 1], m2, None, ALU.is_ge),
                    reads=[r_pt, r_m2], writes=[r_mka], tag=f"f{th}w{tc}k{e}")
                kb.op("vector", lambda v, e=e: v.tensor_mul(
                    wtm[:, e:e + 1], mka, pt[:, e:e + 1]),
                    reads=[r_mka, r_pt], writes=[r_wtm], tag=f"f{th}w{tc}x{e}")
            kb.op("vector", lambda v: v.tensor_scalar_mul(wtm, wtm, den),
                  reads=[r_wtm, r_den], writes=[r_wtm], tag=f"f{th}w{tc}n")
            kb.op("vector", lambda v: v.tensor_copy(wtmb, wtm),
                  reads=[r_wtm], writes=[r_wtm], tag=f"f{th}w{tc}c")
            kb.op("tensor", lambda t: t.transpose(
                psbf[0:8, 3072:3200], wtmb, ident),
                reads=[r_wtm, cr(cx, "c_ident")], war_only=[cx.psb(3)],
                writes=[cx.psb(3)], tag=f"f{th}w{tc}t")
            kb.op("vector", lambda v, tc=tc: v.tensor_copy(
                wt_sb[:, tc * 128:(tc + 1) * 128], psbf[0:8, 3072:3200]),
                reads=[cx.psb(3)], writes=[r_wtsb], tag=f"f{th}w{tc}o")

        kb.phase_barrier(f"l1f{th}x")
        xn2 = [(xn2v(c), kb.reg(f"xn2_{c}")) for c in range(KC)]
        wt_sb = cx.bf(OFF_SM + 4096, 1024, rows=8)
        r_wtsb = kb.reg("wt_sb")
        uacc = [(cx.f32(OFF_U + 4096 * i, HN), kb.reg(f"uacc{i}"))
                for i in range(KC)]

        gu = [cx.bf(OFF_W + 8192 * k, 4096) for k in range(2)]
        dn = [cx.bf(OFF_W + 16384 + 4096 * k, 2048) for k in range(2)]
        r_gu = [kb.reg(f"gu{k}") for k in range(2)]
        r_dn = [kb.reg(f"dn{k}") for k in range(2)]
        wrow = cx.bf(OFF_W + 24576, 1024, rows=1)
        r_wrow = kb.reg("wrow")
        sg_sb = cx.bf(OFF_W + 24576, 1024)
        r_sg = kb.reg("sg_sb")
        wbc = cx.bf(OFF_SM + 6144, 1024)
        r_wbc = kb.reg("wbc")
        hch = [(cx.bf(OFF_A + 2048 * c, HN), kb.reg(f"h{c}")) for c in range(16)]

        for e in range(8):
            kb.dma("gpsimd", lambda g, e=e: g.dma_start(wrow, wt_sb[e:e + 1, :]),
                   reads=[r_wtsb], war_only=[r_sg], writes=[r_wrow],
                   tag=f"f{th}.{e}row")
            for j in range(2):
                sl = slice(j * 512, (j + 1) * 512)
                kb.op("tensor", lambda t, sl=sl: t.matmul(
                    cx.ps_b[:, sl], cx.bf(OFF_ONES128, 128)[0:1, :],
                    wrow[:, sl], start=True, stop=True),
                    reads=[r_wrow, cr(cx, "c_ones128")],
                    war_only=[cx.psb(j)], writes=[cx.psb(j)],
                    tag=f"f{th}.{e}bc{j}")
            kb.op("vector", lambda v: v.tensor_copy(wbc, cx.ps_b[:, 0:1024]),
                  reads=[cx.psb(0), cx.psb(1)], war_only=[r_wbc],
                  writes=[r_wbc], tag=f"f{th}.{e}bcc")
            for st_ in range(8):
                sk = st_ % 2
                kb.dma("gpsimd", lambda g, e=e, st_=st_, sk=sk: g.dma_start(
                    gu[sk], cx.inp["l1_gu"][e, st_]),
                    writes=[r_gu[sk]], tag=f"f{th}.{e}.{st_}gu")
                for c in range(2):
                    hc = st_ * 2 + c
                    pr = hc % 2
                    gps = cx.ps_a[:, pr * 1024:(pr + 1) * 1024]
                    ups = cx.ps_b[:, pr * 1024:(pr + 1) * 1024]
                    gr = [cx.psa(pr * 2), cx.psa(pr * 2 + 1)]
                    ur = [cx.psb(pr * 2), cx.psb(pr * 2 + 1)]
                    for kc in range(KC):
                        for j in range(2):
                            sl = slice(j * 512, (j + 1) * 512)
                            kb.op("tensor", lambda t, sk=sk, c=c, kc=kc, sl=sl,
                                  gps=gps: t.matmul(gps[:, sl],
                                           gu[sk][:, kc * 512 + c * 128:
                                                  kc * 512 + c * 128 + 128],
                                           xn2[kc][0][:, sl],
                                           start=(kc == 0), stop=(kc == KC - 1)),
                                  reads=[r_gu[sk], xn2[kc][1]],
                                  war_only=[gr[j]] if kc == 0 else (),
                                  writes=[gr[j]] if kc == KC - 1 else (),
                                  tag=f"f{th}g{e}.{hc}.{j}")
                    for kc in range(KC):
                        for j in range(2):
                            sl = slice(j * 512, (j + 1) * 512)
                            kb.op("tensor", lambda t, sk=sk, c=c, kc=kc, sl=sl,
                                  ups=ups: t.matmul(ups[:, sl],
                                           gu[sk][:, kc * 512 + 256 + c * 128:
                                                  kc * 512 + 256 + c * 128 + 128],
                                           xn2[kc][0][:, sl],
                                           start=(kc == 0), stop=(kc == KC - 1)),
                                  reads=[r_gu[sk], xn2[kc][1]],
                                  war_only=[ur[j]] if kc == 0 else (),
                                  writes=[ur[j]] if kc == KC - 1 else (),
                                  tag=f"f{th}u{e}.{hc}.{j}")
                    kb.op("scalar", lambda sc, hc=hc, ups=ups: sc.copy(
                        hch[hc][0], ups),
                        reads=ur, writes=[hch[hc][1]],
                        tag=f"f{th}he{e}.{hc}")
                    kb.op("gpsimd", lambda g, hc=hc: g.tensor_tensor(
                        hch[hc][0], hch[hc][0], wbc, ALU.mult),
                        reads=[hch[hc][1], r_wbc], writes=[hch[hc][1]],
                        tag=f"f{th}hw{e}.{hc}")
                    kb.op("scalar", lambda sc, gps=gps: sc.activation(
                        sg_sb, gps, AF.Silu),
                        reads=gr,
                        war_only=[r_wrow], writes=[r_sg],
                        tag=f"f{th}sg{e}.{hc}")
                    kb.op("vector", lambda v, hc=hc: v.tensor_tensor(
                        hch[hc][0], hch[hc][0], sg_sb, ALU.mult),
                        reads=[hch[hc][1], r_sg], writes=[hch[hc][1]],
                        tag=f"f{th}hs{e}.{hc}")
            for j in range(2):
                sl = slice(j * 512, (j + 1) * 512)
                for st_ in range(8):
                    sk = st_ % 2
                    kb.dma("gpsimd", lambda g, e=e, st_=st_, sk=sk: g.dma_start(
                        dn[sk], cx.inp["l1_dn"][e, st_]),
                        writes=[r_dn[sk]], tag=f"f{th}.{e}.{j}.{st_}dn")
                    for c2 in range(2):
                        hc = st_ * 2 + c2
                        first = st_ == 0 and c2 == 0
                        last = st_ == 7 and c2 == 1
                        for od in range(8):
                            ps = cx.ps_a if od < 4 else cx.ps_b
                            psr = (cx.psa if od < 4 else cx.psb)(od % 4)
                            kb.op("tensor",
                                  lambda t, sk=sk, c2=c2, od=od, sl=sl, hc=hc,
                                  first=first, last=last, ps=ps: t.matmul(
                                      ps[:, (od % 4) * 512:(od % 4 + 1) * 512],
                                      dn[sk][:, c2 * 1024 + od * 128:
                                             c2 * 1024 + (od + 1) * 128],
                                      hch[hc][0][:, sl],
                                      start=first, stop=last),
                                  reads=[r_dn[sk], hch[hc][1]],
                                  war_only=[psr] if first else (),
                                  writes=[psr] if last else (),
                                  tag=f"f{th}d{e}.{j}.{hc}.{od}")
                for od in range(8):
                    ps = cx.ps_a if od < 4 else cx.ps_b
                    psr = (cx.psa if od < 4 else cx.psb)(od % 4)
                    if e == 0:
                        kb.op("vector",
                              lambda v, ps=ps, od=od, sl=sl,
                              uu=uacc[od][0]: v.tensor_copy(
                                  uu[:, sl], ps[:, (od % 4) * 512:(od % 4 + 1) * 512]),
                              reads=[psr], war_only=[uacc[od][1]],
                              writes=[uacc[od][1]], tag=f"f{th}ue{e}.{j}.{od}")
                    else:
                        kb.op("vector",
                              lambda v, ps=ps, od=od, sl=sl,
                              uu=uacc[od][0]: v.tensor_add(
                                  uu[:, sl], uu[:, sl],
                                  ps[:, (od % 4) * 512:(od % 4 + 1) * 512]),
                              reads=[psr, uacc[od][1]], writes=[uacc[od][1]],
                              tag=f"f{th}ue{e}.{j}.{od}")
        kb.phase_barrier(f"l1f{th}ln")
        uacc = [(cx.f32(OFF_U + 4096 * i, HN), kb.reg(f"uacc{i}"))
                for i in range(KC)]
        ln_apply(cx, f"f{th}o_", uacc, HN, 1e-5, True, False, "addxr", n0=n0)


def build_program(phases):
    nc = bass.Bass("TRN2", target_bir_lowering=False)
    needed = ["xin"] + list(CONST_DECLS.keys())
    decls = [("xin", [NB, P, KC * T], F32)]
    decls += [(k, list(s), d) for k, (s, d) in CONST_DECLS.items()]
    if "l0a" in phases:
        decls.append(("wfT_strips", [16, P, KC * 128], F32))
        needed.append("wfT_strips")
    if "l0f" in phases:
        extra = [("cw_ln1w", [P, 8], F32), ("l0_wq_slices", [16, P, 512], F32),
                 ("l0_wkv", [P, 2048], F32), ("l0_kup", [P, 2048], F32),
                 ("l0_vup", [P, 2048], F32), ("l0_wo", [P, 8192], F32),
                 ("c_ropeq", [P, T], BF16), ("c_ropek", [P, T], BF16),
                 ("c_causal", [P, P], F32),
                 ("cw_ln2w", [P, 8], F32), ("cw_rbias", [8, 1], F32),
                 ("l0_wr", [P, 64], F32),
                 ("l0_moe_w1", [9, 8, P, 2048], F32),
                 ("l0_moe_w3", [9, 8, P, 2048], F32),
                 ("l0_moe_w2", [9, 8, P, 2048], F32)]
        decls += extra
        needed += [n for n, _, _ in extra]
    if "l1a" in phases:
        extra = [("l1_wq_slices", [16, P, 512], F32),
                 ("l1_wk_slices", [16, P, 512], F32),
                 ("l1_wv_slices", [16, P, 512], F32),
                 ("l1_wo", [P, 8192], F32)]
        decls += extra
        needed += [n for n, _, _ in extra]
    if "l1f" in phases:
        extra = [("l1_gatew", [P, 64], F32),
                 ("l1_gu", [8, 8, P, 4096], F32),
                 ("l1_dn", [8, 8, P, 2048], F32)]
        decls += extra
        needed += [n for n, _, _ in extra]

    inp = {}
    for name, shape, dt in decls:
        inp[name] = nc.dram_tensor(name, list(shape), dt, kind="ExternalInput").ap()
    xout = nc.dram_tensor("xout", [NB, P, KC * T], F32, kind="ExternalOutput").ap()

    with ExitStack() as st:
        sb = st.enter_context(nc.sbuf_tensor("SB", [P, SB_COLS], BF16))
        ps_a = st.enter_context(nc.psum_tensor("PSA", [P, 2048], F32))
        ps_b = st.enter_context(nc.psum_tensor("PSB", [P, 2048], F32))
        kb = KB(nc)
        cx = Ctx(nc, kb, inp, sb, ps_a, ps_b)

        load_consts(cx, list(CONST_DECLS.keys()))
        for b in range(NB):
            kb.dma("sync",
                   lambda s, b=b: s.dma_start(cx.f32(OFF_XR, KC * T), inp["xin"][b]),
                   writes=[kb.reg(f"xr{i}") for i in range(KC)], tag=f"ldx{b}")
            kb.phase_barrier(f"b{b}start")
            for ph in phases:
                EMITTERS[ph](cx)
                kb.phase_barrier(f"b{b}{ph}end")
            kb.dma("sync",
                   lambda s, b=b: s.dma_start(xout[b], cx.f32(OFF_XR, KC * T)),
                   reads=[kb.reg(f"xr{i}") for i in range(KC)], tag=f"stx{b}")
            kb.phase_barrier(f"b{b}end")
        stats = kb.emit()
    return nc, needed, stats


def emit_l0f(cx):
    emit_l0f_attn(cx)
    import os
    if os.environ.get("L0F_DBG"):
        return
    cx.kb.phase_barrier("l0f_mid")
    if os.environ.get("L0F_VI_MID"):
        for i in range(KC):
            cx.kb.op("vector", lambda v, i=i: v.tensor_copy(
                cx.f32(OFF_XR + 8192 * i, T), cx.bf(OFF_B + 4096 * i, T)),
                reads=[cx.kb.reg(f"dmid{i}")], writes=[cx.kb.reg(f"xr{i}")],
                tag="dbgmid")
        return
    emit_l0f_moe(cx)


EMITTERS = {"l0a": emit_l0a, "l0f": emit_l0f, "l1a": emit_l1a,
            "l1f": emit_l1f}

LAUNCHES = [["l0a", "l0f", "l1a", "l1f"]]


def prep_weights(inputs):
    g = lambda k: np.asarray(inputs[k], np.float32)
    w = {}
    wft = np.ascontiguousarray(g("L0_attn_W_f").T)
    w["wfT_strips"] = np.stack([tiles(wft[:, s * 128:(s + 1) * 128])
                                for s in range(16)])
    w["cw_ln1w"] = np.ascontiguousarray(g("L0_ffn_ln1_w").reshape(KC, P).T)
    wq = g("L0_ffn_Wq")
    w["l0_wq_slices"] = np.stack([tiles(wq[:, h * 64:(h + 1) * 64])
                                  for h in range(16)])
    w["l0_wkv"] = tiles(g("L0_ffn_Wkv"))
    w["l0_kup"] = np.concatenate([tiles(g("L0_ffn_Wk_up")[h]) for h in range(16)],
                                 axis=1)
    w["l0_vup"] = np.concatenate([tiles(g("L0_ffn_Wv_up")[h]) for h in range(16)],
                                 axis=1)
    w["l0_wo"] = tiles(g("L0_ffn_Wo"))
    w["cw_ln2w"] = np.ascontiguousarray(g("L0_ffn_ln2_w").reshape(KC, P).T)
    w["cw_rbias"] = np.ascontiguousarray(g("L0_ffn_r_bias").reshape(8, 1))
    w["l0_wr"] = tiles(g("L0_ffn_Wr"))
    w1 = np.concatenate([g("L0_ffn_ew1"), g("L0_ffn_sw1")[None]], axis=0)
    w3 = np.concatenate([g("L0_ffn_ew3"), g("L0_ffn_sw3")[None]], axis=0)
    w2 = np.concatenate([g("L0_ffn_ew2"), g("L0_ffn_sw2")[None]], axis=0)
    w["l0_moe_w1"] = np.stack([np.stack([tiles(w1[e][:, s2 * 256:(s2 + 1) * 256])
                                         for s2 in range(8)]) for e in range(9)])
    w["l0_moe_w3"] = np.stack([np.stack([tiles(w3[e][:, s2 * 256:(s2 + 1) * 256])
                                         for s2 in range(8)]) for e in range(9)])
    w["l0_moe_w2"] = np.stack([np.stack([tiles(w2[e][s2 * 256:(s2 + 1) * 256, :])
                                         for s2 in range(8)]) for e in range(9)])
    for nm, key in [("l1_wq_slices", "L1_attn_Wq"), ("l1_wk_slices", "L1_attn_Wk"),
                    ("l1_wv_slices", "L1_attn_Wv")]:
        wt = np.ascontiguousarray(g(key).T)
        w[nm] = np.stack([tiles(wt[:, h * 64:(h + 1) * 64]) for h in range(16)])
    w["l1_wo"] = tiles(np.ascontiguousarray(g("L1_attn_Wo").T))
    w["l1_gatew"] = tiles(g("L1_ffn_gate_w"))
    gup = g("L1_ffn_gate_up_w")
    dwn = g("L1_ffn_down_w")
    w["l1_gu"] = np.stack([np.stack([
        tiles(np.concatenate([gup[e][:, s2 * 256:(s2 + 1) * 256],
                              gup[e][:, 2048 + s2 * 256: 2048 + (s2 + 1) * 256]],
                             axis=1)) for s2 in range(8)]) for e in range(8)])
    w["l1_dn"] = np.stack([np.stack([tiles(dwn[e][s2 * 256:(s2 + 1) * 256, :])
                                     for s2 in range(8)]) for e in range(8)])
    return w


_PROG_CACHE = {}


def kernel(**inputs):
    cc = host_consts()
    w = prep_weights(inputs)

    x = np.asarray(inputs["L0_attn_x"], dtype=np.float32)
    cur = [np.stack([fm_batch(x[ci * NB + b]) for b in range(NB)])
           for ci in range(NCORES)]

    for phases in LAUNCHES:
        key = tuple(phases)
        if key not in _PROG_CACHE:
            _PROG_CACHE[key] = build_program(phases)
        nc, needed, stats = _PROG_CACHE[key]
        in_maps = []
        for ci in range(NCORES):
            m = {"xin": cur[ci]}
            for k in needed:
                if k in cc:
                    m[k] = cc[k]
                elif k in w:
                    m[k] = w[k]
            in_maps.append(m)
        res = run_bass_kernel_spmd(nc, in_maps, core_ids=list(range(NCORES)))
        cur = [np.asarray(res.results[ci]["xout"], dtype=np.float32)
               for ci in range(NCORES)]

    return np.concatenate(
        [np.stack([un_fm_batch(cur[ci][b]) for b in range(NB)]) for ci in range(NCORES)],
        axis=0)
